# version 15
# baseline (speedup 1.0000x reference)
import math
import numpy as np
import ml_dtypes
import concourse.bass as bass
import concourse.mybir as mybir
from concourse.bass_utils import run_bass_kernel_spmd

F32 = mybir.dt.float32
BF16 = mybir.dt.bfloat16
AF = mybir.ActivationFunctionType
ALU = mybir.AluOpType
AX = mybir.AxisListType

S = 4096
HID = 4096
H_PER = 8
D = 128
NQ = 1024
SHIFT = 8.0
SCALE = 1.0 / math.sqrt(D)
INSTR_START, INSTR_END = 513, 4040
G4 = [[0, 1, 2, 3], [4, 5, 6, 7]]
G8 = [[0, 1, 2, 3, 4, 5, 6, 7]]
SWAP_MASK = [m for i in range(0, 32, 2) for m in (i + 1, i)]
ENGINES = ("sync", "gpsimd", "scalar", "vector", "tensor")


class Prog:

    def __init__(self, nc, sems):
        self.nc = nc
        self.sems = sems
        self.steps = {k: [] for k in ENGINES}
        self.cnt = {s: 0 for s in sems}
        self.res = {}
        self.pending = {k: [] for k in ENGINES}

    def op(self, eng, fn, waits=(), incs=(), reads=(), writes=()):
        wl = list(waits) + self.pending[eng]
        self.pending[eng] = []
        for r in reads:
            st = self.res.get(r)
            if st and st["w"]:
                wl.append(st["w"])
        for r in writes:
            st = self.res.get(r)
            if st:
                wl.extend(st["r"])
                if st["w"]:
                    wl.append(st["w"])
        wd = {}
        for s, v in wl:
            assert v <= self.cnt[s], f"wait {s}>={v} unproducible (at {self.cnt[s]})"
            wd[s] = max(wd.get(s, 0), v)
        assert incs, "every op needs a completion inc"
        for s, k in incs:
            self.cnt[s] += k
        self.steps[eng].append((fn, sorted(wd.items()), list(incs)))
        done = (incs[-1][0], self.cnt[incs[-1][0]])
        for r in reads:
            self.res.setdefault(r, {"w": None, "r": []})["r"].append(done)
        for r in writes:
            self.res[r] = {"w": done, "r": []}
        return done

    def dma(self, eng, out, in_, waits=(), reads=(), writes=(), sem="ld", **kw):
        def fn(e, out=out, in_=in_, kw=kw):
            return e.dma_start(out=out, in_=in_, **kw)
        return self.op(eng, fn, waits, [(sem, 16)], reads, writes)

    def barrier(self, exclude=("cc", "go")):
        snap = [(s, v) for s, v in self.cnt.items() if s not in exclude and v > 0]
        for e in ENGINES:
            self.pending[e] = list(snap)

    def n(self, name):
        return self.cnt[name]

    def emit(self, block):
        sems = self.sems

        def runner(name):
            def run(e):
                for fn, waits, incs in self.steps[name]:
                    for s, v in waits:
                        e.wait_ge(sems[s], v)
                    ins = fn(e)
                    for s, k in incs:
                        ins.then_inc(sems[s], k)
            return run

        block.sync(runner("sync"))
        block.gpsimd(runner("gpsimd"))
        block.scalar(runner("scalar"))
        block.vector(runner("vector"))
        block.tensor(runner("tensor"))


def build_nc():
    nc = bass.Bass("TRN2", target_bir_lowering=False)

    x_q = nc.dram_tensor("x_q", [NQ, HID], F32, kind="ExternalInput").ap()
    qw_sl = nc.dram_tensor("qw_sl", [8, 128, 4096], F32, kind="ExternalInput").ap()
    kw_sl = nc.dram_tensor("kw_sl", [8, 128, 4096], F32, kind="ExternalInput").ap()
    vw_sl = nc.dram_tensor("vw_sl", [8, 128, 4096], F32, kind="ExternalInput").ap()
    ow_sl = nc.dram_tensor("ow_sl", [32, 128, 1024], F32, kind="ExternalInput").ap()
    wb_sl = nc.dram_tensor("wb_sl", [32, 128, 4096], F32, kind="ExternalInput").ap()
    wg_sl = nc.dram_tensor("wg_sl", [128, 256], F32, kind="ExternalInput").ap()
    af_sl = nc.dram_tensor("af_sl", [128, 4096], F32, kind="ExternalInput").ap()
    BfT = nc.dram_tensor("BfT", [128, HID], F32, kind="ExternalInput").ap()
    b_pm = nc.dram_tensor("b_pm", [128, 32], F32, kind="ExternalInput").ap()
    cos_il = nc.dram_tensor("cos_il", [128, S], BF16, kind="ExternalInput").ap()
    sin_il = nc.dram_tensor("sin_il", [128, S], BF16, kind="ExternalInput").ap()
    tri_in = nc.dram_tensor("tri_in", [128, 2048], BF16, kind="ExternalInput").ap()
    e16_in = nc.dram_tensor("e16_in", [8, 128], BF16, kind="ExternalInput").ap()
    ident_in = nc.dram_tensor("ident_in", [128, 128], BF16, kind="ExternalInput").ap()
    identf_in = nc.dram_tensor("identf_in", [128, 128], F32, kind="ExternalInput").ap()
    ones_in = nc.dram_tensor("ones_in", [128, 128], BF16, kind="ExternalInput").ap()
    impsc_in = nc.dram_tensor("impsc_in", [1, 1], F32, kind="ExternalInput").ap()
    out_q = nc.dram_tensor("out_q", [NQ, HID], F32, kind="ExternalOutput").ap()

    xn_ag_in = nc.dram_tensor("xn_ag_in", [32, 128, NQ], BF16)
    xn_ag_out = nc.dram_tensor("xn_ag_out", [128, 128, NQ], BF16)
    qs = nc.dram_tensor("qs", [H_PER, 128, S], BF16)
    ks = nc.dram_tensor("ks", [H_PER, 128, S], BF16)
    vs = nc.dram_tensor("vs", [H_PER, 32, 128, 128], BF16)
    at_sp = nc.dram_tensor("at_sp", [H_PER, 128, S], BF16)
    imp_in = nc.dram_tensor("imp_in", [1, 512], F32)
    imp_out = nc.dram_tensor("imp_out", [1, 512], F32)
    s_rs_in = [nc.dram_tensor(f"s_rs_in{g}", [4, 8, 128, NQ], BF16) for g in range(4)]
    s_rs_out = [nc.dram_tensor(f"s_rs_out{g}", [8, 128, NQ], BF16) for g in range(4)]
    g_sp = nc.dram_tensor("g_sp", [32, 128, NQ], BF16)
    sgT_sp = nc.dram_tensor("sgT_sp", [8, 128, HID], BF16)

    ctxs = []

    def ent(g):
        ctxs.append(g)
        return g.__enter__()

    bigb = ent(nc.sbuf_tensor("bigb", [128, 49152], BF16))
    bigf = ent(nc.sbuf_tensor("bigf", [128, 13312], F32))
    cosb = ent(nc.sbuf_tensor("cosb", [128, S], BF16))
    sinb = ent(nc.sbuf_tensor("sinb", [128, S], BF16))
    trib = ent(nc.sbuf_tensor("trib", [128, 2048], BF16))
    identb = ent(nc.sbuf_tensor("identb", [128, 128], BF16))
    identf = ent(nc.sbuf_tensor("identf", [128, 128], F32))
    onesb = ent(nc.sbuf_tensor("onesb", [128, 128], BF16))
    e16b = ent(nc.sbuf_tensor("e16b", [8, 128], BF16))
    bpmb = ent(nc.sbuf_tensor("bpmb", [128, 32], F32))
    impscb = ent(nc.sbuf_tensor("impscb", [1, 1], F32))
    smallf = ent(nc.sbuf_tensor("smallf", [128, 64], F32))
    qbarf = ent(nc.sbuf_tensor("qbarf", [128, 8], F32))
    qbarb = ent(nc.sbuf_tensor("qbarb", [128, 8], BF16))
    biaskt = ent(nc.sbuf_tensor("biaskt", [128, 8], F32))
    impktpm = ent(nc.sbuf_tensor("impktpm", [128, 8], F32))
    impbc = ent(nc.sbuf_tensor("impbc", [128, 512], F32))
    negshb = ent(nc.sbuf_tensor("negshb", [128, 1], F32))
    imps = ent(nc.sbuf_tensor("imps", [1, 512], F32))

    psA = ent(nc.psum_tensor("psA", [128, 1024], F32))
    psB = ent(nc.psum_tensor("psB", [128, 1024], F32))
    psD = ent(nc.psum_tensor("psD", [128, 512], F32))
    psO = ent(nc.psum_tensor("psO", [128, 512], F32))
    psT = ent(nc.psum_tensor("psT", [128, 2048], BF16))
    SEQ8 = [0, 4, 1, 5, 2, 6, 3, 7]
    PTC = [(s % 4) * 128 + (s // 4) * 1024 for s in range(8)]
    PS6 = [psA[:, 0:512], psA[:, 512:1024], psB[:, 0:512], psB[:, 512:1024],
           psD[:, :], psO[:, :]]
    PS6N = ["psA", "psA", "psB", "psB", "psDn", "psOu"]

    sems = {}
    for s in ["ld", "ld0", "ld1", "st", "st0", "st1", "wl0", "wl1",
              "cc", "pe", "act", "dve", "go"]:
        sems[s] = ent(nc.semaphore(s))

    p = Prog(nc, sems)

    for dst, src in [(cosb[:, :], cos_il), (sinb[:, :], sin_il), (trib[:, :], tri_in),
                     (identb[:, :], ident_in), (identf[:, :], identf_in),
                     (onesb[:, :], ones_in), (e16b[:, :], e16_in),
                     (bpmb[:, :], b_pm), (impscb[:, :], impsc_in)]:
        p.dma("sync", dst, src, writes=["consts"])

    x_t = [bigf[:, 0:4096], bigf[:, 4096:8192]]
    sqs = bigf[:, 8192:12288]
    stage = bigb[:, 0:32768]
    xn_tm = [bigb[:, 32768:36864], bigb[:, 36864:40960]]
    s1 = smallf[:, 0:1]; s2 = smallf[:, 1:2]; mmean = smallf[:, 2:3]
    mvar = smallf[:, 3:4]; rstd = smallf[:, 4:5]; nbias = smallf[:, 5:6]
    mm2 = smallf[:, 6:7]; tmpw = smallf[:, 8:9]

    for tt in range(8):
        bx = f"x{tt % 2}"
        p.dma("sync", x_t[tt % 2], x_q[tt * 128:(tt + 1) * 128, :], writes=[bx],
              sem=f"ld{tt % 2}")
        p.op("scalar", lambda e, b=x_t[tt % 2]: e.activation(sqs, b, AF.Square, accum_out=s2),
             incs=[("act", 1)], reads=[bx], writes=["s2", "sqs"])
        p.op("vector", lambda e, b=x_t[tt % 2]: e.reduce_sum(s1, b, AX.X),
             incs=[("dve", 1)], reads=[bx], writes=["s1"])
        p.op("vector", lambda e: e.tensor_scalar(mmean, s1, 1.0 / HID, None, ALU.mult),
             incs=[("dve", 1)], reads=["s1"], writes=["m"])
        p.op("vector", lambda e: e.tensor_tensor(mm2, mmean, mmean, ALU.mult),
             incs=[("dve", 1)], reads=["m"], writes=["mm2"])
        p.op("vector", lambda e: e.tensor_scalar(mvar, s2, 1.0 / HID, None, ALU.mult),
             incs=[("dve", 1)], reads=["s2"], writes=["var"])
        p.op("vector", lambda e: e.tensor_sub(mvar, mvar, mm2),
             incs=[("dve", 1)], reads=["mm2", "var"], writes=["var"])
        p.op("vector", lambda e: e.tensor_scalar(mvar, mvar, 1e-5, None, ALU.add),
             incs=[("dve", 1)], reads=["var"], writes=["var"])
        p.op("scalar", lambda e: e.activation(rstd, mvar, AF.Ln),
             incs=[("act", 1)], reads=["var"], writes=["rstd"])
        p.op("scalar", lambda e: e.activation(rstd, rstd, AF.Exp, scale=-0.5),
             incs=[("act", 1)], reads=["rstd"], writes=["rstd"])
        p.op("vector", lambda e: e.scalar_tensor_tensor(nbias, mmean, -1.0, rstd,
                                                        ALU.mult, ALU.mult),
             incs=[("dve", 1)], reads=["m", "rstd"], writes=["nb"])
        bxn = f"xnt{tt % 2}"
        p.op("vector", lambda e, b=x_t[tt % 2], o=xn_tm[tt % 2]: e.tensor_scalar(
            o, b, rstd, nbias, ALU.mult, ALU.add),
             incs=[("dve", 1)], reads=[bx, "rstd", "nb"], writes=[bxn])
        for ci in range(32):
            s8 = SEQ8[ci % 8]
            slot = f"psT{s8}"
            p.op("tensor", lambda e, ci=ci, o=xn_tm[tt % 2], c0=PTC[s8]: e.transpose(
                psT[:, c0:c0 + 128], o[:, ci * 128:(ci + 1) * 128], identb[:, :]),
                 incs=[("pe", 1)], reads=[bxn, "consts"], writes=[slot])
            p.op("vector", lambda e, ci=ci, tt=tt, c0=PTC[s8]: e.tensor_copy(
                stage[:, ci * 1024 + tt * 128: ci * 1024 + (tt + 1) * 128],
                psT[:, c0:c0 + 128]),
                 incs=[("dve", 1)], reads=[slot], writes=[f"stg_{ci}_{tt}"])
    for ci in range(32):
        p.dma("scalar", xn_ag_in.ap()[ci, :, :], stage[:, ci * 1024:(ci + 1) * 1024],
              reads=[f"stg_{ci}_{tt}" for tt in range(8)], writes=[f"agin_{ci}"], sem="st")
    p.op("gpsimd", lambda e: e.collective_compute(
        "AllGather", ALU.bypass, replica_groups=G4,
        ins=[xn_ag_in.ap().opt()], outs=[xn_ag_out.ap().opt()]),
         incs=[("cc", 1)], reads=[f"agin_{ci}" for ci in range(32)], writes=["xnag"])

    p.barrier()

    xn_sb = [bigb[:, ci * 1024:(ci + 1) * 1024] for ci in range(32)]
    w_sb = [bigb[:, 32768:36864], bigb[:, 36864:40960]]
    qraw = bigb[:, 40960:41472]
    rotb = bigb[:, 41472:41984]
    tcos = bigb[:, 41984:42496]
    qst = [bigb[:, 42496:43008], bigb[:, 43008:43520]]
    vstage = bigb[:, 43520:44032]

    p.op("vector", lambda e: e.memset(qbarf[:, :], 0.0), incs=[("dve", 1)],
         writes=["qbar"])
    negsh = negshb[:, 0:1]
    p.op("vector", lambda e: e.memset(negsh, -SHIFT), incs=[("dve", 1)],
         writes=["nsh"])
    wsrc = {"q": qw_sl, "k": kw_sl, "v": vw_sl}
    wstg = [bigf[:, 0:4096], bigf[:, 4096:8192]]

    def w_src(s):
        obi = s % 24
        kind, h = ("q", obi) if obi < 8 else (("k", obi - 8) if obi < 16 else ("v", obi - 16))
        return wsrc[kind][h, :, :]

    def kind_h(obi):
        return ("q", obi) if obi < 8 else (("k", obi - 8) if obi < 16 else ("v", obi - 16))

    kvis = bigb[:, 44032:44544]
    p.dma("sync", wstg[0], w_src(0), writes=["ws0"], sem="wl0")
    p.dma("sync", wstg[1], w_src(1), writes=["ws1"], sem="wl1")
    p.op("vector", lambda e: e.tensor_copy(w_sb[0], wstg[0]),
         incs=[("dve", 1)], reads=["ws0"], writes=["w0"])

    for s in range(96):
        qi, obi = s // 24, s % 24
        kind, h = kind_h(obi)
        if obi == 0:
            for ci in range(32):
                p.dma("sync", xn_sb[ci], xn_ag_out.ap()[qi * 32 + ci, :, :],
                      reads=["xnag"],
                      writes=[f"xn_{ci}"] + (["xnq"] if ci == 31 else []))
        if s + 2 < 96:
            p.dma("sync", wstg[s % 2], w_src(s + 2), writes=[f"ws{s % 2}"],
                  sem=f"wl{s % 2}")
        if s + 1 < 96:
            p.op("vector", lambda e, b=(s + 1) % 2: e.tensor_copy(w_sb[b], wstg[b]),
                 incs=[("dve", 1)], reads=[f"ws{(s + 1) % 2}"], writes=[f"w{(s + 1) % 2}"])
        wres = f"w{s % 2}"
        slot0, slot1 = (s * 2) % 6, (s * 2 + 1) % 6
        for ic in range(32):
            p.op("tensor", lambda e, ic=ic, w=w_sb[s % 2], ps=PS6[slot0]:
                 e.matmul(ps, w[:, ic * 128:(ic + 1) * 128],
                          xn_sb[ic][:, 0:512],
                          start=(ic == 0), stop=(ic == 31)),
                 incs=[("pe", 1)],
                 reads=[wres, f"xn_{ic}"] + (["xnq"] if ic == 0 else []),
                 writes=[PS6N[slot0]] if ic in (0, 31) else [])
            p.op("tensor", lambda e, ic=ic, w=w_sb[s % 2], ps=PS6[slot1]:
                 e.matmul(ps, w[:, ic * 128:(ic + 1) * 128],
                          xn_sb[ic][:, 512:1024],
                          start=(ic == 0), stop=(ic == 31)),
                 incs=[("pe", 1)],
                 reads=[wres, f"xn_{ic}"],
                 writes=[PS6N[slot1]] if ic in (0, 31) else [])
        for rng in range(2):
            slot = (s * 2 + rng) % 6
            pres = PS6N[slot]
            tok0 = qi * 1024 + rng * 512
            if kind in ("q", "k"):
                p.op("scalar", lambda e, ps=PS6[slot]: e.activation(qraw, ps, AF.Copy),
                     incs=[("act", 1)], reads=[pres], writes=["qraw"])
                stres = f"qst{s % 2}"
                p.op("vector", lambda e, t0=tok0: e.tensor_mul(
                    tcos, qraw, cosb[:, t0:t0 + 512]),
                     incs=[("dve", 1)], reads=["qraw"], writes=["tcos"])
                p.op("vector", lambda e: e.stream_shuffle(rotb, qraw, SWAP_MASK),
                     incs=[("dve", 1)], reads=["qraw"], writes=["rot"])
                p.op("vector", lambda e, t0=tok0: e.tensor_mul(
                    rotb, rotb, sinb[:, t0:t0 + 512]),
                     incs=[("dve", 1)], reads=["rot"], writes=["rot"])
                p.op("vector", lambda e, sb2=qst[s % 2]: e.tensor_add(
                    sb2, tcos, rotb),
                     incs=[("dve", 1)], reads=["tcos", "rot"], writes=[stres])
                dst = (qs if kind == "q" else ks).ap()[h, :, tok0:tok0 + 512]
                p.dma("scalar", dst, qst[s % 2],
                      reads=[stres], writes=[f"sp_{kind}{h}_{qi}{rng}"],
                      sem=f"st{s % 2}")
                if kind == "q":
                    lo = max(INSTR_START, tok0); hi = min(INSTR_END, tok0 + 512)
                    if lo < hi:
                        p.op("vector", lambda e, sb2=qst[s % 2],
                             lo=lo, hi=hi, t0=tok0: e.reduce_sum(
                                 tmpw, sb2[:, lo - t0:hi - t0], AX.X),
                             incs=[("dve", 1)], reads=[stres], writes=["tmpw"])
                        p.op("vector", lambda e, h=h: e.tensor_add(
                            qbarf[:, h:h + 1], qbarf[:, h:h + 1], tmpw),
                             incs=[("dve", 1)], reads=["tmpw", "qbar"], writes=["qbar"])
            else:
                p.op("scalar", lambda e, ps=PS6[slot]: e.activation(vstage, ps, AF.Copy),
                     incs=[("act", 1)], reads=[pres], writes=["vst"])
                for i4 in range(4):
                    p.op("tensor", lambda e, i4=i4: e.transpose(
                        psT[:, i4 * 128:(i4 + 1) * 128],
                        vstage[:, i4 * 128:(i4 + 1) * 128], identb[:, :]),
                         incs=[("pe", 1)], reads=["vst"], writes=[f"psT{i4}"])
                stres = f"qst{s % 2}"
                p.op("vector", lambda e, v2=qst[s % 2]: e.tensor_copy(
                    v2, psT[:, 0:512]),
                     incs=[("dve", 1)], reads=[f"psT{i4}" for i4 in range(4)],
                     writes=[stres])
                kt0 = tok0 // 128
                p.dma("scalar", vs.ap()[h, kt0:kt0 + 4, :, :].rearrange(
                    "t p c -> p t c"),
                      qst[s % 2].rearrange("p (t c) -> p t c", c=128),
                      reads=[stres], writes=[f"sp_v{h}_{qi}{rng}"],
                      sem=f"st{s % 2}")
        if s == 72 + 7:
            p.op("vector", lambda e: e.tensor_copy(qbarb[:, :], qbarf[:, :]),
                 incs=[("dve", 1)], reads=["qbar"], writes=["qbarb"])
            for h2 in range(H_PER):
                p.dma("sync", kvis[:, :], ks.ap()[h2, :, 1:513],
                      reads=[f"sp_k{h2}_00"], writes=["kvis"])
                p.op("tensor", lambda e, h2=h2: e.matmul(
                    psD[0:1, :], qbarb[:, h2:h2 + 1], kvis[:, :],
                    start=(h2 == 0), stop=(h2 == 7)),
                     incs=[("pe", 1)], reads=["kvis", "qbarb"],
                     writes=(["psDn"] if h2 in (0, 7) else []))
            p.op("vector", lambda e: e.tensor_scalar(imps[:, :], psD[0:1, :],
                                                     impscb[0:1, 0:1], None, ALU.mult),
                 incs=[("dve", 1)], reads=["psDn", "consts"], writes=["imps"])
            p.dma("scalar", imp_in.ap()[:, :], imps[:, :], reads=["imps"],
                  writes=["impin"], sem="st")
            p.op("gpsimd", lambda e: e.collective_compute(
                "AllReduce", ALU.add, replica_groups=G8,
                ins=[imp_in.ap().opt()], outs=[imp_out.ap().opt()]),
                 incs=[("cc", 1)], reads=["impin"], writes=["impout"])
    p.op("vector", lambda e: e.memset(impktpm[:, :], 3.0e38),
         incs=[("dve", 1)], writes=["iktpm"])
    p.dma("sync", impbc[:, :], imp_out.ap()[0:1, :].partition_broadcast(128),
          reads=["impout"], writes=["impbc"])
    p.dma("sync", impktpm[1:128, 0:1], imp_out.ap()[0:1, 0:127].rearrange("a c -> c a"),
          reads=["impout"], writes=["iktpm"])
    for j in range(1, 4):
        p.dma("sync", impktpm[:, j:j + 1],
              imp_out.ap()[0:1, j * 128 - 1:(j + 1) * 128 - 1].rearrange("a c -> c a"),
              reads=["impout"], writes=["iktpm"])
    p.dma("sync", impktpm[0:1, 4:5], imp_out.ap()[0:1, 511:512],
          reads=["impout"], writes=["iktpm"])
    gtbuf = bigf[:, 12288:12800]
    cntt = smallf[:, 16:24]
    for j in range(5):
        p.op("vector", lambda e, j=j: e.tensor_scalar(
            gtbuf, impbc[:, :], impktpm[:, j:j + 1], None, ALU.is_gt),
             incs=[("dve", 1)], reads=["impbc", "iktpm"], writes=["gt"])
        p.op("vector", lambda e, j=j: e.reduce_sum(cntt[:, j:j + 1], gtbuf, AX.X),
             incs=[("dve", 1)], reads=["gt"], writes=[f"cnt{j}"])
    p.op("vector", lambda e: e.tensor_scalar(biaskt[:, 0:5], cntt[:, 0:5], 255.5,
                                             None, ALU.is_lt),
         incs=[("dve", 1)], reads=[f"cnt{j}" for j in range(5)], writes=["bk"])
    p.op("vector", lambda e: e.tensor_scalar(biaskt[:, 0:5], biaskt[:, 0:5],
                                             1.0e6 - SHIFT, -1.0e6, ALU.mult, ALU.add),
         incs=[("dve", 1)], reads=["bk"], writes=["bk"])

    p.barrier()

    qh = [bigb[:, 0:4096], bigb[:, 4096:8192]]
    kh = [bigb[:, 8192:12288], bigb[:, 12288:16384]]
    vh = [bigb[:, 16384:20480], bigb[:, 20480:24576]]
    p_sb = bigb[:, 24576:40960]
    atst = [bigb[:, 40960:41472], bigb[:, 41472:41984]]
    denf = bigf[:, 0:512]
    recipf = bigf[:, 512:1024]

    for h in range(H_PER):
        hb = h % 2
        p.dma("sync", qh[hb], qs.ap()[h, :, :],
              reads=[f"sp_q{h}_{qi}{rng}" for qi in range(4) for rng in range(2)],
              writes=[f"qh{hb}"], sem=f"ld{hb}")
        p.dma("sync", kh[hb], ks.ap()[h, :, :],
              reads=[f"sp_k{h}_{qi}{rng}" for qi in range(4) for rng in range(2)],
              writes=[f"kh{hb}"], sem=f"ld{hb}")
        p.dma("sync", vh[hb].rearrange("p (t c) -> p t c", c=128),
              vs.ap()[h, :, :, :].rearrange("t p c -> p t c"),
              reads=[f"sp_v{h}_{qi}{rng}" for qi in range(4) for rng in range(2)],
              writes=[f"vh{hb}"], sem=f"ld{hb}")
        SC3 = [psA, psB]
        SC3N = ["sc0", "sc1"]
        DEPTH = 2
        for r in range(8):
            nkt = 4 * (r + 1)
            npair = nkt // 2

            def emit_scores(pp, r=r, hb=hb, nkt=nkt):
                jp = 2 * pp
                pb = SC3[pp % 2]
                pres = SC3N[pp % 2]
                for o2 in range(2):
                    j = jp + o2
                    p.op("tensor", lambda e, j=j, r=r, pb=pb, o2=o2, hb=hb: e.matmul(
                        pb[:, o2 * 512:(o2 + 1) * 512],
                        kh[hb][:, j * 128:(j + 1) * 128],
                        qh[hb][:, r * 512:(r + 1) * 512],
                        start=True, stop=True),
                         incs=[("pe", 1)], reads=[f"qh{hb}", f"kh{hb}"],
                         writes=[pres])
                if jp < 6:
                    for j in (jp, jp + 1):
                        o2 = j - jp
                        if j < 5:
                            p.op("scalar", lambda e, j=j, pb=pb, o2=o2: e.activation(
                                p_sb[:, j * 512:(j + 1) * 512],
                                pb[:, o2 * 512:(o2 + 1) * 512], AF.Exp,
                                bias=biaskt[:, j:j + 1], scale=SCALE),
                                 incs=[("act", 1)], reads=[pres, "bk"],
                                 writes=[f"p{j}"])
                        else:
                            p.op("scalar", lambda e, j=j, pb=pb, o2=o2: e.activation(
                                p_sb[:, j * 512:(j + 1) * 512],
                                pb[:, o2 * 512:(o2 + 1) * 512], AF.Exp,
                                bias=negsh, scale=SCALE),
                                 incs=[("act", 1)], reads=[pres, "nsh"],
                                 writes=[f"p{j}"])
                else:
                    p.op("scalar", lambda e, j0=jp, pb=pb: e.activation(
                        p_sb[:, j0 * 512:(j0 + 2) * 512], pb[:, 0:1024], AF.Exp,
                        bias=negsh, scale=SCALE),
                         incs=[("act", 1)], reads=[pres, "nsh"],
                         writes=[f"p{jp}", f"p{jp + 1}"])
                for o2 in range(2):
                    j = jp + o2
                    dj = j - 4 * r
                    if 0 <= dj < 4:
                        p.op("vector", lambda e, j=j, dj=dj: e.tensor_mul(
                            p_sb[:, j * 512:(j + 1) * 512],
                            p_sb[:, j * 512:(j + 1) * 512],
                            trib[:, dj * 512:(dj + 1) * 512]),
                             incs=[("dve", 1)], reads=[f"p{j}"], writes=[f"p{j}"])

            def emit_consume(qp, hb=hb, nkt=nkt):
                jq = 2 * qp
                for j in (jq, jq + 1):
                    p.op("tensor", lambda e, j=j, nkt=nkt: e.matmul(
                        psD[:, :], onesb[:, :], p_sb[:, j * 512:(j + 1) * 512],
                        start=(j == 0), stop=(j == nkt - 1)),
                         incs=[("pe", 1)], reads=[f"p{j}"],
                         writes=["psDn"] if j in (0, nkt - 1) else [])
                for j in (jq, jq + 1):
                    p.op("tensor", lambda e, j=j, hb=hb, nkt=nkt: e.matmul(
                        psO[:, :], vh[hb][:, j * 128:(j + 1) * 128],
                        p_sb[:, j * 512:(j + 1) * 512],
                        start=(j == 0), stop=(j == nkt - 1)),
                         incs=[("pe", 1)], reads=[f"p{j}", f"vh{hb}"],
                         writes=["psOu"] if j in (0, nkt - 1) else [])

            for pp in range(npair + DEPTH):
                if pp < npair:
                    emit_scores(pp)
                if pp >= DEPTH and pp - DEPTH < npair:
                    emit_consume(pp - DEPTH)
            p.op("scalar", lambda e: e.activation(denf, psD[:, :], AF.Ln),
                 incs=[("act", 1)], reads=["psDn"], writes=["denf"])
            p.op("scalar", lambda e: e.activation(recipf, denf, AF.Exp, scale=-1.0),
                 incs=[("act", 1)], reads=["denf"], writes=["recip"])
            ares = f"at{r % 2}"
            p.op("vector", lambda e, ab=atst[r % 2]: e.tensor_mul(ab, psO[:, :], recipf),
                 incs=[("dve", 1)], reads=["psOu", "recip"], writes=[ares])
            p.dma("scalar", at_sp.ap()[h, :, r * 512:(r + 1) * 512], atst[r % 2],
                  reads=[ares], writes=[f"at_{h}_{r}"], sem=f"st{r % 2}")

    p.barrier()

    owb = [bigb[:, 41984:43008], bigb[:, 43008:44032]]
    atc = [bigb[:, hc * 4096:(hc + 1) * 4096] for hc in range(8)]
    sstg = [bigb[:, 44032:44544], bigb[:, 44544:45056]]

    for hc in range(8):
        p.dma("sync", atc[hc], at_sp.ap()[hc, :, :],
              reads=[f"at_{hc}_{r}" for r in range(8)],
              writes=[f"atc{hc}"], sem=f"ld{hc % 2}")
    owstg = [bigf[:, 8192:9216], bigf[:, 9216:10240]]
    p.dma("sync", owstg[0], ow_sl[0, :, :], writes=["ows0"], sem="wl0")
    p.dma("sync", owstg[1], ow_sl[1, :, :], writes=["ows1"], sem="wl1")
    p.op("vector", lambda e: e.tensor_copy(owb[0], owstg[0]),
         incs=[("dve", 1)], reads=["ows0"], writes=["ow0"])
    for ob in range(32):
        wres = f"ow{ob % 2}"
        if ob + 2 < 32:
            p.dma("sync", owstg[ob % 2], ow_sl[ob + 2, :, :], writes=[f"ows{ob % 2}"],
                  sem=f"wl{ob % 2}")
        if ob + 1 < 32:
            p.op("vector", lambda e, b=(ob + 1) % 2: e.tensor_copy(owb[b], owstg[b]),
                 incs=[("dve", 1)], reads=[f"ows{(ob + 1) % 2}"], writes=[f"ow{(ob + 1) % 2}"])
        PSP4 = [psA[:, 0:512], psA[:, 512:1024], psB[:, 0:512], psB[:, 512:1024]]
        PSP4N = ["psP0", "psP1", "psP2", "psP3"]
        for rrg in range(2):
            for hc in range(8):
                for r4 in range(4):
                    rr = rrg * 4 + r4
                    p.op("tensor", lambda e, hc=hc, ob=ob, rr=rr, ps=PSP4[r4]: e.matmul(
                        ps, owb[ob % 2][:, hc * 128:(hc + 1) * 128],
                        atc[hc][:, rr * 512:(rr + 1) * 512],
                        start=(hc == 0), stop=(hc == 7)),
                         incs=[("pe", 1)],
                         reads=[wres, f"atc{hc}"],
                         writes=([PSP4N[r4]] if hc in (0, 7) else []))
            for r4 in range(4):
                rr = rrg * 4 + r4
                sres = f"ss{rr % 2}"
                p.op("vector", lambda e, ps=PSP4[r4], sb2=sstg[rr % 2]: e.tensor_copy(sb2, ps),
                     incs=[("dve", 1)], reads=[PSP4N[r4]], writes=[sres])
                qq, hh = rr // 2, rr % 2
                p.dma("scalar", s_rs_in[ob // 8].ap()[qq, ob % 8, :, hh * 512:(hh + 1) * 512],
                      sstg[rr % 2], reads=[sres], writes=[f"srs_{ob}_{rr}"],
                      sem=f"st{rr % 2}")
        if ob % 8 == 7:
            g = ob // 8
            p.op("gpsimd", lambda e, g=g: e.collective_compute(
                "ReduceScatter", ALU.add, replica_groups=G4,
                ins=[s_rs_in[g].ap().opt()], outs=[s_rs_out[g].ap().opt()]),
                 incs=[("cc", 1)],
                 reads=[f"srs_{ob2}_{rr}" for ob2 in range(g * 8, g * 8 + 8)
                        for rr in range(8)],
                 writes=[f"srs{g}"])

    p.barrier()

    for ci in range(32):
        p.dma("sync", xn_sb[ci], xn_ag_in.ap()[ci, :, :],
              writes=[f"xn_{ci}"] + (["xnq"] if ci == 31 else []))
    PSP = [psA[:, 0:512], psA[:, 512:1024]]
    wgb = bigb[:, 45056:45312]
    wgstg = bigf[:, 10240:10496]
    p.dma("sync", wgstg, wg_sl[:, :], writes=["wgs"], sem="wl0")
    p.op("vector", lambda e: e.tensor_copy(wgb, wgstg),
         incs=[("dve", 1)], reads=["wgs"], writes=["wgb"])
    logitf = bigf[:, 0:1024]
    for rng in range(2):
        for ic in range(32):
            p.op("tensor", lambda e, ic=ic, rng=rng: e.matmul(
                psD[0:8, 0:512], wgb[:, ic * 8:(ic + 1) * 8],
                xn_sb[ic][:, rng * 512:(rng + 1) * 512],
                start=(ic == 0), stop=(ic == 31)),
                 incs=[("pe", 1)],
                 reads=["wgb", f"xn_{ic}"] + (["xnq"] if ic == 0 else []),
                 writes=["psDn"] if ic in (0, 31) else [])
        p.op("vector", lambda e, rng=rng: e.tensor_copy(
            logitf[0:8, rng * 512:(rng + 1) * 512], psD[0:8, 0:512]),
             incs=[("dve", 1)], reads=["psDn"], writes=[f"lg{rng}"])
    routef = bigb[:, 45312:46336]
    ltt = bigf[:, 2048:2056]
    m1 = smallf[:, 40:41]; m2 = smallf[:, 41:42]; nm1 = smallf[:, 42:43]
    eqm = smallf[:, 43:51]; l2f = smallf[:, 51:59]
    for t2 in range(8):
        p.op("tensor", lambda e, t2=t2: e.transpose(
            psD[:, 0:8], logitf[0:8, t2 * 128:(t2 + 1) * 128], identf[0:8, 0:8]),
             incs=[("pe", 1)], reads=["lg0", "lg1"], writes=["psDn"])
        p.op("vector", lambda e: e.tensor_copy(ltt, psD[:, 0:8]),
             incs=[("dve", 1)], reads=["psDn"], writes=["ltt"])
        p.op("vector", lambda e: e.reduce_max(m1, ltt, AX.X),
             incs=[("dve", 1)], reads=["ltt"], writes=["gm1"])
        p.op("vector", lambda e: e.tensor_scalar(eqm, ltt, m1, None, ALU.is_equal),
             incs=[("dve", 1)], reads=["ltt", "gm1"], writes=["geq"])
        p.op("vector", lambda e: e.scalar_tensor_tensor(l2f, eqm, -30000.0, ltt,
                                                        ALU.mult, ALU.add),
             incs=[("dve", 1)], reads=["geq", "ltt"], writes=["gl2"])
        p.op("vector", lambda e: e.reduce_max(m2, l2f, AX.X),
             incs=[("dve", 1)], reads=["gl2"], writes=["gm2"])
        p.op("vector", lambda e: e.tensor_scalar(eqm, ltt, m2, None, ALU.is_ge),
             incs=[("dve", 1)], reads=["ltt", "gm2"], writes=["geq"])
        p.op("vector", lambda e: e.tensor_scalar(nm1, m1, -1.0, None, ALU.mult),
             incs=[("dve", 1)], reads=["gm1"], writes=["gnm1"])
        p.op("scalar", lambda e: e.activation(l2f, ltt, AF.Exp, bias=nm1),
             incs=[("act", 1)], reads=["ltt", "gnm1", "gl2"], writes=["gl2"])
        p.op("vector", lambda e: e.tensor_mul(l2f, l2f, eqm),
             incs=[("dve", 1)], reads=["gl2", "geq"], writes=["gl2"])
        p.op("vector", lambda e: e.reduce_sum(m2, l2f, AX.X),
             incs=[("dve", 1)], reads=["gl2"], writes=["gm2"])
        p.op("vector", lambda e: e.reciprocal(m1, m2),
             incs=[("dve", 1)], reads=["gm2"], writes=["gm1"])
        p.op("vector", lambda e: e.tensor_scalar(l2f, l2f, m1, None, ALU.mult),
             incs=[("dve", 1)], reads=["gl2", "gm1"], writes=["gl2"])
        p.op("tensor", lambda e: e.transpose(psD[0:8, 128:256], l2f, identf[:, :]),
             incs=[("pe", 1)], reads=["gl2"], writes=["psDn"])
        p.op("vector", lambda e, t2=t2: e.tensor_copy(
            routef[0:8, t2 * 128:(t2 + 1) * 128], psD[0:8, 128:256]),
             incs=[("dve", 1)], reads=["psDn"], writes=[f"rt{t2}"])
    afb = bigb[:, 40960:45056]
    hsb = bigb[:, 46336:47360]
    hwb = bigb[:, 47360:48384]
    afstg = bigf[:, 0:4096]
    p.dma("sync", afstg, af_sl[:, :],
          writes=["afs", "ows0", "ows1", "lg0", "lg1", "ltt"], sem="wl1")
    p.op("vector", lambda e: e.tensor_copy(afb, afstg),
         incs=[("dve", 1)], reads=["afs"], writes=["afb", "ow0", "ow1", "ss0", "ss1"])
    for rng in range(2):
        for ic in range(32):
            p.op("tensor", lambda e, ic=ic, rng=rng: e.matmul(
                psO[:, :], afb[:, ic * 128:(ic + 1) * 128],
                xn_sb[ic][:, rng * 512:(rng + 1) * 512],
                start=(ic == 0), stop=(ic == 31)),
                 incs=[("pe", 1)],
                 reads=["afb", f"xn_{ic}"] + (["xnq"] if ic == 0 else []),
                 writes=["psOu"] if ic in (0, 31) else [])
        p.op("vector", lambda e, rng=rng: e.tensor_copy(
            hsb[:, rng * 512:(rng + 1) * 512], psO[:, :]),
             incs=[("dve", 1)], reads=["psOu"], writes=[f"h{rng}"])
        p.op("tensor", lambda e, rng=rng: e.matmul(
            psD[:, :], e16b[:, :], routef[0:8, rng * 512:(rng + 1) * 512],
            start=True, stop=True),
             incs=[("pe", 1)], reads=[f"rt{t2}" for t2 in range(8)] + ["consts"],
             writes=["psDn"])
        p.op("vector", lambda e, rng=rng: e.scalar_tensor_tensor(
            hwb[:, rng * 512:(rng + 1) * 512], psD[:, :], 1.0,
            hsb[:, rng * 512:(rng + 1) * 512], ALU.mult, ALU.mult),
             incs=[("dve", 1)], reads=["psDn", f"h{rng}"], writes=[f"hw{rng}"])
    wbb = [bigb[:, 32768:36864], bigb[:, 36864:40960]]
    bfb = bigb[:, 40960:45056]
    bfstg = bigf[:, 4096:8192]
    p.dma("sync", bfstg, BfT[:, :], writes=["bfs"], sem="wl0")
    p.op("vector", lambda e: e.tensor_copy(bfb[:, 0:4096], bfstg),
         incs=[("dve", 1)], reads=["bfs"], writes=["afb", "bfb"])
    gstg = [bigb[:, 46336:46848], bigb[:, 46848:47360]]
    wbstg = [bigf[:, 0:4096], bigf[:, 4096:8192]]
    p.dma("sync", wbstg[0], wb_sl[0, :, :], writes=["wbs0", "afs", "bfs"], sem="wl0")
    p.dma("sync", wbstg[1], wb_sl[1, :, :], writes=["wbs1", "bfs"], sem="wl1")
    p.op("vector", lambda e: e.tensor_copy(wbb[0], wbstg[0]),
         incs=[("dve", 1)], reads=["wbs0"], writes=["wb0"])
    for ob in range(32):
        wres = f"wb{ob % 2}"
        if ob + 2 < 32:
            p.dma("sync", wbstg[ob % 2], wb_sl[ob + 2, :, :], writes=[f"wbs{ob % 2}"],
                  sem=f"wl{ob % 2}")
        if ob + 1 < 32:
            p.op("vector", lambda e, b=(ob + 1) % 2: e.tensor_copy(wbb[b], wbstg[b]),
                 incs=[("dve", 1)], reads=[f"wbs{(ob + 1) % 2}"], writes=[f"wb{(ob + 1) % 2}"])
        for ic in range(32):
            for rng in range(2):
                p.op("tensor", lambda e, ic=ic, ob=ob, rng=rng, ps=PSP[rng % 2]: e.matmul(
                    ps, wbb[ob % 2][:, ic * 128:(ic + 1) * 128],
                    xn_sb[ic][:, rng * 512:(rng + 1) * 512],
                    start=(ic == 0), stop=False),
                     incs=[("pe", 1)],
                     reads=[wres, f"xn_{ic}"] + (["xnq"] if ic == 0 and rng == 0 else []),
                     writes=(["psP%d" % (rng % 2)] if ic == 0 else []))
        for rng in range(2):
            pres = "psP%d" % (rng % 2)
            p.op("tensor", lambda e, ob=ob, rng=rng, ps=PSP[rng % 2]: e.matmul(
                ps, bfb[:, ob * 128:(ob + 1) * 128],
                hwb[:, rng * 512:(rng + 1) * 512],
                start=False, stop=True),
                 incs=[("pe", 1)], reads=["bfb", f"hw{rng}"], writes=[pres])
            gres = f"gs{rng}"
            p.op("scalar", lambda e, ob=ob, ps=PSP[rng % 2], sb2=gstg[rng]: e.activation(
                sb2, ps, AF.Silu, bias=bpmb[:, ob:ob + 1]),
                 incs=[("act", 1)], reads=[pres, "consts"],
                 writes=[gres] + ([f"h{rng}"] if ob == 0 else []))
            p.dma("scalar", g_sp.ap()[ob, :, rng * 512:(rng + 1) * 512], gstg[rng],
                  reads=[gres], writes=[f"g_{ob}_{rng}"], sem=f"st{rng}")

    p.barrier()

    ssb = [bigb[:, 0:1024], bigb[:, 1024:2048]]
    gsb = [bigb[:, 2048:3072], bigb[:, 3072:4096]]
    sgb = bigb[:, 4096:5120]
    sgT = bigb[:, 5120:6144]
    for ob in range(32):
        p.dma("sync", ssb[ob % 2], s_rs_out[ob // 8].ap()[ob % 8, :, :],
              reads=[f"srs{ob // 8}"], writes=[f"sq{ob % 2}"], sem=f"ld{ob % 2}")
        p.dma("sync", gsb[ob % 2], g_sp.ap()[ob, :, :],
              reads=[f"g_{ob}_0", f"g_{ob}_1"], writes=[f"gq{ob % 2}"],
              sem=f"ld{ob % 2}")
        p.op("vector", lambda e, ob=ob: e.tensor_mul(sgb, ssb[ob % 2], gsb[ob % 2]),
             incs=[("dve", 1)], reads=[f"sq{ob % 2}", f"gq{ob % 2}"],
             writes=["sgb"])
        for t2 in range(8):
            s8 = SEQ8[t2]
            p.op("tensor", lambda e, t2=t2, c0=PTC[s8]: e.transpose(
                psT[:, c0:c0 + 128],
                sgb[:, t2 * 128:(t2 + 1) * 128], identb[:, :]),
                 incs=[("pe", 1)], reads=["sgb"], writes=[f"psT{s8}"])
            p.op("vector", lambda e, t2=t2, c0=PTC[s8]: e.tensor_copy(
                sgT[:, t2 * 128:(t2 + 1) * 128], psT[:, c0:c0 + 128]),
                 incs=[("dve", 1)], reads=[f"psT{s8}"], writes=[f"sgT_{t2}"])
        p.dma("scalar", sgT_sp.ap()[:, :, ob * 128:(ob + 1) * 128].rearrange(
            "t p c -> p t c"), sgT.rearrange("p (t c) -> p t c", c=128),
              reads=[f"sgT_{t2}" for t2 in range(8)], writes=[f"sgsp{ob}"], sem="st")
    xf = bigf[:, 0:4096]
    of = [bigf[:, 4096:8192], bigf[:, 8192:12288]]
    sgTl = [bigb[:, 6144:10240], bigb[:, 10240:14336]]
    for t2 in range(8):
        p.dma("sync", sgTl[t2 % 2], sgT_sp.ap()[t2, :, :],
              reads=[f"sgsp{ob}" for ob in range(32)] if t2 < 2 else [],
              writes=[f"sgl{t2 % 2}"], sem=f"ld{t2 % 2}")
        p.dma("sync", xf, x_q[t2 * 128:(t2 + 1) * 128, :], writes=["xf"])
        ores = f"of{t2 % 2}"
        p.op("vector", lambda e, t2=t2: e.scalar_tensor_tensor(
            of[t2 % 2], sgTl[t2 % 2], 1.0, xf, ALU.mult, ALU.add),
             incs=[("dve", 1)], reads=[f"sgl{t2 % 2}", "xf"], writes=[ores])
        p.dma("sync", out_q[t2 * 128:(t2 + 1) * 128, :], of[t2 % 2],
              reads=[ores], writes=[f"outd{t2}"], sem="st")
    p.op("vector", lambda e: e.engine_nop(),
         waits=[("st", p.n("st"))], incs=[("go", 1)])

    with nc.Block() as block:
        p.emit(block)
    for g in reversed(ctxs):
        g.__exit__(None, None, None)
    return nc


_NC_CACHE = None


def _get_nc():
    global _NC_CACHE
    if _NC_CACHE is None:
        _NC_CACHE = build_nc()
    return _NC_CACHE


def _host_prep(inputs):
    x = inputs["L0_attn_hidden_states"]
    qw = inputs["L0_attn_q_w"]; kw = inputs["L0_attn_k_w"]
    vw = inputs["L0_attn_v_w"]; ow = inputs["L0_attn_o_w"]
    Wb = inputs["L0_ffn_W_base"]; bb = inputs["L0_ffn_b_base"]
    Wg = inputs["L0_ffn_W_gate"]; A = inputs["L0_ffn_A"]; Bm = inputs["L0_ffn_B"]
    BF = ml_dtypes.bfloat16

    perm = np.empty(128, np.int64)
    perm[0::2] = np.arange(64)
    perm[1::2] = np.arange(64, 128)
    inv_freq = 1.0 / (10000.0 ** (np.arange(0, D, 2, dtype=np.float64) / D))
    freqs = np.arange(S, dtype=np.float64)[:, None] * inv_freq[None, :]
    cos64 = np.cos(freqs); sin64 = np.sin(freqs)
    cos_il = np.empty((128, S), np.float32)
    sin_il = np.empty((128, S), np.float32)
    cos_il[0::2] = cos64.T.astype(np.float32); cos_il[1::2] = cos64.T.astype(np.float32)
    sin_il[0::2] = (-sin64.T).astype(np.float32); sin_il[1::2] = sin64.T.astype(np.float32)
    cos_il = cos_il.astype(BF); sin_il = sin_il.astype(BF)

    pp, ff = np.mgrid[0:128, 0:512]
    tri = np.zeros((128, 2048), np.float32)
    for dj in range(4):
        tri[:, dj * 512:(dj + 1) * 512] = (ff >= pp + 128 * dj).astype(np.float32)
    tri = tri.astype(BF)
    e16 = np.zeros((8, 128), np.float32)
    for e in range(8):
        e16[e, e * 16:(e + 1) * 16] = 1.0
    e16 = e16.astype(BF)
    identb = np.eye(128, dtype=np.float32).astype(BF)
    identf = np.eye(128, dtype=np.float32)
    ones = np.ones((128, 128), np.float32).astype(BF)
    b_pm = np.ascontiguousarray(bb.reshape(32, 128).T)

    WbT = Wb.T
    wb_sl = np.ascontiguousarray(
        WbT.reshape(32, 128, 32, 128).transpose(2, 1, 0, 3).reshape(32, 128, 4096))
    wg_sl = np.ascontiguousarray(
        Wg.T.reshape(32, 128, 8).transpose(1, 0, 2).reshape(128, 256))
    af_sl = np.ascontiguousarray(
        A.reshape(128, HID).T.reshape(32, 128, 128).transpose(1, 0, 2).reshape(128, 4096))
    BfT = np.ascontiguousarray(Bm.transpose(0, 2, 1).reshape(128, HID))

    in_maps = []
    for c in range(8):
        b, i = c // 4, c % 4
        hsl = slice(i * 1024, (i + 1) * 1024)
        qwh = qw[hsl].reshape(8, 128, HID)[:, perm, :].reshape(1024, HID)
        kwh = kw[hsl].reshape(8, 128, HID)[:, perm, :].reshape(1024, HID)

        def slab_qkv(wT):
            return np.ascontiguousarray(
                wT.reshape(32, 128, 8, 128).transpose(2, 1, 0, 3).reshape(8, 128, 4096))

        owT = ow[:, hsl].T
        ow_sl = np.ascontiguousarray(
            owT.reshape(8, 128, 32, 128).transpose(2, 1, 0, 3).reshape(32, 128, 1024))
        in_maps.append({
            "x_q": np.ascontiguousarray(x[b, i * 1024:(i + 1) * 1024, :]),
            "qw_sl": slab_qkv(qwh.T),
            "kw_sl": slab_qkv(kwh.T),
            "vw_sl": slab_qkv(vw[hsl].T),
            "ow_sl": ow_sl,
            "wb_sl": wb_sl, "wg_sl": wg_sl, "af_sl": af_sl, "BfT": BfT, "b_pm": b_pm,
            "cos_il": cos_il, "sin_il": sin_il, "tri_in": tri, "e16_in": e16,
            "ident_in": identb, "identf_in": identf, "ones_in": ones,
            "impsc_in": np.array([[SCALE / (32.0 * (INSTR_END - INSTR_START))
                                   if b == 0 else 0.0]], np.float32),
        })
    return in_maps


def kernel(**inputs):
    nc = _get_nc()
    in_maps = _host_prep(inputs)
    res = run_bass_kernel_spmd(nc, in_maps, core_ids=list(range(8)))
    out = np.empty((2, S, HID), np.float32)
    for c in range(8):
        b, i = c // 4, c % 4
        out[b, i * 1024:(i + 1) * 1024, :] = res.results[c]["out_q"]
    return out


# version 16
# speedup vs baseline: 1.0198x; 1.0198x over previous
import math
import numpy as np
import ml_dtypes
import concourse.bass as bass
import concourse.mybir as mybir
from concourse.bass_utils import run_bass_kernel_spmd

F32 = mybir.dt.float32
BF16 = mybir.dt.bfloat16
AF = mybir.ActivationFunctionType
ALU = mybir.AluOpType
AX = mybir.AxisListType

S = 4096
HID = 4096
H_PER = 8
D = 128
NQ = 1024
SHIFT = 8.0
SCALE = 1.0 / math.sqrt(D)
INSTR_START, INSTR_END = 513, 4040
G4 = [[0, 1, 2, 3], [4, 5, 6, 7]]
G8 = [[0, 1, 2, 3, 4, 5, 6, 7]]
SWAP_MASK = [m for i in range(0, 32, 2) for m in (i + 1, i)]
ENGINES = ("sync", "gpsimd", "scalar", "vector", "tensor")


class Prog:

    def __init__(self, nc, sems):
        self.nc = nc
        self.sems = sems
        self.steps = {k: [] for k in ENGINES}
        self.cnt = {s: 0 for s in sems}
        self.res = {}
        self.pending = {k: [] for k in ENGINES}

    def op(self, eng, fn, waits=(), incs=(), reads=(), writes=()):
        wl = list(waits) + self.pending[eng]
        self.pending[eng] = []
        for r in reads:
            st = self.res.get(r)
            if st and st["w"]:
                wl.append(st["w"])
        for r in writes:
            st = self.res.get(r)
            if st:
                wl.extend(st["r"])
                if st["w"]:
                    wl.append(st["w"])
        wd = {}
        for s, v in wl:
            assert v <= self.cnt[s], f"wait {s}>={v} unproducible (at {self.cnt[s]})"
            wd[s] = max(wd.get(s, 0), v)
        assert incs, "every op needs a completion inc"
        for s, k in incs:
            self.cnt[s] += k
        self.steps[eng].append((fn, sorted(wd.items()), list(incs)))
        done = (incs[-1][0], self.cnt[incs[-1][0]])
        for r in reads:
            self.res.setdefault(r, {"w": None, "r": []})["r"].append(done)
        for r in writes:
            self.res[r] = {"w": done, "r": []}
        return done

    def dma(self, eng, out, in_, waits=(), reads=(), writes=(), sem="ld", **kw):
        def fn(e, out=out, in_=in_, kw=kw):
            return e.dma_start(out=out, in_=in_, **kw)
        return self.op(eng, fn, waits, [(sem, 16)], reads, writes)

    def barrier(self, exclude=("cc", "go")):
        snap = [(s, v) for s, v in self.cnt.items() if s not in exclude and v > 0]
        for e in ENGINES:
            self.pending[e] = list(snap)

    def n(self, name):
        return self.cnt[name]

    def emit(self, block):
        sems = self.sems

        def runner(name):
            def run(e):
                for fn, waits, incs in self.steps[name]:
                    for s, v in waits:
                        e.wait_ge(sems[s], v)
                    ins = fn(e)
                    for s, k in incs:
                        ins.then_inc(sems[s], k)
            return run

        block.sync(runner("sync"))
        block.gpsimd(runner("gpsimd"))
        block.scalar(runner("scalar"))
        block.vector(runner("vector"))
        block.tensor(runner("tensor"))


def build_nc():
    nc = bass.Bass("TRN2", target_bir_lowering=False)

    x_q = nc.dram_tensor("x_q", [NQ, HID], F32, kind="ExternalInput").ap()
    qw_sl = nc.dram_tensor("qw_sl", [8, 128, 4096], F32, kind="ExternalInput").ap()
    kw_sl = nc.dram_tensor("kw_sl", [8, 128, 4096], F32, kind="ExternalInput").ap()
    vw_sl = nc.dram_tensor("vw_sl", [8, 128, 4096], F32, kind="ExternalInput").ap()
    ow_sl = nc.dram_tensor("ow_sl", [32, 128, 1024], F32, kind="ExternalInput").ap()
    wb_sl = nc.dram_tensor("wb_sl", [32, 128, 4096], F32, kind="ExternalInput").ap()
    wg_sl = nc.dram_tensor("wg_sl", [128, 256], F32, kind="ExternalInput").ap()
    af_sl = nc.dram_tensor("af_sl", [128, 4096], F32, kind="ExternalInput").ap()
    BfT = nc.dram_tensor("BfT", [128, HID], F32, kind="ExternalInput").ap()
    b_pm = nc.dram_tensor("b_pm", [128, 32], F32, kind="ExternalInput").ap()
    cos_il = nc.dram_tensor("cos_il", [128, S], BF16, kind="ExternalInput").ap()
    sin_il = nc.dram_tensor("sin_il", [128, S], BF16, kind="ExternalInput").ap()
    tri_in = nc.dram_tensor("tri_in", [128, 2048], BF16, kind="ExternalInput").ap()
    e16_in = nc.dram_tensor("e16_in", [8, 128], BF16, kind="ExternalInput").ap()
    ident_in = nc.dram_tensor("ident_in", [128, 128], BF16, kind="ExternalInput").ap()
    identf_in = nc.dram_tensor("identf_in", [128, 128], F32, kind="ExternalInput").ap()
    ones_in = nc.dram_tensor("ones_in", [128, 128], BF16, kind="ExternalInput").ap()
    impsc_in = nc.dram_tensor("impsc_in", [1, 1], F32, kind="ExternalInput").ap()
    out_q = nc.dram_tensor("out_q", [NQ, HID], F32, kind="ExternalOutput").ap()

    xn_ag_in = nc.dram_tensor("xn_ag_in", [32, 128, NQ], BF16)
    xn_ag_out = nc.dram_tensor("xn_ag_out", [128, 128, NQ], BF16)
    qs = nc.dram_tensor("qs", [H_PER, 128, S], BF16)
    ks = nc.dram_tensor("ks", [H_PER, 128, S], BF16)
    vs = nc.dram_tensor("vs", [H_PER, 32, 128, 128], BF16)
    at_sp = nc.dram_tensor("at_sp", [H_PER, 128, S], BF16)
    imp_in = nc.dram_tensor("imp_in", [1, 512], F32)
    imp_out = nc.dram_tensor("imp_out", [1, 512], F32)
    s_rs_in = [nc.dram_tensor(f"s_rs_in{g}", [4, 8, 128, NQ], BF16) for g in range(4)]
    s_rs_out = [nc.dram_tensor(f"s_rs_out{g}", [8, 128, NQ], BF16) for g in range(4)]
    g_sp = nc.dram_tensor("g_sp", [32, 128, NQ], BF16)
    sgT_sp = nc.dram_tensor("sgT_sp", [8, 128, HID], BF16)

    ctxs = []

    def ent(g):
        ctxs.append(g)
        return g.__enter__()

    bigb = ent(nc.sbuf_tensor("bigb", [128, 49152], BF16))
    bigf = ent(nc.sbuf_tensor("bigf", [128, 13312], F32))
    cosb = ent(nc.sbuf_tensor("cosb", [128, S], BF16))
    sinb = ent(nc.sbuf_tensor("sinb", [128, S], BF16))
    trib = ent(nc.sbuf_tensor("trib", [128, 2048], BF16))
    identb = ent(nc.sbuf_tensor("identb", [128, 128], BF16))
    identf = ent(nc.sbuf_tensor("identf", [128, 128], F32))
    onesb = ent(nc.sbuf_tensor("onesb", [128, 128], BF16))
    e16b = ent(nc.sbuf_tensor("e16b", [8, 128], BF16))
    bpmb = ent(nc.sbuf_tensor("bpmb", [128, 32], F32))
    impscb = ent(nc.sbuf_tensor("impscb", [1, 1], F32))
    smallf = ent(nc.sbuf_tensor("smallf", [128, 64], F32))
    qbarf = ent(nc.sbuf_tensor("qbarf", [128, 8], F32))
    qbarb = ent(nc.sbuf_tensor("qbarb", [128, 8], BF16))
    biaskt = ent(nc.sbuf_tensor("biaskt", [128, 8], F32))
    impktpm = ent(nc.sbuf_tensor("impktpm", [128, 8], F32))
    impbc = ent(nc.sbuf_tensor("impbc", [128, 512], F32))
    negshb = ent(nc.sbuf_tensor("negshb", [128, 1], F32))
    imps = ent(nc.sbuf_tensor("imps", [1, 512], F32))

    psA = ent(nc.psum_tensor("psA", [128, 1024], F32))
    psB = ent(nc.psum_tensor("psB", [128, 1024], F32))
    psD = ent(nc.psum_tensor("psD", [128, 512], F32))
    psO = ent(nc.psum_tensor("psO", [128, 512], F32))
    psT = ent(nc.psum_tensor("psT", [128, 2048], BF16))
    psTf = psT[:, :].bitcast(F32)
    SEQ8 = [0, 4, 1, 5, 2, 6, 3, 7]
    PTC = [(s % 4) * 128 + (s // 4) * 1024 for s in range(8)]
    PS6 = [psA[:, 0:512], psA[:, 512:1024], psB[:, 0:512], psB[:, 512:1024],
           psD[:, :], psO[:, :]]
    PS6N = ["psA", "psA", "psB", "psB", "psDn", "psOu"]

    sems = {}
    for s in ["ld", "ld0", "ld1", "st", "st0", "st1", "wl0", "wl1",
              "cc", "pe", "act", "dve", "go"]:
        sems[s] = ent(nc.semaphore(s))

    p = Prog(nc, sems)

    for dst, src in [(cosb[:, :], cos_il), (sinb[:, :], sin_il), (trib[:, :], tri_in),
                     (identb[:, :], ident_in), (identf[:, :], identf_in),
                     (onesb[:, :], ones_in), (e16b[:, :], e16_in),
                     (bpmb[:, :], b_pm), (impscb[:, :], impsc_in)]:
        p.dma("sync", dst, src, writes=["consts"])

    x_t = [bigf[:, 0:4096], bigf[:, 4096:8192]]
    sqs = bigf[:, 8192:12288]
    stage = bigb[:, 0:32768]
    xn_tm = [bigb[:, 32768:36864], bigb[:, 36864:40960]]
    s1 = smallf[:, 0:1]; s2 = smallf[:, 1:2]; mmean = smallf[:, 2:3]
    mvar = smallf[:, 3:4]; rstd = smallf[:, 4:5]; nbias = smallf[:, 5:6]
    mm2 = smallf[:, 6:7]; tmpw = smallf[:, 8:9]

    for tt in range(8):
        bx = f"x{tt % 2}"
        p.dma("sync", x_t[tt % 2], x_q[tt * 128:(tt + 1) * 128, :], writes=[bx],
              sem=f"ld{tt % 2}")
        p.op("scalar", lambda e, b=x_t[tt % 2]: e.activation(sqs, b, AF.Square, accum_out=s2),
             incs=[("act", 1)], reads=[bx], writes=["s2", "sqs"])
        p.op("vector", lambda e, b=x_t[tt % 2]: e.reduce_sum(s1, b, AX.X),
             incs=[("dve", 1)], reads=[bx], writes=["s1"])
        p.op("vector", lambda e: e.tensor_scalar(mmean, s1, 1.0 / HID, None, ALU.mult),
             incs=[("dve", 1)], reads=["s1"], writes=["m"])
        p.op("vector", lambda e: e.tensor_tensor(mm2, mmean, mmean, ALU.mult),
             incs=[("dve", 1)], reads=["m"], writes=["mm2"])
        p.op("vector", lambda e: e.tensor_scalar(mvar, s2, 1.0 / HID, None, ALU.mult),
             incs=[("dve", 1)], reads=["s2"], writes=["var"])
        p.op("vector", lambda e: e.tensor_sub(mvar, mvar, mm2),
             incs=[("dve", 1)], reads=["mm2", "var"], writes=["var"])
        p.op("vector", lambda e: e.tensor_scalar(mvar, mvar, 1e-5, None, ALU.add),
             incs=[("dve", 1)], reads=["var"], writes=["var"])
        p.op("scalar", lambda e: e.activation(rstd, mvar, AF.Ln),
             incs=[("act", 1)], reads=["var"], writes=["rstd"])
        p.op("scalar", lambda e: e.activation(rstd, rstd, AF.Exp, scale=-0.5),
             incs=[("act", 1)], reads=["rstd"], writes=["rstd"])
        p.op("vector", lambda e: e.scalar_tensor_tensor(nbias, mmean, -1.0, rstd,
                                                        ALU.mult, ALU.mult),
             incs=[("dve", 1)], reads=["m", "rstd"], writes=["nb"])
        bxn = f"xnt{tt % 2}"
        p.op("vector", lambda e, b=x_t[tt % 2], o=xn_tm[tt % 2]: e.tensor_scalar(
            o, b, rstd, nbias, ALU.mult, ALU.add),
             incs=[("dve", 1)], reads=[bx, "rstd", "nb"], writes=[bxn])
        for ci in range(32):
            s8 = SEQ8[ci % 8]
            slot = f"psT{s8}"
            p.op("tensor", lambda e, ci=ci, o=xn_tm[tt % 2], c0=PTC[s8]: e.transpose(
                psT[:, c0:c0 + 128], o[:, ci * 128:(ci + 1) * 128], identb[:, :]),
                 incs=[("pe", 1)], reads=[bxn, "consts"], writes=[slot])
            p.op("vector", lambda e, ci=ci, tt=tt, c0=PTC[s8]: e.tensor_copy(
                stage[:, ci * 1024 + tt * 128: ci * 1024 + (tt + 1) * 128],
                psT[:, c0:c0 + 128]),
                 incs=[("dve", 1)], reads=[slot], writes=[f"stg_{ci}_{tt}"])
    for ci in range(32):
        p.dma("scalar", xn_ag_in.ap()[ci, :, :], stage[:, ci * 1024:(ci + 1) * 1024],
              reads=[f"stg_{ci}_{tt}" for tt in range(8)], writes=[f"agin_{ci}"], sem="st")
    p.op("gpsimd", lambda e: e.collective_compute(
        "AllGather", ALU.bypass, replica_groups=G4,
        ins=[xn_ag_in.ap().opt()], outs=[xn_ag_out.ap().opt()]),
         incs=[("cc", 1)], reads=[f"agin_{ci}" for ci in range(32)], writes=["xnag"])

    p.barrier()

    xn_sb = [bigb[:, ci * 1024:(ci + 1) * 1024] for ci in range(32)]
    w_sb = [bigb[:, 32768:36864], bigb[:, 36864:40960]]
    qraw = bigb[:, 40960:41472]
    rotb = bigb[:, 41472:41984]
    tcos = bigb[:, 41984:42496]
    qst = [bigb[:, 42496:43008], bigb[:, 43008:43520]]
    vstage = bigb[:, 43520:44032]

    p.op("vector", lambda e: e.memset(qbarf[:, :], 0.0), incs=[("dve", 1)],
         writes=["qbar"])
    negsh = negshb[:, 0:1]
    p.op("vector", lambda e: e.memset(negsh, -SHIFT), incs=[("dve", 1)],
         writes=["nsh"])
    wsrc = {"q": qw_sl, "k": kw_sl, "v": vw_sl}
    wstg = [bigf[:, 0:4096], bigf[:, 4096:8192]]

    def w_src(s):
        obi = s % 24
        kind, h = ("q", obi) if obi < 8 else (("k", obi - 8) if obi < 16 else ("v", obi - 16))
        return wsrc[kind][h, :, :]

    def kind_h(obi):
        return ("q", obi) if obi < 8 else (("k", obi - 8) if obi < 16 else ("v", obi - 16))

    kvis = bigb[:, 44032:44544]
    p.dma("sync", wstg[0], w_src(0), writes=["ws0"], sem="wl0")
    p.dma("sync", wstg[1], w_src(1), writes=["ws1"], sem="wl1")
    p.op("vector", lambda e: e.tensor_copy(w_sb[0], wstg[0]),
         incs=[("dve", 1)], reads=["ws0"], writes=["w0"])

    for s in range(96):
        qi, obi = s // 24, s % 24
        kind, h = kind_h(obi)
        if obi == 0:
            for ci in range(32):
                p.dma("sync", xn_sb[ci], xn_ag_out.ap()[qi * 32 + ci, :, :],
                      reads=["xnag"],
                      writes=[f"xn_{ci}"] + (["xnq"] if ci == 31 else []))
        if s + 2 < 96:
            p.dma("sync", wstg[s % 2], w_src(s + 2), writes=[f"ws{s % 2}"],
                  sem=f"wl{s % 2}")
        if s + 1 < 96:
            p.op("vector", lambda e, b=(s + 1) % 2: e.tensor_copy(w_sb[b], wstg[b]),
                 incs=[("dve", 1)], reads=[f"ws{(s + 1) % 2}"], writes=[f"w{(s + 1) % 2}"])
        wres = f"w{s % 2}"
        slot0, slot1 = (s * 2) % 6, (s * 2 + 1) % 6
        for ic in range(32):
            p.op("tensor", lambda e, ic=ic, w=w_sb[s % 2], ps=PS6[slot0]:
                 e.matmul(ps, w[:, ic * 128:(ic + 1) * 128],
                          xn_sb[ic][:, 0:512],
                          start=(ic == 0), stop=(ic == 31)),
                 incs=[("pe", 1)],
                 reads=[wres, f"xn_{ic}"] + (["xnq"] if ic == 0 else []),
                 writes=[PS6N[slot0]] if ic in (0, 31) else [])
            p.op("tensor", lambda e, ic=ic, w=w_sb[s % 2], ps=PS6[slot1]:
                 e.matmul(ps, w[:, ic * 128:(ic + 1) * 128],
                          xn_sb[ic][:, 512:1024],
                          start=(ic == 0), stop=(ic == 31)),
                 incs=[("pe", 1)],
                 reads=[wres, f"xn_{ic}"],
                 writes=[PS6N[slot1]] if ic in (0, 31) else [])
        for rng in range(2):
            slot = (s * 2 + rng) % 6
            pres = PS6N[slot]
            tok0 = qi * 1024 + rng * 512
            if kind in ("q", "k"):
                p.op("scalar", lambda e, ps=PS6[slot]: e.activation(qraw, ps, AF.Copy),
                     incs=[("act", 1)], reads=[pres], writes=["qraw"])
                stres = f"qst{s % 2}"
                p.op("vector", lambda e, t0=tok0: e.tensor_mul(
                    tcos, qraw, cosb[:, t0:t0 + 512]),
                     incs=[("dve", 1)], reads=["qraw"], writes=["tcos"])
                p.op("vector", lambda e: e.stream_shuffle(rotb, qraw, SWAP_MASK),
                     incs=[("dve", 1)], reads=["qraw"], writes=["rot"])
                p.op("vector", lambda e, t0=tok0: e.tensor_mul(
                    rotb, rotb, sinb[:, t0:t0 + 512]),
                     incs=[("dve", 1)], reads=["rot"], writes=["rot"])
                p.op("vector", lambda e, sb2=qst[s % 2]: e.tensor_add(
                    sb2, tcos, rotb),
                     incs=[("dve", 1)], reads=["tcos", "rot"], writes=[stres])
                dst = (qs if kind == "q" else ks).ap()[h, :, tok0:tok0 + 512]
                p.dma("scalar", dst, qst[s % 2],
                      reads=[stres], writes=[f"sp_{kind}{h}_{qi}{rng}"],
                      sem=f"st{s % 2}")
                if kind == "q":
                    lo = max(INSTR_START, tok0); hi = min(INSTR_END, tok0 + 512)
                    if lo < hi:
                        p.op("vector", lambda e, sb2=qst[s % 2],
                             lo=lo, hi=hi, t0=tok0: e.reduce_sum(
                                 tmpw, sb2[:, lo - t0:hi - t0], AX.X),
                             incs=[("dve", 1)], reads=[stres], writes=["tmpw"])
                        p.op("vector", lambda e, h=h: e.tensor_add(
                            qbarf[:, h:h + 1], qbarf[:, h:h + 1], tmpw),
                             incs=[("dve", 1)], reads=["tmpw", "qbar"], writes=["qbar"])
            else:
                p.op("scalar", lambda e, ps=PS6[slot]: e.activation(vstage, ps, AF.Copy),
                     incs=[("act", 1)], reads=[pres], writes=["vst"])
                for i4 in range(4):
                    p.op("tensor", lambda e, i4=i4: e.transpose(
                        psT[:, i4 * 128:(i4 + 1) * 128],
                        vstage[:, i4 * 128:(i4 + 1) * 128], identb[:, :]),
                         incs=[("pe", 1)], reads=["vst"], writes=[f"psT{i4}"])
                stres = f"qst{s % 2}"
                p.op("vector", lambda e, v2=qst[s % 2]: e.tensor_copy(
                    v2, psT[:, 0:512]),
                     incs=[("dve", 1)], reads=[f"psT{i4}" for i4 in range(4)],
                     writes=[stres])
                kt0 = tok0 // 128
                p.dma("scalar", vs.ap()[h, kt0:kt0 + 4, :, :].rearrange(
                    "t p c -> p t c"),
                      qst[s % 2].rearrange("p (t c) -> p t c", c=128),
                      reads=[stres], writes=[f"sp_v{h}_{qi}{rng}"],
                      sem=f"st{s % 2}")
        if s == 72 + 7:
            p.op("vector", lambda e: e.tensor_copy(qbarb[:, :], qbarf[:, :]),
                 incs=[("dve", 1)], reads=["qbar"], writes=["qbarb"])
            for h2 in range(H_PER):
                p.dma("sync", kvis[:, :], ks.ap()[h2, :, 1:513],
                      reads=[f"sp_k{h2}_00"], writes=["kvis"])
                p.op("tensor", lambda e, h2=h2: e.matmul(
                    psD[0:1, :], qbarb[:, h2:h2 + 1], kvis[:, :],
                    start=(h2 == 0), stop=(h2 == 7)),
                     incs=[("pe", 1)], reads=["kvis", "qbarb"],
                     writes=(["psDn"] if h2 in (0, 7) else []))
            p.op("vector", lambda e: e.tensor_scalar(imps[:, :], psD[0:1, :],
                                                     impscb[0:1, 0:1], None, ALU.mult),
                 incs=[("dve", 1)], reads=["psDn", "consts"], writes=["imps"])
            p.dma("scalar", imp_in.ap()[:, :], imps[:, :], reads=["imps"],
                  writes=["impin"], sem="st")
            p.op("gpsimd", lambda e: e.collective_compute(
                "AllReduce", ALU.add, replica_groups=G8,
                ins=[imp_in.ap().opt()], outs=[imp_out.ap().opt()]),
                 incs=[("cc", 1)], reads=["impin"], writes=["impout"])
    p.op("vector", lambda e: e.memset(impktpm[:, :], 3.0e38),
         incs=[("dve", 1)], writes=["iktpm"])
    p.dma("sync", impbc[:, :], imp_out.ap()[0:1, :].partition_broadcast(128),
          reads=["impout"], writes=["impbc"])
    p.dma("sync", impktpm[1:128, 0:1], imp_out.ap()[0:1, 0:127].rearrange("a c -> c a"),
          reads=["impout"], writes=["iktpm"])
    for j in range(1, 4):
        p.dma("sync", impktpm[:, j:j + 1],
              imp_out.ap()[0:1, j * 128 - 1:(j + 1) * 128 - 1].rearrange("a c -> c a"),
              reads=["impout"], writes=["iktpm"])
    p.dma("sync", impktpm[0:1, 4:5], imp_out.ap()[0:1, 511:512],
          reads=["impout"], writes=["iktpm"])
    gtbuf = bigf[:, 12288:12800]
    cntt = smallf[:, 16:24]
    for j in range(5):
        p.op("vector", lambda e, j=j: e.tensor_scalar(
            gtbuf, impbc[:, :], impktpm[:, j:j + 1], None, ALU.is_gt),
             incs=[("dve", 1)], reads=["impbc", "iktpm"], writes=["gt"])
        p.op("vector", lambda e, j=j: e.reduce_sum(cntt[:, j:j + 1], gtbuf, AX.X),
             incs=[("dve", 1)], reads=["gt"], writes=[f"cnt{j}"])
    p.op("vector", lambda e: e.tensor_scalar(biaskt[:, 0:5], cntt[:, 0:5], 255.5,
                                             None, ALU.is_lt),
         incs=[("dve", 1)], reads=[f"cnt{j}" for j in range(5)], writes=["bk"])
    p.op("vector", lambda e: e.tensor_scalar(biaskt[:, 0:5], biaskt[:, 0:5],
                                             1.0e6 - SHIFT, -1.0e6, ALU.mult, ALU.add),
         incs=[("dve", 1)], reads=["bk"], writes=["bk"])

    p.barrier()

    qh = [bigb[:, 0:4096], bigb[:, 4096:8192]]
    kh = [bigb[:, 8192:12288], bigb[:, 12288:16384]]
    vh = [bigb[:, 16384:20480], bigb[:, 20480:24576]]
    p_sb = bigb[:, 24576:40960]
    atst = [bigb[:, 40960:41472], bigb[:, 41472:41984]]
    denf = bigf[:, 0:512]
    recipf = bigf[:, 512:1024]

    for h in range(H_PER):
        hb = h % 2
        p.dma("sync", qh[hb], qs.ap()[h, :, :],
              reads=[f"sp_q{h}_{qi}{rng}" for qi in range(4) for rng in range(2)],
              writes=[f"qh{hb}"], sem=f"ld{hb}")
        p.dma("sync", kh[hb], ks.ap()[h, :, :],
              reads=[f"sp_k{h}_{qi}{rng}" for qi in range(4) for rng in range(2)],
              writes=[f"kh{hb}"], sem=f"ld{hb}")
        p.dma("sync", vh[hb].rearrange("p (t c) -> p t c", c=128),
              vs.ap()[h, :, :, :].rearrange("t p c -> p t c"),
              reads=[f"sp_v{h}_{qi}{rng}" for qi in range(4) for rng in range(2)],
              writes=[f"vh{hb}"], sem=f"ld{hb}")
        SC3 = [psA, psB, psTf]
        SC3N = ["sc0", "sc1", "sc2"]
        DEPTH = 2
        for r in range(8):
            nkt = 4 * (r + 1)
            npair = nkt // 2

            def emit_scores(pp, r=r, hb=hb, nkt=nkt):
                jp = 2 * pp
                pb = SC3[pp % 3]
                pres = SC3N[pp % 3]
                for o2 in range(2):
                    j = jp + o2
                    p.op("tensor", lambda e, j=j, r=r, pb=pb, o2=o2, hb=hb: e.matmul(
                        pb[:, o2 * 512:(o2 + 1) * 512],
                        kh[hb][:, j * 128:(j + 1) * 128],
                        qh[hb][:, r * 512:(r + 1) * 512],
                        start=True, stop=True),
                         incs=[("pe", 1)], reads=[f"qh{hb}", f"kh{hb}"],
                         writes=[pres])
                if jp < 6:
                    for j in (jp, jp + 1):
                        o2 = j - jp
                        if j < 5:
                            p.op("scalar", lambda e, j=j, pb=pb, o2=o2: e.activation(
                                p_sb[:, j * 512:(j + 1) * 512],
                                pb[:, o2 * 512:(o2 + 1) * 512], AF.Exp,
                                bias=biaskt[:, j:j + 1], scale=SCALE),
                                 incs=[("act", 1)], reads=[pres, "bk"],
                                 writes=[f"p{j}"])
                        else:
                            p.op("scalar", lambda e, j=j, pb=pb, o2=o2: e.activation(
                                p_sb[:, j * 512:(j + 1) * 512],
                                pb[:, o2 * 512:(o2 + 1) * 512], AF.Exp,
                                bias=negsh, scale=SCALE),
                                 incs=[("act", 1)], reads=[pres, "nsh"],
                                 writes=[f"p{j}"])
                else:
                    p.op("scalar", lambda e, j0=jp, pb=pb: e.activation(
                        p_sb[:, j0 * 512:(j0 + 2) * 512], pb[:, 0:1024], AF.Exp,
                        bias=negsh, scale=SCALE),
                         incs=[("act", 1)], reads=[pres, "nsh"],
                         writes=[f"p{jp}", f"p{jp + 1}"])
                for o2 in range(2):
                    j = jp + o2
                    dj = j - 4 * r
                    if 0 <= dj < 4:
                        p.op("vector", lambda e, j=j, dj=dj: e.tensor_mul(
                            p_sb[:, j * 512:(j + 1) * 512],
                            p_sb[:, j * 512:(j + 1) * 512],
                            trib[:, dj * 512:(dj + 1) * 512]),
                             incs=[("dve", 1)], reads=[f"p{j}"], writes=[f"p{j}"])

            def emit_consume(qp, hb=hb, nkt=nkt):
                jq = 2 * qp
                for j in (jq, jq + 1):
                    p.op("tensor", lambda e, j=j, nkt=nkt: e.matmul(
                        psD[:, :], onesb[:, :], p_sb[:, j * 512:(j + 1) * 512],
                        start=(j == 0), stop=(j == nkt - 1)),
                         incs=[("pe", 1)], reads=[f"p{j}"],
                         writes=["psDn"] if j in (0, nkt - 1) else [])
                for j in (jq, jq + 1):
                    p.op("tensor", lambda e, j=j, hb=hb, nkt=nkt: e.matmul(
                        psO[:, :], vh[hb][:, j * 128:(j + 1) * 128],
                        p_sb[:, j * 512:(j + 1) * 512],
                        start=(j == 0), stop=(j == nkt - 1)),
                         incs=[("pe", 1)], reads=[f"p{j}", f"vh{hb}"],
                         writes=["psOu"] if j in (0, nkt - 1) else [])

            for pp in range(npair + DEPTH):
                if pp < npair:
                    emit_scores(pp)
                if pp >= DEPTH and pp - DEPTH < npair:
                    emit_consume(pp - DEPTH)
            p.op("scalar", lambda e: e.activation(denf, psD[:, :], AF.Ln),
                 incs=[("act", 1)], reads=["psDn"], writes=["denf"])
            p.op("scalar", lambda e: e.activation(recipf, denf, AF.Exp, scale=-1.0),
                 incs=[("act", 1)], reads=["denf"], writes=["recip"])
            ares = f"at{r % 2}"
            p.op("vector", lambda e, ab=atst[r % 2]: e.tensor_mul(ab, psO[:, :], recipf),
                 incs=[("dve", 1)], reads=["psOu", "recip"], writes=[ares])
            p.dma("scalar", at_sp.ap()[h, :, r * 512:(r + 1) * 512], atst[r % 2],
                  reads=[ares], writes=[f"at_{h}_{r}"], sem=f"st{r % 2}")

    p.barrier()

    owb = [bigb[:, 41984:43008], bigb[:, 43008:44032]]
    atc = [bigb[:, hc * 4096:(hc + 1) * 4096] for hc in range(8)]
    sstg = [bigb[:, 44032:44544], bigb[:, 44544:45056]]

    for hc in range(8):
        p.dma("sync", atc[hc], at_sp.ap()[hc, :, :],
              reads=[f"at_{hc}_{r}" for r in range(8)],
              writes=[f"atc{hc}"], sem=f"ld{hc % 2}")
    owstg = [bigf[:, 8192:9216], bigf[:, 9216:10240]]
    p.dma("sync", owstg[0], ow_sl[0, :, :], writes=["ows0"], sem="wl0")
    p.dma("sync", owstg[1], ow_sl[1, :, :], writes=["ows1"], sem="wl1")
    p.op("vector", lambda e: e.tensor_copy(owb[0], owstg[0]),
         incs=[("dve", 1)], reads=["ows0"], writes=["ow0"])
    for ob in range(32):
        wres = f"ow{ob % 2}"
        if ob + 2 < 32:
            p.dma("sync", owstg[ob % 2], ow_sl[ob + 2, :, :], writes=[f"ows{ob % 2}"],
                  sem=f"wl{ob % 2}")
        if ob + 1 < 32:
            p.op("vector", lambda e, b=(ob + 1) % 2: e.tensor_copy(owb[b], owstg[b]),
                 incs=[("dve", 1)], reads=[f"ows{(ob + 1) % 2}"], writes=[f"ow{(ob + 1) % 2}"])
        PSP8 = [psA[:, 0:512], psA[:, 512:1024], psB[:, 0:512], psB[:, 512:1024],
                psD[:, :], psO[:, :], psTf[:, 0:512], psTf[:, 512:1024]]
        for hc in range(8):
            for rr in range(8):
                p.op("tensor", lambda e, hc=hc, ob=ob, rr=rr, ps=PSP8[rr]: e.matmul(
                    ps, owb[ob % 2][:, hc * 128:(hc + 1) * 128],
                    atc[hc][:, rr * 512:(rr + 1) * 512],
                    start=(hc == 0), stop=(hc == 7)),
                     incs=[("pe", 1)],
                     reads=[wres, f"atc{hc}"],
                     writes=([f"psQ{rr}"] if hc in (0, 7) else []))
        for rr in range(8):
            sres = f"ss{rr % 2}"
            p.op("vector", lambda e, ps=PSP8[rr], sb2=sstg[rr % 2]: e.tensor_copy(sb2, ps),
                 incs=[("dve", 1)], reads=[f"psQ{rr}"], writes=[sres])
            qq, hh = rr // 2, rr % 2
            p.dma("scalar", s_rs_in[ob // 8].ap()[qq, ob % 8, :, hh * 512:(hh + 1) * 512],
                  sstg[rr % 2], reads=[sres], writes=[f"srs_{ob}_{rr}"],
                  sem=f"st{rr % 2}")
        if ob % 8 == 7:
            g = ob // 8
            p.op("gpsimd", lambda e, g=g: e.collective_compute(
                "ReduceScatter", ALU.add, replica_groups=G4,
                ins=[s_rs_in[g].ap().opt()], outs=[s_rs_out[g].ap().opt()]),
                 incs=[("cc", 1)],
                 reads=[f"srs_{ob2}_{rr}" for ob2 in range(g * 8, g * 8 + 8)
                        for rr in range(8)],
                 writes=[f"srs{g}"])

    p.barrier()

    for ci in range(32):
        p.dma("sync", xn_sb[ci], xn_ag_in.ap()[ci, :, :],
              writes=[f"xn_{ci}"] + (["xnq"] if ci == 31 else []))
    PSP = [psA[:, 0:512], psA[:, 512:1024]]
    wgb = bigb[:, 45056:45312]
    wgstg = bigf[:, 10240:10496]
    p.dma("sync", wgstg, wg_sl[:, :], writes=["wgs"], sem="wl0")
    p.op("vector", lambda e: e.tensor_copy(wgb, wgstg),
         incs=[("dve", 1)], reads=["wgs"], writes=["wgb"])
    logitf = bigf[:, 0:1024]
    for rng in range(2):
        for ic in range(32):
            p.op("tensor", lambda e, ic=ic, rng=rng: e.matmul(
                psD[0:8, 0:512], wgb[:, ic * 8:(ic + 1) * 8],
                xn_sb[ic][:, rng * 512:(rng + 1) * 512],
                start=(ic == 0), stop=(ic == 31)),
                 incs=[("pe", 1)],
                 reads=["wgb", f"xn_{ic}"] + (["xnq"] if ic == 0 else []),
                 writes=["psDn"] if ic in (0, 31) else [])
        p.op("vector", lambda e, rng=rng: e.tensor_copy(
            logitf[0:8, rng * 512:(rng + 1) * 512], psD[0:8, 0:512]),
             incs=[("dve", 1)], reads=["psDn"], writes=[f"lg{rng}"])
    routef = bigb[:, 45312:46336]
    ltt = bigf[:, 2048:2056]
    m1 = smallf[:, 40:41]; m2 = smallf[:, 41:42]; nm1 = smallf[:, 42:43]
    eqm = smallf[:, 43:51]; l2f = smallf[:, 51:59]
    for t2 in range(8):
        p.op("tensor", lambda e, t2=t2: e.transpose(
            psD[:, 0:8], logitf[0:8, t2 * 128:(t2 + 1) * 128], identf[0:8, 0:8]),
             incs=[("pe", 1)], reads=["lg0", "lg1"], writes=["psDn"])
        p.op("vector", lambda e: e.tensor_copy(ltt, psD[:, 0:8]),
             incs=[("dve", 1)], reads=["psDn"], writes=["ltt"])
        p.op("vector", lambda e: e.reduce_max(m1, ltt, AX.X),
             incs=[("dve", 1)], reads=["ltt"], writes=["gm1"])
        p.op("vector", lambda e: e.tensor_scalar(eqm, ltt, m1, None, ALU.is_equal),
             incs=[("dve", 1)], reads=["ltt", "gm1"], writes=["geq"])
        p.op("vector", lambda e: e.scalar_tensor_tensor(l2f, eqm, -30000.0, ltt,
                                                        ALU.mult, ALU.add),
             incs=[("dve", 1)], reads=["geq", "ltt"], writes=["gl2"])
        p.op("vector", lambda e: e.reduce_max(m2, l2f, AX.X),
             incs=[("dve", 1)], reads=["gl2"], writes=["gm2"])
        p.op("vector", lambda e: e.tensor_scalar(eqm, ltt, m2, None, ALU.is_ge),
             incs=[("dve", 1)], reads=["ltt", "gm2"], writes=["geq"])
        p.op("vector", lambda e: e.tensor_scalar(nm1, m1, -1.0, None, ALU.mult),
             incs=[("dve", 1)], reads=["gm1"], writes=["gnm1"])
        p.op("scalar", lambda e: e.activation(l2f, ltt, AF.Exp, bias=nm1),
             incs=[("act", 1)], reads=["ltt", "gnm1", "gl2"], writes=["gl2"])
        p.op("vector", lambda e: e.tensor_mul(l2f, l2f, eqm),
             incs=[("dve", 1)], reads=["gl2", "geq"], writes=["gl2"])
        p.op("vector", lambda e: e.reduce_sum(m2, l2f, AX.X),
             incs=[("dve", 1)], reads=["gl2"], writes=["gm2"])
        p.op("vector", lambda e: e.reciprocal(m1, m2),
             incs=[("dve", 1)], reads=["gm2"], writes=["gm1"])
        p.op("vector", lambda e: e.tensor_scalar(l2f, l2f, m1, None, ALU.mult),
             incs=[("dve", 1)], reads=["gl2", "gm1"], writes=["gl2"])
        p.op("tensor", lambda e: e.transpose(psD[0:8, 128:256], l2f, identf[:, :]),
             incs=[("pe", 1)], reads=["gl2"], writes=["psDn"])
        p.op("vector", lambda e, t2=t2: e.tensor_copy(
            routef[0:8, t2 * 128:(t2 + 1) * 128], psD[0:8, 128:256]),
             incs=[("dve", 1)], reads=["psDn"], writes=[f"rt{t2}"])
    afb = bigb[:, 40960:45056]
    hsb = bigb[:, 46336:47360]
    hwb = bigb[:, 47360:48384]
    afstg = bigf[:, 0:4096]
    p.dma("sync", afstg, af_sl[:, :],
          writes=["afs", "ows0", "ows1", "lg0", "lg1", "ltt"], sem="wl1")
    p.op("vector", lambda e: e.tensor_copy(afb, afstg),
         incs=[("dve", 1)], reads=["afs"], writes=["afb", "ow0", "ow1", "ss0", "ss1"])
    for rng in range(2):
        for ic in range(32):
            p.op("tensor", lambda e, ic=ic, rng=rng: e.matmul(
                psO[:, :], afb[:, ic * 128:(ic + 1) * 128],
                xn_sb[ic][:, rng * 512:(rng + 1) * 512],
                start=(ic == 0), stop=(ic == 31)),
                 incs=[("pe", 1)],
                 reads=["afb", f"xn_{ic}"] + (["xnq"] if ic == 0 else []),
                 writes=["psOu"] if ic in (0, 31) else [])
        p.op("vector", lambda e, rng=rng: e.tensor_copy(
            hsb[:, rng * 512:(rng + 1) * 512], psO[:, :]),
             incs=[("dve", 1)], reads=["psOu"], writes=[f"h{rng}"])
        p.op("tensor", lambda e, rng=rng: e.matmul(
            psD[:, :], e16b[:, :], routef[0:8, rng * 512:(rng + 1) * 512],
            start=True, stop=True),
             incs=[("pe", 1)], reads=[f"rt{t2}" for t2 in range(8)] + ["consts"],
             writes=["psDn"])
        p.op("vector", lambda e, rng=rng: e.scalar_tensor_tensor(
            hwb[:, rng * 512:(rng + 1) * 512], psD[:, :], 1.0,
            hsb[:, rng * 512:(rng + 1) * 512], ALU.mult, ALU.mult),
             incs=[("dve", 1)], reads=["psDn", f"h{rng}"], writes=[f"hw{rng}"])
    wbb = [bigb[:, 32768:36864], bigb[:, 36864:40960]]
    bfb = bigb[:, 40960:45056]
    bfstg = bigf[:, 4096:8192]
    p.dma("sync", bfstg, BfT[:, :], writes=["bfs"], sem="wl0")
    p.op("vector", lambda e: e.tensor_copy(bfb[:, 0:4096], bfstg),
         incs=[("dve", 1)], reads=["bfs"], writes=["afb", "bfb"])
    gstg = [bigb[:, 46336:46848], bigb[:, 46848:47360]]
    wbstg = [bigf[:, 0:4096], bigf[:, 4096:8192]]
    p.dma("sync", wbstg[0], wb_sl[0, :, :], writes=["wbs0", "afs", "bfs"], sem="wl0")
    p.dma("sync", wbstg[1], wb_sl[1, :, :], writes=["wbs1", "bfs"], sem="wl1")
    p.op("vector", lambda e: e.tensor_copy(wbb[0], wbstg[0]),
         incs=[("dve", 1)], reads=["wbs0"], writes=["wb0"])
    for ob in range(32):
        wres = f"wb{ob % 2}"
        if ob + 2 < 32:
            p.dma("sync", wbstg[ob % 2], wb_sl[ob + 2, :, :], writes=[f"wbs{ob % 2}"],
                  sem=f"wl{ob % 2}")
        if ob + 1 < 32:
            p.op("vector", lambda e, b=(ob + 1) % 2: e.tensor_copy(wbb[b], wbstg[b]),
                 incs=[("dve", 1)], reads=[f"wbs{(ob + 1) % 2}"], writes=[f"wb{(ob + 1) % 2}"])
        for ic in range(32):
            for rng in range(2):
                p.op("tensor", lambda e, ic=ic, ob=ob, rng=rng, ps=PSP[rng % 2]: e.matmul(
                    ps, wbb[ob % 2][:, ic * 128:(ic + 1) * 128],
                    xn_sb[ic][:, rng * 512:(rng + 1) * 512],
                    start=(ic == 0), stop=False),
                     incs=[("pe", 1)],
                     reads=[wres, f"xn_{ic}"] + (["xnq"] if ic == 0 and rng == 0 else []),
                     writes=(["psP%d" % (rng % 2)] if ic == 0 else []))
        for rng in range(2):
            pres = "psP%d" % (rng % 2)
            p.op("tensor", lambda e, ob=ob, rng=rng, ps=PSP[rng % 2]: e.matmul(
                ps, bfb[:, ob * 128:(ob + 1) * 128],
                hwb[:, rng * 512:(rng + 1) * 512],
                start=False, stop=True),
                 incs=[("pe", 1)], reads=["bfb", f"hw{rng}"], writes=[pres])
            gres = f"gs{rng}"
            p.op("scalar", lambda e, ob=ob, ps=PSP[rng % 2], sb2=gstg[rng]: e.activation(
                sb2, ps, AF.Silu, bias=bpmb[:, ob:ob + 1]),
                 incs=[("act", 1)], reads=[pres, "consts"],
                 writes=[gres] + ([f"h{rng}"] if ob == 0 else []))
            p.dma("scalar", g_sp.ap()[ob, :, rng * 512:(rng + 1) * 512], gstg[rng],
                  reads=[gres], writes=[f"g_{ob}_{rng}"], sem=f"st{rng}")

    p.barrier()

    ssb = [bigb[:, 0:1024], bigb[:, 1024:2048]]
    gsb = [bigb[:, 2048:3072], bigb[:, 3072:4096]]
    sgb = bigb[:, 4096:5120]
    sgT = bigb[:, 5120:6144]
    for ob in range(32):
        p.dma("sync", ssb[ob % 2], s_rs_out[ob // 8].ap()[ob % 8, :, :],
              reads=[f"srs{ob // 8}"], writes=[f"sq{ob % 2}"], sem=f"ld{ob % 2}")
        p.dma("sync", gsb[ob % 2], g_sp.ap()[ob, :, :],
              reads=[f"g_{ob}_0", f"g_{ob}_1"], writes=[f"gq{ob % 2}"],
              sem=f"ld{ob % 2}")
        p.op("vector", lambda e, ob=ob: e.tensor_mul(sgb, ssb[ob % 2], gsb[ob % 2]),
             incs=[("dve", 1)], reads=[f"sq{ob % 2}", f"gq{ob % 2}"],
             writes=["sgb"])
        for t2 in range(8):
            s8 = SEQ8[t2]
            p.op("tensor", lambda e, t2=t2, c0=PTC[s8]: e.transpose(
                psT[:, c0:c0 + 128],
                sgb[:, t2 * 128:(t2 + 1) * 128], identb[:, :]),
                 incs=[("pe", 1)], reads=["sgb"], writes=[f"psT{s8}"])
            p.op("vector", lambda e, t2=t2, c0=PTC[s8]: e.tensor_copy(
                sgT[:, t2 * 128:(t2 + 1) * 128], psT[:, c0:c0 + 128]),
                 incs=[("dve", 1)], reads=[f"psT{s8}"], writes=[f"sgT_{t2}"])
        p.dma("scalar", sgT_sp.ap()[:, :, ob * 128:(ob + 1) * 128].rearrange(
            "t p c -> p t c"), sgT.rearrange("p (t c) -> p t c", c=128),
              reads=[f"sgT_{t2}" for t2 in range(8)], writes=[f"sgsp{ob}"], sem="st")
    xf = bigf[:, 0:4096]
    of = [bigf[:, 4096:8192], bigf[:, 8192:12288]]
    sgTl = [bigb[:, 6144:10240], bigb[:, 10240:14336]]
    for t2 in range(8):
        p.dma("sync", sgTl[t2 % 2], sgT_sp.ap()[t2, :, :],
              reads=[f"sgsp{ob}" for ob in range(32)] if t2 < 2 else [],
              writes=[f"sgl{t2 % 2}"], sem=f"ld{t2 % 2}")
        p.dma("sync", xf, x_q[t2 * 128:(t2 + 1) * 128, :], writes=["xf"])
        ores = f"of{t2 % 2}"
        p.op("vector", lambda e, t2=t2: e.scalar_tensor_tensor(
            of[t2 % 2], sgTl[t2 % 2], 1.0, xf, ALU.mult, ALU.add),
             incs=[("dve", 1)], reads=[f"sgl{t2 % 2}", "xf"], writes=[ores])
        p.dma("sync", out_q[t2 * 128:(t2 + 1) * 128, :], of[t2 % 2],
              reads=[ores], writes=[f"outd{t2}"], sem="st")
    p.op("vector", lambda e: e.engine_nop(),
         waits=[("st", p.n("st"))], incs=[("go", 1)])

    with nc.Block() as block:
        p.emit(block)
    for g in reversed(ctxs):
        g.__exit__(None, None, None)
    return nc


_NC_CACHE = None


def _get_nc():
    global _NC_CACHE
    if _NC_CACHE is None:
        _NC_CACHE = build_nc()
    return _NC_CACHE


def _host_prep(inputs):
    x = inputs["L0_attn_hidden_states"]
    qw = inputs["L0_attn_q_w"]; kw = inputs["L0_attn_k_w"]
    vw = inputs["L0_attn_v_w"]; ow = inputs["L0_attn_o_w"]
    Wb = inputs["L0_ffn_W_base"]; bb = inputs["L0_ffn_b_base"]
    Wg = inputs["L0_ffn_W_gate"]; A = inputs["L0_ffn_A"]; Bm = inputs["L0_ffn_B"]
    BF = ml_dtypes.bfloat16

    perm = np.empty(128, np.int64)
    perm[0::2] = np.arange(64)
    perm[1::2] = np.arange(64, 128)
    inv_freq = 1.0 / (10000.0 ** (np.arange(0, D, 2, dtype=np.float64) / D))
    freqs = np.arange(S, dtype=np.float64)[:, None] * inv_freq[None, :]
    cos64 = np.cos(freqs); sin64 = np.sin(freqs)
    cos_il = np.empty((128, S), np.float32)
    sin_il = np.empty((128, S), np.float32)
    cos_il[0::2] = cos64.T.astype(np.float32); cos_il[1::2] = cos64.T.astype(np.float32)
    sin_il[0::2] = (-sin64.T).astype(np.float32); sin_il[1::2] = sin64.T.astype(np.float32)
    cos_il = cos_il.astype(BF); sin_il = sin_il.astype(BF)

    pp, ff = np.mgrid[0:128, 0:512]
    tri = np.zeros((128, 2048), np.float32)
    for dj in range(4):
        tri[:, dj * 512:(dj + 1) * 512] = (ff >= pp + 128 * dj).astype(np.float32)
    tri = tri.astype(BF)
    e16 = np.zeros((8, 128), np.float32)
    for e in range(8):
        e16[e, e * 16:(e + 1) * 16] = 1.0
    e16 = e16.astype(BF)
    identb = np.eye(128, dtype=np.float32).astype(BF)
    identf = np.eye(128, dtype=np.float32)
    ones = np.ones((128, 128), np.float32).astype(BF)
    b_pm = np.ascontiguousarray(bb.reshape(32, 128).T)

    WbT = Wb.T
    wb_sl = np.ascontiguousarray(
        WbT.reshape(32, 128, 32, 128).transpose(2, 1, 0, 3).reshape(32, 128, 4096))
    wg_sl = np.ascontiguousarray(
        Wg.T.reshape(32, 128, 8).transpose(1, 0, 2).reshape(128, 256))
    af_sl = np.ascontiguousarray(
        A.reshape(128, HID).T.reshape(32, 128, 128).transpose(1, 0, 2).reshape(128, 4096))
    BfT = np.ascontiguousarray(Bm.transpose(0, 2, 1).reshape(128, HID))

    in_maps = []
    for c in range(8):
        b, i = c // 4, c % 4
        hsl = slice(i * 1024, (i + 1) * 1024)
        qwh = qw[hsl].reshape(8, 128, HID)[:, perm, :].reshape(1024, HID)
        kwh = kw[hsl].reshape(8, 128, HID)[:, perm, :].reshape(1024, HID)

        def slab_qkv(wT):
            return np.ascontiguousarray(
                wT.reshape(32, 128, 8, 128).transpose(2, 1, 0, 3).reshape(8, 128, 4096))

        owT = ow[:, hsl].T
        ow_sl = np.ascontiguousarray(
            owT.reshape(8, 128, 32, 128).transpose(2, 1, 0, 3).reshape(32, 128, 1024))
        in_maps.append({
            "x_q": np.ascontiguousarray(x[b, i * 1024:(i + 1) * 1024, :]),
            "qw_sl": slab_qkv(qwh.T),
            "kw_sl": slab_qkv(kwh.T),
            "vw_sl": slab_qkv(vw[hsl].T),
            "ow_sl": ow_sl,
            "wb_sl": wb_sl, "wg_sl": wg_sl, "af_sl": af_sl, "BfT": BfT, "b_pm": b_pm,
            "cos_il": cos_il, "sin_il": sin_il, "tri_in": tri, "e16_in": e16,
            "ident_in": identb, "identf_in": identf, "ones_in": ones,
            "impsc_in": np.array([[SCALE / (32.0 * (INSTR_END - INSTR_START))
                                   if b == 0 else 0.0]], np.float32),
        })
    return in_maps


def kernel(**inputs):
    nc = _get_nc()
    in_maps = _host_prep(inputs)
    res = run_bass_kernel_spmd(nc, in_maps, core_ids=list(range(8)))
    out = np.empty((2, S, HID), np.float32)
    for c in range(8):
        b, i = c // 4, c % 4
        out[b, i * 1024:(i + 1) * 1024, :] = res.results[c]["out_q"]
    return out


# version 17
# speedup vs baseline: 1.0700x; 1.0493x over previous
import math
import numpy as np
import ml_dtypes
import concourse.bass as bass
import concourse.mybir as mybir
from concourse.bass_utils import run_bass_kernel_spmd

F32 = mybir.dt.float32
BF16 = mybir.dt.bfloat16
AF = mybir.ActivationFunctionType
ALU = mybir.AluOpType
AX = mybir.AxisListType

S = 4096
HID = 4096
H_PER = 8
D = 128
NQ = 1024
SHIFT = 8.0
SCALE = 1.0 / math.sqrt(D)
INSTR_START, INSTR_END = 513, 4040
G4 = [[0, 1, 2, 3], [4, 5, 6, 7]]
G8 = [[0, 1, 2, 3, 4, 5, 6, 7]]
SWAP_MASK = [m for i in range(0, 32, 2) for m in (i + 1, i)]
ENGINES = ("sync", "gpsimd", "scalar", "vector", "tensor")


class Prog:

    def __init__(self, nc, sems):
        self.nc = nc
        self.sems = sems
        self.steps = {k: [] for k in ENGINES}
        self.cnt = {s: 0 for s in sems}
        self.res = {}
        self.pending = {k: [] for k in ENGINES}

    def op(self, eng, fn, waits=(), incs=(), reads=(), writes=()):
        wl = list(waits) + self.pending[eng]
        self.pending[eng] = []
        for r in reads:
            st = self.res.get(r)
            if st and st["w"]:
                wl.append(st["w"])
        for r in writes:
            st = self.res.get(r)
            if st:
                wl.extend(st["r"])
                if st["w"]:
                    wl.append(st["w"])
        wd = {}
        for s, v in wl:
            assert v <= self.cnt[s], f"wait {s}>={v} unproducible (at {self.cnt[s]})"
            wd[s] = max(wd.get(s, 0), v)
        assert incs, "every op needs a completion inc"
        for s, k in incs:
            self.cnt[s] += k
        self.steps[eng].append((fn, sorted(wd.items()), list(incs)))
        done = (incs[-1][0], self.cnt[incs[-1][0]])
        for r in reads:
            self.res.setdefault(r, {"w": None, "r": []})["r"].append(done)
        for r in writes:
            self.res[r] = {"w": done, "r": []}
        return done

    def dma(self, eng, out, in_, waits=(), reads=(), writes=(), sem="ld", **kw):
        def fn(e, out=out, in_=in_, kw=kw):
            return e.dma_start(out=out, in_=in_, **kw)
        return self.op(eng, fn, waits, [(sem, 16)], reads, writes)

    def barrier(self, exclude=("cc", "go")):
        snap = [(s, v) for s, v in self.cnt.items() if s not in exclude and v > 0]
        for e in ENGINES:
            self.pending[e] = list(snap)

    def n(self, name):
        return self.cnt[name]

    def emit(self, block):
        sems = self.sems

        def runner(name):
            def run(e):
                for fn, waits, incs in self.steps[name]:
                    for s, v in waits:
                        e.wait_ge(sems[s], v)
                    ins = fn(e)
                    for s, k in incs:
                        ins.then_inc(sems[s], k)
            return run

        block.sync(runner("sync"))
        block.gpsimd(runner("gpsimd"))
        block.scalar(runner("scalar"))
        block.vector(runner("vector"))
        block.tensor(runner("tensor"))


def build_nc():
    nc = bass.Bass("TRN2", target_bir_lowering=False)

    x_q = nc.dram_tensor("x_q", [NQ, HID], F32, kind="ExternalInput").ap()
    qw_sl = nc.dram_tensor("qw_sl", [8, 128, 4096], F32, kind="ExternalInput").ap()
    kw_sl = nc.dram_tensor("kw_sl", [8, 128, 4096], F32, kind="ExternalInput").ap()
    vw_sl = nc.dram_tensor("vw_sl", [8, 128, 4096], F32, kind="ExternalInput").ap()
    ow_sl = nc.dram_tensor("ow_sl", [32, 128, 1024], F32, kind="ExternalInput").ap()
    wb_sl = nc.dram_tensor("wb_sl", [32, 128, 4096], F32, kind="ExternalInput").ap()
    wg_sl = nc.dram_tensor("wg_sl", [128, 256], F32, kind="ExternalInput").ap()
    af_sl = nc.dram_tensor("af_sl", [128, 4096], F32, kind="ExternalInput").ap()
    BfT = nc.dram_tensor("BfT", [128, HID], F32, kind="ExternalInput").ap()
    b_pm = nc.dram_tensor("b_pm", [128, 32], F32, kind="ExternalInput").ap()
    cos_il = nc.dram_tensor("cos_il", [128, S], BF16, kind="ExternalInput").ap()
    sin_il = nc.dram_tensor("sin_il", [128, S], BF16, kind="ExternalInput").ap()
    tri_in = nc.dram_tensor("tri_in", [128, 2048], BF16, kind="ExternalInput").ap()
    e16_in = nc.dram_tensor("e16_in", [8, 128], BF16, kind="ExternalInput").ap()
    ident_in = nc.dram_tensor("ident_in", [128, 128], BF16, kind="ExternalInput").ap()
    identf_in = nc.dram_tensor("identf_in", [128, 128], F32, kind="ExternalInput").ap()
    ones_in = nc.dram_tensor("ones_in", [128, 128], BF16, kind="ExternalInput").ap()
    impsc_in = nc.dram_tensor("impsc_in", [1, 1], F32, kind="ExternalInput").ap()
    out_q = nc.dram_tensor("out_q", [NQ, HID], F32, kind="ExternalOutput").ap()

    xn_ag_in = nc.dram_tensor("xn_ag_in", [32, 128, NQ], BF16)
    xn_ag_out = nc.dram_tensor("xn_ag_out", [128, 128, NQ], BF16)
    qs = nc.dram_tensor("qs", [H_PER, 128, S], BF16)
    ks = nc.dram_tensor("ks", [H_PER, 128, S], BF16)
    vs = nc.dram_tensor("vs", [H_PER, 32, 128, 128], BF16)
    at_sp = nc.dram_tensor("at_sp", [H_PER, 128, S], BF16)
    imp_in = nc.dram_tensor("imp_in", [1, 512], F32)
    imp_out = nc.dram_tensor("imp_out", [1, 512], F32)
    s_rs_in = [nc.dram_tensor(f"s_rs_in{g}", [4, 8, 128, NQ], BF16) for g in range(4)]
    s_rs_out = [nc.dram_tensor(f"s_rs_out{g}", [8, 128, NQ], BF16) for g in range(4)]
    g_sp = nc.dram_tensor("g_sp", [32, 128, NQ], BF16)
    sgT_sp = nc.dram_tensor("sgT_sp", [8, 128, HID], BF16)

    ctxs = []

    def ent(g):
        ctxs.append(g)
        return g.__enter__()

    bigb = ent(nc.sbuf_tensor("bigb", [128, 49152], BF16))
    bigf = ent(nc.sbuf_tensor("bigf", [128, 13312], F32))
    cosb = ent(nc.sbuf_tensor("cosb", [128, S], BF16))
    sinb = ent(nc.sbuf_tensor("sinb", [128, S], BF16))
    trib = ent(nc.sbuf_tensor("trib", [128, 2048], BF16))
    identb = ent(nc.sbuf_tensor("identb", [128, 128], BF16))
    identf = ent(nc.sbuf_tensor("identf", [128, 128], F32))
    onesb = ent(nc.sbuf_tensor("onesb", [128, 128], BF16))
    e16b = ent(nc.sbuf_tensor("e16b", [8, 128], BF16))
    bpmb = ent(nc.sbuf_tensor("bpmb", [128, 32], F32))
    impscb = ent(nc.sbuf_tensor("impscb", [1, 1], F32))
    smallf = ent(nc.sbuf_tensor("smallf", [128, 64], F32))
    qbarf = ent(nc.sbuf_tensor("qbarf", [128, 8], F32))
    qbarb = ent(nc.sbuf_tensor("qbarb", [128, 8], BF16))
    biaskt = ent(nc.sbuf_tensor("biaskt", [128, 8], F32))
    impktpm = ent(nc.sbuf_tensor("impktpm", [128, 8], F32))
    impbc = ent(nc.sbuf_tensor("impbc", [128, 512], F32))
    negshb = ent(nc.sbuf_tensor("negshb", [128, 1], F32))
    imps = ent(nc.sbuf_tensor("imps", [1, 512], F32))

    psA = ent(nc.psum_tensor("psA", [128, 1024], F32))
    psB = ent(nc.psum_tensor("psB", [128, 1024], F32))
    psD = ent(nc.psum_tensor("psD", [128, 512], F32))
    psO = ent(nc.psum_tensor("psO", [128, 512], F32))
    psT = ent(nc.psum_tensor("psT", [128, 2048], BF16))
    psTf = psT[:, :].bitcast(F32)
    SEQ8 = [0, 4, 1, 5, 2, 6, 3, 7]
    PTC = [(s % 4) * 128 + (s // 4) * 1024 for s in range(8)]
    PS6 = [psA[:, 0:512], psA[:, 512:1024], psB[:, 0:512], psB[:, 512:1024],
           psD[:, :], psO[:, :]]
    PS6N = ["psA", "psA", "psB", "psB", "psDn", "psOu"]

    sems = {}
    for s in ["ld", "ld0", "ld1", "st", "st0", "st1", "wl0", "wl1",
              "cc", "pe", "act", "dve", "go"]:
        sems[s] = ent(nc.semaphore(s))

    p = Prog(nc, sems)

    for dst, src in [(cosb[:, :], cos_il), (sinb[:, :], sin_il), (trib[:, :], tri_in),
                     (identb[:, :], ident_in), (identf[:, :], identf_in),
                     (onesb[:, :], ones_in), (e16b[:, :], e16_in),
                     (bpmb[:, :], b_pm), (impscb[:, :], impsc_in)]:
        p.dma("sync", dst, src, writes=["consts"])

    x_t = [bigf[:, 0:4096], bigf[:, 4096:8192]]
    sqs = bigf[:, 8192:12288]
    stage = bigb[:, 0:32768]
    xn_tm = [bigb[:, 32768:36864], bigb[:, 36864:40960]]
    s1 = smallf[:, 0:1]; s2 = smallf[:, 1:2]; mmean = smallf[:, 2:3]
    mvar = smallf[:, 3:4]; rstd = smallf[:, 4:5]; nbias = smallf[:, 5:6]
    mm2 = smallf[:, 6:7]; tmpw = smallf[:, 8:9]

    for tt in range(8):
        bx = f"x{tt % 2}"
        p.dma("sync", x_t[tt % 2], x_q[tt * 128:(tt + 1) * 128, :], writes=[bx],
              sem=f"ld{tt % 2}")
        p.op("scalar", lambda e, b=x_t[tt % 2]: e.activation(sqs, b, AF.Square, accum_out=s2),
             incs=[("act", 1)], reads=[bx], writes=["s2", "sqs"])
        p.op("vector", lambda e, b=x_t[tt % 2]: e.reduce_sum(s1, b, AX.X),
             incs=[("dve", 1)], reads=[bx], writes=["s1"])
        p.op("vector", lambda e: e.tensor_scalar(mmean, s1, 1.0 / HID, None, ALU.mult),
             incs=[("dve", 1)], reads=["s1"], writes=["m"])
        p.op("vector", lambda e: e.tensor_tensor(mm2, mmean, mmean, ALU.mult),
             incs=[("dve", 1)], reads=["m"], writes=["mm2"])
        p.op("vector", lambda e: e.tensor_scalar(mvar, s2, 1.0 / HID, None, ALU.mult),
             incs=[("dve", 1)], reads=["s2"], writes=["var"])
        p.op("vector", lambda e: e.tensor_sub(mvar, mvar, mm2),
             incs=[("dve", 1)], reads=["mm2", "var"], writes=["var"])
        p.op("vector", lambda e: e.tensor_scalar(mvar, mvar, 1e-5, None, ALU.add),
             incs=[("dve", 1)], reads=["var"], writes=["var"])
        p.op("scalar", lambda e: e.activation(rstd, mvar, AF.Ln),
             incs=[("act", 1)], reads=["var"], writes=["rstd"])
        p.op("scalar", lambda e: e.activation(rstd, rstd, AF.Exp, scale=-0.5),
             incs=[("act", 1)], reads=["rstd"], writes=["rstd"])
        p.op("vector", lambda e: e.scalar_tensor_tensor(nbias, mmean, -1.0, rstd,
                                                        ALU.mult, ALU.mult),
             incs=[("dve", 1)], reads=["m", "rstd"], writes=["nb"])
        bxn = f"xnt{tt % 2}"
        p.op("vector", lambda e, b=x_t[tt % 2], o=xn_tm[tt % 2]: e.tensor_scalar(
            o, b, rstd, nbias, ALU.mult, ALU.add),
             incs=[("dve", 1)], reads=[bx, "rstd", "nb"], writes=[bxn])
        for ci in range(32):
            s8 = SEQ8[ci % 8]
            slot = f"psT{s8}"
            p.op("tensor", lambda e, ci=ci, o=xn_tm[tt % 2], c0=PTC[s8]: e.transpose(
                psT[:, c0:c0 + 128], o[:, ci * 128:(ci + 1) * 128], identb[:, :]),
                 incs=[("pe", 1)], reads=[bxn, "consts"], writes=[slot])
            p.op("vector", lambda e, ci=ci, tt=tt, c0=PTC[s8]: e.tensor_copy(
                stage[:, ci * 1024 + tt * 128: ci * 1024 + (tt + 1) * 128],
                psT[:, c0:c0 + 128]),
                 incs=[("dve", 1)], reads=[slot],
                 writes=[f"stg_{ci}_{tt}", f"xn_{ci}"])
    for ci in range(32):
        p.dma("scalar", xn_ag_in.ap()[ci, :, :], stage[:, ci * 1024:(ci + 1) * 1024],
              reads=[f"stg_{ci}_{tt}" for tt in range(8)], writes=[f"agin_{ci}"], sem="st")
    p.op("gpsimd", lambda e: e.collective_compute(
        "AllGather", ALU.bypass, replica_groups=G4,
        ins=[xn_ag_in.ap().opt()], outs=[xn_ag_out.ap().opt()]),
         incs=[("cc", 1)], reads=[f"agin_{ci}" for ci in range(32)], writes=["xnag"])

    PSP = [psA[:, 0:512], psA[:, 512:1024]]
    wgb = bigb[:, 45056:45312]
    wgstg = bigf[:, 10240:10496]
    p.dma("sync", wgstg, wg_sl[:, :], writes=["wgs", "sqs"], sem="wl0")
    p.op("vector", lambda e: e.tensor_copy(wgb, wgstg),
         incs=[("dve", 1)], reads=["wgs"], writes=["wgb"])
    logitf = bigf[:, 0:1024]
    for rng in range(2):
        for ic in range(32):
            p.op("tensor", lambda e, ic=ic, rng=rng: e.matmul(
                psD[0:8, 0:512], wgb[:, ic * 8:(ic + 1) * 8],
                xn_sb[ic][:, rng * 512:(rng + 1) * 512],
                start=(ic == 0), stop=(ic == 31)),
                 incs=[("pe", 1)],
                 reads=["wgb", f"xn_{ic}"],
                 writes=["psDn"] if ic in (0, 31) else [])
        p.op("vector", lambda e, rng=rng: e.tensor_copy(
            logitf[0:8, rng * 512:(rng + 1) * 512], psD[0:8, 0:512]),
             incs=[("dve", 1)], reads=["psDn"], writes=[f"lg{rng}", "x0"])
    routef = bigb[:, 45312:46336]
    ltt = bigf[:, 2048:2056]
    m1 = smallf[:, 40:41]; m2 = smallf[:, 41:42]; nm1 = smallf[:, 42:43]
    eqm = smallf[:, 43:51]; l2f = smallf[:, 51:59]
    for t2 in range(8):
        p.op("tensor", lambda e, t2=t2: e.transpose(
            psD[:, 0:8], logitf[0:8, t2 * 128:(t2 + 1) * 128], identf[0:8, 0:8]),
             incs=[("pe", 1)], reads=["lg0", "lg1"], writes=["psDn"])
        p.op("vector", lambda e, t2=t2: e.tensor_copy(ltt, psD[:, 0:8]),
             incs=[("dve", 1)], reads=["psDn"],
             writes=["ltt"] + (["x0"] if t2 == 0 else []))
        p.op("vector", lambda e: e.reduce_max(m1, ltt, AX.X),
             incs=[("dve", 1)], reads=["ltt"], writes=["gm1"])
        p.op("vector", lambda e: e.tensor_scalar(eqm, ltt, m1, None, ALU.is_equal),
             incs=[("dve", 1)], reads=["ltt", "gm1"], writes=["geq"])
        p.op("vector", lambda e: e.scalar_tensor_tensor(l2f, eqm, -30000.0, ltt,
                                                        ALU.mult, ALU.add),
             incs=[("dve", 1)], reads=["geq", "ltt"], writes=["gl2"])
        p.op("vector", lambda e: e.reduce_max(m2, l2f, AX.X),
             incs=[("dve", 1)], reads=["gl2"], writes=["gm2"])
        p.op("vector", lambda e: e.tensor_scalar(eqm, ltt, m2, None, ALU.is_ge),
             incs=[("dve", 1)], reads=["ltt", "gm2"], writes=["geq"])
        p.op("vector", lambda e: e.tensor_scalar(nm1, m1, -1.0, None, ALU.mult),
             incs=[("dve", 1)], reads=["gm1"], writes=["gnm1"])
        p.op("scalar", lambda e: e.activation(l2f, ltt, AF.Exp, bias=nm1),
             incs=[("act", 1)], reads=["ltt", "gnm1", "gl2"], writes=["gl2"])
        p.op("vector", lambda e: e.tensor_mul(l2f, l2f, eqm),
             incs=[("dve", 1)], reads=["gl2", "geq"], writes=["gl2"])
        p.op("vector", lambda e: e.reduce_sum(m2, l2f, AX.X),
             incs=[("dve", 1)], reads=["gl2"], writes=["gm2"])
        p.op("vector", lambda e: e.reciprocal(m1, m2),
             incs=[("dve", 1)], reads=["gm2"], writes=["gm1"])
        p.op("vector", lambda e: e.tensor_scalar(l2f, l2f, m1, None, ALU.mult),
             incs=[("dve", 1)], reads=["gl2", "gm1"], writes=["gl2"])
        p.op("tensor", lambda e: e.transpose(psD[0:8, 128:256], l2f, identf[:, :]),
             incs=[("pe", 1)], reads=["gl2"], writes=["psDn"])
        p.op("vector", lambda e, t2=t2: e.tensor_copy(
            routef[0:8, t2 * 128:(t2 + 1) * 128], psD[0:8, 128:256]),
             incs=[("dve", 1)], reads=["psDn"], writes=[f"rt{t2}"])
    afb = bigb[:, 40960:45056]
    hsb = bigb[:, 46336:47360]
    hwb = bigb[:, 47360:48384]
    afstg = bigf[:, 0:4096]
    p.dma("sync", afstg, af_sl[:, :],
          writes=["afs", "lg0", "lg1", "ltt"], sem="wl1")
    p.op("vector", lambda e: e.tensor_copy(afb, afstg),
         incs=[("dve", 1)], reads=["afs"], writes=["afb"])
    for rng in range(2):
        for ic in range(32):
            p.op("tensor", lambda e, ic=ic, rng=rng: e.matmul(
                psO[:, :], afb[:, ic * 128:(ic + 1) * 128],
                xn_sb[ic][:, rng * 512:(rng + 1) * 512],
                start=(ic == 0), stop=(ic == 31)),
                 incs=[("pe", 1)],
                 reads=["afb", f"xn_{ic}"],
                 writes=["psOu"] if ic in (0, 31) else [])
        p.op("vector", lambda e, rng=rng: e.tensor_copy(
            hsb[:, rng * 512:(rng + 1) * 512], psO[:, :]),
             incs=[("dve", 1)], reads=["psOu"], writes=[f"h{rng}"])
        p.op("tensor", lambda e, rng=rng: e.matmul(
            psD[:, :], e16b[:, :], routef[0:8, rng * 512:(rng + 1) * 512],
            start=True, stop=True),
             incs=[("pe", 1)], reads=[f"rt{t2}" for t2 in range(8)] + ["consts"],
             writes=["psDn"])
        p.op("vector", lambda e, rng=rng: e.scalar_tensor_tensor(
            hwb[:, rng * 512:(rng + 1) * 512], psD[:, :], 1.0,
            hsb[:, rng * 512:(rng + 1) * 512], ALU.mult, ALU.mult),
             incs=[("dve", 1)], reads=["psDn", f"h{rng}"], writes=[f"hw{rng}"])
    wbb = [bigb[:, 32768:36864], bigb[:, 36864:40960]]
    bfb = bigb[:, 40960:45056]
    bfstg = bigf[:, 4096:8192]
    p.dma("sync", bfstg, BfT[:, :], writes=["bfs", "x1"], sem="wl0")
    p.op("vector", lambda e: e.tensor_copy(bfb[:, 0:4096], bfstg),
         incs=[("dve", 1)], reads=["bfs"], writes=["afb", "bfb"])
    gstg = [bigb[:, 46336:46848], bigb[:, 46848:47360]]
    wbstg = [bigf[:, 0:4096], bigf[:, 4096:8192]]
    p.dma("sync", wbstg[0], wb_sl[0, :, :], writes=["wbs0", "afs", "bfs"], sem="wl0")
    p.dma("sync", wbstg[1], wb_sl[1, :, :], writes=["wbs1", "bfs"], sem="wl1")
    p.op("vector", lambda e: e.tensor_copy(wbb[0], wbstg[0]),
         incs=[("dve", 1)], reads=["wbs0"], writes=["wb0", "xnt0"])
    for ob in range(32):
        wres = f"wb{ob % 2}"
        if ob + 2 < 32:
            p.dma("sync", wbstg[ob % 2], wb_sl[ob + 2, :, :], writes=[f"wbs{ob % 2}"],
                  sem=f"wl{ob % 2}")
        if ob + 1 < 32:
            p.op("vector", lambda e, b=(ob + 1) % 2: e.tensor_copy(wbb[b], wbstg[b]),
                 incs=[("dve", 1)], reads=[f"wbs{(ob + 1) % 2}"],
                 writes=[f"wb{(ob + 1) % 2}"] + (["xnt1"] if ob == 0 else []))
        for ic in range(32):
            for rng in range(2):
                p.op("tensor", lambda e, ic=ic, ob=ob, rng=rng, ps=PSP[rng % 2]: e.matmul(
                    ps, wbb[ob % 2][:, ic * 128:(ic + 1) * 128],
                    xn_sb[ic][:, rng * 512:(rng + 1) * 512],
                    start=(ic == 0), stop=False),
                     incs=[("pe", 1)],
                     reads=[wres, f"xn_{ic}"],
                     writes=(["psP%d" % (rng % 2)] if ic == 0 else []))
        for rng in range(2):
            pres = "psP%d" % (rng % 2)
            p.op("tensor", lambda e, ob=ob, rng=rng, ps=PSP[rng % 2]: e.matmul(
                ps, bfb[:, ob * 128:(ob + 1) * 128],
                hwb[:, rng * 512:(rng + 1) * 512],
                start=False, stop=True),
                 incs=[("pe", 1)], reads=["bfb", f"hw{rng}"], writes=[pres])
            gres = f"gs{rng}"
            p.op("scalar", lambda e, ob=ob, ps=PSP[rng % 2], sb2=gstg[rng]: e.activation(
                sb2, ps, AF.Silu, bias=bpmb[:, ob:ob + 1]),
                 incs=[("act", 1)], reads=[pres, "consts"],
                 writes=[gres] + ([f"h{rng}"] if ob == 0 else []))
            p.dma("scalar", g_sp.ap()[ob, :, rng * 512:(rng + 1) * 512], gstg[rng],
                  reads=[gres], writes=[f"g_{ob}_{rng}"], sem=f"st{rng}")


    p.barrier()

    xn_sb = [bigb[:, ci * 1024:(ci + 1) * 1024] for ci in range(32)]
    w_sb = [bigb[:, 32768:36864], bigb[:, 36864:40960]]
    qraw = bigb[:, 40960:41472]
    rotb = bigb[:, 41472:41984]
    tcos = bigb[:, 41984:42496]
    qst = [bigb[:, 42496:43008], bigb[:, 43008:43520]]
    vstage = bigb[:, 43520:44032]

    p.op("vector", lambda e: e.memset(qbarf[:, :], 0.0), incs=[("dve", 1)],
         writes=["qbar"])
    negsh = negshb[:, 0:1]
    p.op("vector", lambda e: e.memset(negsh, -SHIFT), incs=[("dve", 1)],
         writes=["nsh"])
    wsrc = {"q": qw_sl, "k": kw_sl, "v": vw_sl}
    wstg = [bigf[:, 0:4096], bigf[:, 4096:8192]]

    def w_src(s):
        obi = s % 24
        kind, h = ("q", obi) if obi < 8 else (("k", obi - 8) if obi < 16 else ("v", obi - 16))
        return wsrc[kind][h, :, :]

    def kind_h(obi):
        return ("q", obi) if obi < 8 else (("k", obi - 8) if obi < 16 else ("v", obi - 16))

    kvis = bigb[:, 44032:44544]
    p.dma("sync", wstg[0], w_src(0), writes=["ws0"], sem="wl0")
    p.dma("sync", wstg[1], w_src(1), writes=["ws1"], sem="wl1")
    p.op("vector", lambda e: e.tensor_copy(w_sb[0], wstg[0]),
         incs=[("dve", 1)], reads=["ws0"], writes=["w0"])

    for s in range(96):
        qi, obi = s // 24, s % 24
        kind, h = kind_h(obi)
        if obi == 0:
            for ci in range(32):
                p.dma("sync", xn_sb[ci], xn_ag_out.ap()[qi * 32 + ci, :, :],
                      reads=["xnag"],
                      writes=[f"xn_{ci}"] + (["xnq"] if ci == 31 else []))
        if s + 2 < 96:
            p.dma("sync", wstg[s % 2], w_src(s + 2), writes=[f"ws{s % 2}"],
                  sem=f"wl{s % 2}")
        if s + 1 < 96:
            p.op("vector", lambda e, b=(s + 1) % 2: e.tensor_copy(w_sb[b], wstg[b]),
                 incs=[("dve", 1)], reads=[f"ws{(s + 1) % 2}"], writes=[f"w{(s + 1) % 2}"])
        wres = f"w{s % 2}"
        slot0, slot1 = (s * 2) % 6, (s * 2 + 1) % 6
        for ic in range(32):
            p.op("tensor", lambda e, ic=ic, w=w_sb[s % 2], ps=PS6[slot0]:
                 e.matmul(ps, w[:, ic * 128:(ic + 1) * 128],
                          xn_sb[ic][:, 0:512],
                          start=(ic == 0), stop=(ic == 31)),
                 incs=[("pe", 1)],
                 reads=[wres, f"xn_{ic}"] + (["xnq"] if ic == 0 else []),
                 writes=[PS6N[slot0]] if ic in (0, 31) else [])
            p.op("tensor", lambda e, ic=ic, w=w_sb[s % 2], ps=PS6[slot1]:
                 e.matmul(ps, w[:, ic * 128:(ic + 1) * 128],
                          xn_sb[ic][:, 512:1024],
                          start=(ic == 0), stop=(ic == 31)),
                 incs=[("pe", 1)],
                 reads=[wres, f"xn_{ic}"],
                 writes=[PS6N[slot1]] if ic in (0, 31) else [])
        for rng in range(2):
            slot = (s * 2 + rng) % 6
            pres = PS6N[slot]
            tok0 = qi * 1024 + rng * 512
            if kind in ("q", "k"):
                p.op("scalar", lambda e, ps=PS6[slot]: e.activation(qraw, ps, AF.Copy),
                     incs=[("act", 1)], reads=[pres], writes=["qraw"])
                stres = f"qst{s % 2}"
                p.op("vector", lambda e, t0=tok0: e.tensor_mul(
                    tcos, qraw, cosb[:, t0:t0 + 512]),
                     incs=[("dve", 1)], reads=["qraw"], writes=["tcos"])
                p.op("vector", lambda e: e.stream_shuffle(rotb, qraw, SWAP_MASK),
                     incs=[("dve", 1)], reads=["qraw"], writes=["rot"])
                p.op("vector", lambda e, t0=tok0: e.tensor_mul(
                    rotb, rotb, sinb[:, t0:t0 + 512]),
                     incs=[("dve", 1)], reads=["rot"], writes=["rot"])
                p.op("vector", lambda e, sb2=qst[s % 2]: e.tensor_add(
                    sb2, tcos, rotb),
                     incs=[("dve", 1)], reads=["tcos", "rot"], writes=[stres])
                dst = (qs if kind == "q" else ks).ap()[h, :, tok0:tok0 + 512]
                p.dma("scalar", dst, qst[s % 2],
                      reads=[stres], writes=[f"sp_{kind}{h}_{qi}{rng}"],
                      sem=f"st{s % 2}")
                if kind == "q":
                    lo = max(INSTR_START, tok0); hi = min(INSTR_END, tok0 + 512)
                    if lo < hi:
                        p.op("vector", lambda e, sb2=qst[s % 2],
                             lo=lo, hi=hi, t0=tok0: e.reduce_sum(
                                 tmpw, sb2[:, lo - t0:hi - t0], AX.X),
                             incs=[("dve", 1)], reads=[stres], writes=["tmpw"])
                        p.op("vector", lambda e, h=h: e.tensor_add(
                            qbarf[:, h:h + 1], qbarf[:, h:h + 1], tmpw),
                             incs=[("dve", 1)], reads=["tmpw", "qbar"], writes=["qbar"])
            else:
                p.op("scalar", lambda e, ps=PS6[slot]: e.activation(vstage, ps, AF.Copy),
                     incs=[("act", 1)], reads=[pres], writes=["vst"])
                for i4 in range(4):
                    p.op("tensor", lambda e, i4=i4: e.transpose(
                        psT[:, i4 * 128:(i4 + 1) * 128],
                        vstage[:, i4 * 128:(i4 + 1) * 128], identb[:, :]),
                         incs=[("pe", 1)], reads=["vst"], writes=[f"psT{i4}"])
                stres = f"qst{s % 2}"
                p.op("vector", lambda e, v2=qst[s % 2]: e.tensor_copy(
                    v2, psT[:, 0:512]),
                     incs=[("dve", 1)], reads=[f"psT{i4}" for i4 in range(4)],
                     writes=[stres])
                kt0 = tok0 // 128
                p.dma("scalar", vs.ap()[h, kt0:kt0 + 4, :, :].rearrange(
                    "t p c -> p t c"),
                      qst[s % 2].rearrange("p (t c) -> p t c", c=128),
                      reads=[stres], writes=[f"sp_v{h}_{qi}{rng}"],
                      sem=f"st{s % 2}")
        if s == 72 + 7:
            p.op("vector", lambda e: e.tensor_copy(qbarb[:, :], qbarf[:, :]),
                 incs=[("dve", 1)], reads=["qbar"], writes=["qbarb"])
            for h2 in range(H_PER):
                p.dma("sync", kvis[:, :], ks.ap()[h2, :, 1:513],
                      reads=[f"sp_k{h2}_00"], writes=["kvis"])
                p.op("tensor", lambda e, h2=h2: e.matmul(
                    psD[0:1, :], qbarb[:, h2:h2 + 1], kvis[:, :],
                    start=(h2 == 0), stop=(h2 == 7)),
                     incs=[("pe", 1)], reads=["kvis", "qbarb"],
                     writes=(["psDn"] if h2 in (0, 7) else []))
            p.op("vector", lambda e: e.tensor_scalar(imps[:, :], psD[0:1, :],
                                                     impscb[0:1, 0:1], None, ALU.mult),
                 incs=[("dve", 1)], reads=["psDn", "consts"], writes=["imps"])
            p.dma("scalar", imp_in.ap()[:, :], imps[:, :], reads=["imps"],
                  writes=["impin"], sem="st")
            p.op("gpsimd", lambda e: e.collective_compute(
                "AllReduce", ALU.add, replica_groups=G8,
                ins=[imp_in.ap().opt()], outs=[imp_out.ap().opt()]),
                 incs=[("cc", 1)], reads=["impin"], writes=["impout"])
    p.op("vector", lambda e: e.memset(impktpm[:, :], 3.0e38),
         incs=[("dve", 1)], writes=["iktpm"])
    p.dma("sync", impbc[:, :], imp_out.ap()[0:1, :].partition_broadcast(128),
          reads=["impout"], writes=["impbc"])
    p.dma("sync", impktpm[1:128, 0:1], imp_out.ap()[0:1, 0:127].rearrange("a c -> c a"),
          reads=["impout"], writes=["iktpm"])
    for j in range(1, 4):
        p.dma("sync", impktpm[:, j:j + 1],
              imp_out.ap()[0:1, j * 128 - 1:(j + 1) * 128 - 1].rearrange("a c -> c a"),
              reads=["impout"], writes=["iktpm"])
    p.dma("sync", impktpm[0:1, 4:5], imp_out.ap()[0:1, 511:512],
          reads=["impout"], writes=["iktpm"])
    gtbuf = bigf[:, 12288:12800]
    cntt = smallf[:, 16:24]
    for j in range(5):
        p.op("vector", lambda e, j=j: e.tensor_scalar(
            gtbuf, impbc[:, :], impktpm[:, j:j + 1], None, ALU.is_gt),
             incs=[("dve", 1)], reads=["impbc", "iktpm"], writes=["gt"])
        p.op("vector", lambda e, j=j: e.reduce_sum(cntt[:, j:j + 1], gtbuf, AX.X),
             incs=[("dve", 1)], reads=["gt"], writes=[f"cnt{j}"])
    p.op("vector", lambda e: e.tensor_scalar(biaskt[:, 0:5], cntt[:, 0:5], 255.5,
                                             None, ALU.is_lt),
         incs=[("dve", 1)], reads=[f"cnt{j}" for j in range(5)], writes=["bk"])
    p.op("vector", lambda e: e.tensor_scalar(biaskt[:, 0:5], biaskt[:, 0:5],
                                             1.0e6 - SHIFT, -1.0e6, ALU.mult, ALU.add),
         incs=[("dve", 1)], reads=["bk"], writes=["bk"])

    p.barrier()

    qh = [bigb[:, 0:4096], bigb[:, 4096:8192]]
    kh = [bigb[:, 8192:12288], bigb[:, 12288:16384]]
    vh = [bigb[:, 16384:20480], bigb[:, 20480:24576]]
    p_sb = bigb[:, 24576:40960]
    atst = [bigb[:, 40960:41472], bigb[:, 41472:41984]]
    denf = bigf[:, 0:512]
    recipf = bigf[:, 512:1024]

    for h in range(H_PER):
        hb = h % 2
        p.dma("sync", qh[hb], qs.ap()[h, :, :],
              reads=[f"sp_q{h}_{qi}{rng}" for qi in range(4) for rng in range(2)],
              writes=[f"qh{hb}"], sem=f"ld{hb}")
        p.dma("sync", kh[hb], ks.ap()[h, :, :],
              reads=[f"sp_k{h}_{qi}{rng}" for qi in range(4) for rng in range(2)],
              writes=[f"kh{hb}"], sem=f"ld{hb}")
        p.dma("sync", vh[hb].rearrange("p (t c) -> p t c", c=128),
              vs.ap()[h, :, :, :].rearrange("t p c -> p t c"),
              reads=[f"sp_v{h}_{qi}{rng}" for qi in range(4) for rng in range(2)],
              writes=[f"vh{hb}"], sem=f"ld{hb}")
        SC3 = [psA, psB, psTf]
        SC3N = ["sc0", "sc1", "sc2"]
        DEPTH = 2
        for r in range(8):
            nkt = 4 * (r + 1)
            npair = nkt // 2

            def emit_scores(pp, r=r, hb=hb, nkt=nkt):
                jp = 2 * pp
                pb = SC3[pp % 3]
                pres = SC3N[pp % 3]
                for o2 in range(2):
                    j = jp + o2
                    p.op("tensor", lambda e, j=j, r=r, pb=pb, o2=o2, hb=hb: e.matmul(
                        pb[:, o2 * 512:(o2 + 1) * 512],
                        kh[hb][:, j * 128:(j + 1) * 128],
                        qh[hb][:, r * 512:(r + 1) * 512],
                        start=True, stop=True),
                         incs=[("pe", 1)], reads=[f"qh{hb}", f"kh{hb}"],
                         writes=[pres])
                if jp < 6:
                    for j in (jp, jp + 1):
                        o2 = j - jp
                        if j < 5:
                            p.op("scalar", lambda e, j=j, pb=pb, o2=o2: e.activation(
                                p_sb[:, j * 512:(j + 1) * 512],
                                pb[:, o2 * 512:(o2 + 1) * 512], AF.Exp,
                                bias=biaskt[:, j:j + 1], scale=SCALE),
                                 incs=[("act", 1)], reads=[pres, "bk"],
                                 writes=[f"p{j}"])
                        else:
                            p.op("scalar", lambda e, j=j, pb=pb, o2=o2: e.activation(
                                p_sb[:, j * 512:(j + 1) * 512],
                                pb[:, o2 * 512:(o2 + 1) * 512], AF.Exp,
                                bias=negsh, scale=SCALE),
                                 incs=[("act", 1)], reads=[pres, "nsh"],
                                 writes=[f"p{j}"])
                else:
                    p.op("scalar", lambda e, j0=jp, pb=pb: e.activation(
                        p_sb[:, j0 * 512:(j0 + 2) * 512], pb[:, 0:1024], AF.Exp,
                        bias=negsh, scale=SCALE),
                         incs=[("act", 1)], reads=[pres, "nsh"],
                         writes=[f"p{jp}", f"p{jp + 1}"])
                for o2 in range(2):
                    j = jp + o2
                    dj = j - 4 * r
                    if 0 <= dj < 4:
                        p.op("vector", lambda e, j=j, dj=dj: e.tensor_mul(
                            p_sb[:, j * 512:(j + 1) * 512],
                            p_sb[:, j * 512:(j + 1) * 512],
                            trib[:, dj * 512:(dj + 1) * 512]),
                             incs=[("dve", 1)], reads=[f"p{j}"], writes=[f"p{j}"])

            def emit_consume(qp, hb=hb, nkt=nkt):
                jq = 2 * qp
                for j in (jq, jq + 1):
                    p.op("tensor", lambda e, j=j, nkt=nkt: e.matmul(
                        psD[:, :], onesb[:, :], p_sb[:, j * 512:(j + 1) * 512],
                        start=(j == 0), stop=(j == nkt - 1)),
                         incs=[("pe", 1)], reads=[f"p{j}"],
                         writes=["psDn"] if j in (0, nkt - 1) else [])
                for j in (jq, jq + 1):
                    p.op("tensor", lambda e, j=j, hb=hb, nkt=nkt: e.matmul(
                        psO[:, :], vh[hb][:, j * 128:(j + 1) * 128],
                        p_sb[:, j * 512:(j + 1) * 512],
                        start=(j == 0), stop=(j == nkt - 1)),
                         incs=[("pe", 1)], reads=[f"p{j}", f"vh{hb}"],
                         writes=["psOu"] if j in (0, nkt - 1) else [])

            for pp in range(npair + DEPTH):
                if pp < npair:
                    emit_scores(pp)
                if pp >= DEPTH and pp - DEPTH < npair:
                    emit_consume(pp - DEPTH)
            p.op("scalar", lambda e: e.activation(denf, psD[:, :], AF.Ln),
                 incs=[("act", 1)], reads=["psDn"], writes=["denf"])
            p.op("scalar", lambda e: e.activation(recipf, denf, AF.Exp, scale=-1.0),
                 incs=[("act", 1)], reads=["denf"], writes=["recip"])
            ares = f"at{r % 2}"
            p.op("vector", lambda e, ab=atst[r % 2]: e.tensor_mul(ab, psO[:, :], recipf),
                 incs=[("dve", 1)], reads=["psOu", "recip"], writes=[ares])
            p.dma("scalar", at_sp.ap()[h, :, r * 512:(r + 1) * 512], atst[r % 2],
                  reads=[ares], writes=[f"at_{h}_{r}"], sem=f"st{r % 2}")

    p.barrier()

    owb = [bigb[:, 41984:43008], bigb[:, 43008:44032]]
    atc = [bigb[:, hc * 4096:(hc + 1) * 4096] for hc in range(8)]
    sstg = [bigb[:, 44032:44544], bigb[:, 44544:45056]]

    for hc in range(8):
        p.dma("sync", atc[hc], at_sp.ap()[hc, :, :],
              reads=[f"at_{hc}_{r}" for r in range(8)],
              writes=[f"atc{hc}"], sem=f"ld{hc % 2}")
    owstg = [bigf[:, 8192:9216], bigf[:, 9216:10240]]
    p.dma("sync", owstg[0], ow_sl[0, :, :], writes=["ows0"], sem="wl0")
    p.dma("sync", owstg[1], ow_sl[1, :, :], writes=["ows1"], sem="wl1")
    p.op("vector", lambda e: e.tensor_copy(owb[0], owstg[0]),
         incs=[("dve", 1)], reads=["ows0"], writes=["ow0"])
    for ob in range(32):
        wres = f"ow{ob % 2}"
        if ob + 2 < 32:
            p.dma("sync", owstg[ob % 2], ow_sl[ob + 2, :, :], writes=[f"ows{ob % 2}"],
                  sem=f"wl{ob % 2}")
        if ob + 1 < 32:
            p.op("vector", lambda e, b=(ob + 1) % 2: e.tensor_copy(owb[b], owstg[b]),
                 incs=[("dve", 1)], reads=[f"ows{(ob + 1) % 2}"], writes=[f"ow{(ob + 1) % 2}"])
        PSP8 = [psA[:, 0:512], psA[:, 512:1024], psB[:, 0:512], psB[:, 512:1024],
                psD[:, :], psO[:, :], psTf[:, 0:512], psTf[:, 512:1024]]
        for hc in range(8):
            for rr in range(8):
                p.op("tensor", lambda e, hc=hc, ob=ob, rr=rr, ps=PSP8[rr]: e.matmul(
                    ps, owb[ob % 2][:, hc * 128:(hc + 1) * 128],
                    atc[hc][:, rr * 512:(rr + 1) * 512],
                    start=(hc == 0), stop=(hc == 7)),
                     incs=[("pe", 1)],
                     reads=[wres, f"atc{hc}"],
                     writes=([f"psQ{rr}"] if hc in (0, 7) else []))
        for rr in range(8):
            sres = f"ss{rr % 2}"
            p.op("vector", lambda e, ps=PSP8[rr], sb2=sstg[rr % 2]: e.tensor_copy(sb2, ps),
                 incs=[("dve", 1)], reads=[f"psQ{rr}"], writes=[sres])
            qq, hh = rr // 2, rr % 2
            p.dma("scalar", s_rs_in[ob // 8].ap()[qq, ob % 8, :, hh * 512:(hh + 1) * 512],
                  sstg[rr % 2], reads=[sres], writes=[f"srs_{ob}_{rr}"],
                  sem=f"st{rr % 2}")
        if ob % 8 == 7:
            g = ob // 8
            p.op("gpsimd", lambda e, g=g: e.collective_compute(
                "ReduceScatter", ALU.add, replica_groups=G4,
                ins=[s_rs_in[g].ap().opt()], outs=[s_rs_out[g].ap().opt()]),
                 incs=[("cc", 1)],
                 reads=[f"srs_{ob2}_{rr}" for ob2 in range(g * 8, g * 8 + 8)
                        for rr in range(8)],
                 writes=[f"srs{g}"])

    p.barrier()


    ssb = [bigb[:, 0:1024], bigb[:, 1024:2048]]
    gsb = [bigb[:, 2048:3072], bigb[:, 3072:4096]]
    sgb = bigb[:, 4096:5120]
    sgT = bigb[:, 5120:6144]
    for ob in range(32):
        p.dma("sync", ssb[ob % 2], s_rs_out[ob // 8].ap()[ob % 8, :, :],
              reads=[f"srs{ob // 8}"], writes=[f"sq{ob % 2}"], sem=f"ld{ob % 2}")
        p.dma("sync", gsb[ob % 2], g_sp.ap()[ob, :, :],
              reads=[f"g_{ob}_0", f"g_{ob}_1"], writes=[f"gq{ob % 2}"],
              sem=f"ld{ob % 2}")
        p.op("vector", lambda e, ob=ob: e.tensor_mul(sgb, ssb[ob % 2], gsb[ob % 2]),
             incs=[("dve", 1)], reads=[f"sq{ob % 2}", f"gq{ob % 2}"],
             writes=["sgb"])
        for t2 in range(8):
            s8 = SEQ8[t2]
            p.op("tensor", lambda e, t2=t2, c0=PTC[s8]: e.transpose(
                psT[:, c0:c0 + 128],
                sgb[:, t2 * 128:(t2 + 1) * 128], identb[:, :]),
                 incs=[("pe", 1)], reads=["sgb"], writes=[f"psT{s8}"])
            p.op("vector", lambda e, t2=t2, c0=PTC[s8]: e.tensor_copy(
                sgT[:, t2 * 128:(t2 + 1) * 128], psT[:, c0:c0 + 128]),
                 incs=[("dve", 1)], reads=[f"psT{s8}"], writes=[f"sgT_{t2}"])
        p.dma("scalar", sgT_sp.ap()[:, :, ob * 128:(ob + 1) * 128].rearrange(
            "t p c -> p t c"), sgT.rearrange("p (t c) -> p t c", c=128),
              reads=[f"sgT_{t2}" for t2 in range(8)], writes=[f"sgsp{ob}"], sem="st")
    xf = bigf[:, 0:4096]
    of = [bigf[:, 4096:8192], bigf[:, 8192:12288]]
    sgTl = [bigb[:, 6144:10240], bigb[:, 10240:14336]]
    for t2 in range(8):
        p.dma("sync", sgTl[t2 % 2], sgT_sp.ap()[t2, :, :],
              reads=[f"sgsp{ob}" for ob in range(32)] if t2 < 2 else [],
              writes=[f"sgl{t2 % 2}"], sem=f"ld{t2 % 2}")
        p.dma("sync", xf, x_q[t2 * 128:(t2 + 1) * 128, :], writes=["xf"])
        ores = f"of{t2 % 2}"
        p.op("vector", lambda e, t2=t2: e.scalar_tensor_tensor(
            of[t2 % 2], sgTl[t2 % 2], 1.0, xf, ALU.mult, ALU.add),
             incs=[("dve", 1)], reads=[f"sgl{t2 % 2}", "xf"], writes=[ores])
        p.dma("sync", out_q[t2 * 128:(t2 + 1) * 128, :], of[t2 % 2],
              reads=[ores], writes=[f"outd{t2}"], sem="st")
    p.op("vector", lambda e: e.engine_nop(),
         waits=[("st", p.n("st"))], incs=[("go", 1)])

    with nc.Block() as block:
        p.emit(block)
    for g in reversed(ctxs):
        g.__exit__(None, None, None)
    return nc


_NC_CACHE = None


def _get_nc():
    global _NC_CACHE
    if _NC_CACHE is None:
        _NC_CACHE = build_nc()
    return _NC_CACHE


def _host_prep(inputs):
    x = inputs["L0_attn_hidden_states"]
    qw = inputs["L0_attn_q_w"]; kw = inputs["L0_attn_k_w"]
    vw = inputs["L0_attn_v_w"]; ow = inputs["L0_attn_o_w"]
    Wb = inputs["L0_ffn_W_base"]; bb = inputs["L0_ffn_b_base"]
    Wg = inputs["L0_ffn_W_gate"]; A = inputs["L0_ffn_A"]; Bm = inputs["L0_ffn_B"]
    BF = ml_dtypes.bfloat16

    perm = np.empty(128, np.int64)
    perm[0::2] = np.arange(64)
    perm[1::2] = np.arange(64, 128)
    inv_freq = 1.0 / (10000.0 ** (np.arange(0, D, 2, dtype=np.float64) / D))
    freqs = np.arange(S, dtype=np.float64)[:, None] * inv_freq[None, :]
    cos64 = np.cos(freqs); sin64 = np.sin(freqs)
    cos_il = np.empty((128, S), np.float32)
    sin_il = np.empty((128, S), np.float32)
    cos_il[0::2] = cos64.T.astype(np.float32); cos_il[1::2] = cos64.T.astype(np.float32)
    sin_il[0::2] = (-sin64.T).astype(np.float32); sin_il[1::2] = sin64.T.astype(np.float32)
    cos_il = cos_il.astype(BF); sin_il = sin_il.astype(BF)

    pp, ff = np.mgrid[0:128, 0:512]
    tri = np.zeros((128, 2048), np.float32)
    for dj in range(4):
        tri[:, dj * 512:(dj + 1) * 512] = (ff >= pp + 128 * dj).astype(np.float32)
    tri = tri.astype(BF)
    e16 = np.zeros((8, 128), np.float32)
    for e in range(8):
        e16[e, e * 16:(e + 1) * 16] = 1.0
    e16 = e16.astype(BF)
    identb = np.eye(128, dtype=np.float32).astype(BF)
    identf = np.eye(128, dtype=np.float32)
    ones = np.ones((128, 128), np.float32).astype(BF)
    b_pm = np.ascontiguousarray(bb.reshape(32, 128).T)

    WbT = Wb.T
    wb_sl = np.ascontiguousarray(
        WbT.reshape(32, 128, 32, 128).transpose(2, 1, 0, 3).reshape(32, 128, 4096))
    wg_sl = np.ascontiguousarray(
        Wg.T.reshape(32, 128, 8).transpose(1, 0, 2).reshape(128, 256))
    af_sl = np.ascontiguousarray(
        A.reshape(128, HID).T.reshape(32, 128, 128).transpose(1, 0, 2).reshape(128, 4096))
    BfT = np.ascontiguousarray(Bm.transpose(0, 2, 1).reshape(128, HID))

    in_maps = []
    for c in range(8):
        b, i = c // 4, c % 4
        hsl = slice(i * 1024, (i + 1) * 1024)
        qwh = qw[hsl].reshape(8, 128, HID)[:, perm, :].reshape(1024, HID)
        kwh = kw[hsl].reshape(8, 128, HID)[:, perm, :].reshape(1024, HID)

        def slab_qkv(wT):
            return np.ascontiguousarray(
                wT.reshape(32, 128, 8, 128).transpose(2, 1, 0, 3).reshape(8, 128, 4096))

        owT = ow[:, hsl].T
        ow_sl = np.ascontiguousarray(
            owT.reshape(8, 128, 32, 128).transpose(2, 1, 0, 3).reshape(32, 128, 1024))
        in_maps.append({
            "x_q": np.ascontiguousarray(x[b, i * 1024:(i + 1) * 1024, :]),
            "qw_sl": slab_qkv(qwh.T),
            "kw_sl": slab_qkv(kwh.T),
            "vw_sl": slab_qkv(vw[hsl].T),
            "ow_sl": ow_sl,
            "wb_sl": wb_sl, "wg_sl": wg_sl, "af_sl": af_sl, "BfT": BfT, "b_pm": b_pm,
            "cos_il": cos_il, "sin_il": sin_il, "tri_in": tri, "e16_in": e16,
            "ident_in": identb, "identf_in": identf, "ones_in": ones,
            "impsc_in": np.array([[SCALE / (32.0 * (INSTR_END - INSTR_START))
                                   if b == 0 else 0.0]], np.float32),
        })
    return in_maps


def kernel(**inputs):
    nc = _get_nc()
    in_maps = _host_prep(inputs)
    res = run_bass_kernel_spmd(nc, in_maps, core_ids=list(range(8)))
    out = np.empty((2, S, HID), np.float32)
    for c in range(8):
        b, i = c // 4, c % 4
        out[b, i * 1024:(i + 1) * 1024, :] = res.results[c]["out_q"]
    return out


# version 18
# speedup vs baseline: 1.0876x; 1.0164x over previous
import math
import numpy as np
import ml_dtypes
import concourse.bass as bass
import concourse.mybir as mybir
from concourse.bass_utils import run_bass_kernel_spmd

F32 = mybir.dt.float32
BF16 = mybir.dt.bfloat16
AF = mybir.ActivationFunctionType
ALU = mybir.AluOpType
AX = mybir.AxisListType

S = 4096
HID = 4096
H_PER = 8
D = 128
NQ = 1024
SHIFT = 8.0
SCALE = 1.0 / math.sqrt(D)
INSTR_START, INSTR_END = 513, 4040
G4 = [[0, 1, 2, 3], [4, 5, 6, 7]]
G8 = [[0, 1, 2, 3, 4, 5, 6, 7]]
SWAP_MASK = [m for i in range(0, 32, 2) for m in (i + 1, i)]
ENGINES = ("sync", "gpsimd", "scalar", "vector", "tensor")


class Prog:

    def __init__(self, nc, sems):
        self.nc = nc
        self.sems = sems
        self.steps = {k: [] for k in ENGINES}
        self.cnt = {s: 0 for s in sems}
        self.res = {}
        self.pending = {k: [] for k in ENGINES}

    def op(self, eng, fn, waits=(), incs=(), reads=(), writes=()):
        wl = list(waits) + self.pending[eng]
        self.pending[eng] = []
        for r in reads:
            st = self.res.get(r)
            if st and st["w"]:
                wl.append(st["w"])
        for r in writes:
            st = self.res.get(r)
            if st:
                wl.extend(st["r"])
                if st["w"]:
                    wl.append(st["w"])
        wd = {}
        for s, v in wl:
            assert v <= self.cnt[s], f"wait {s}>={v} unproducible (at {self.cnt[s]})"
            wd[s] = max(wd.get(s, 0), v)
        assert incs, "every op needs a completion inc"
        for s, k in incs:
            self.cnt[s] += k
        self.steps[eng].append((fn, sorted(wd.items()), list(incs)))
        done = (incs[-1][0], self.cnt[incs[-1][0]])
        for r in reads:
            self.res.setdefault(r, {"w": None, "r": []})["r"].append(done)
        for r in writes:
            self.res[r] = {"w": done, "r": []}
        return done

    def dma(self, eng, out, in_, waits=(), reads=(), writes=(), sem="ld", **kw):
        def fn(e, out=out, in_=in_, kw=kw):
            return e.dma_start(out=out, in_=in_, **kw)
        return self.op(eng, fn, waits, [(sem, 16)], reads, writes)

    def barrier(self, exclude=("cc", "go")):
        snap = [(s, v) for s, v in self.cnt.items() if s not in exclude and v > 0]
        for e in ENGINES:
            self.pending[e] = list(snap)

    def n(self, name):
        return self.cnt[name]

    def emit(self, block):
        sems = self.sems

        def runner(name):
            def run(e):
                for fn, waits, incs in self.steps[name]:
                    for s, v in waits:
                        e.wait_ge(sems[s], v)
                    ins = fn(e)
                    for s, k in incs:
                        ins.then_inc(sems[s], k)
            return run

        block.sync(runner("sync"))
        block.gpsimd(runner("gpsimd"))
        block.scalar(runner("scalar"))
        block.vector(runner("vector"))
        block.tensor(runner("tensor"))


def build_nc():
    nc = bass.Bass("TRN2", target_bir_lowering=False)

    x_q = nc.dram_tensor("x_q", [NQ, HID], F32, kind="ExternalInput").ap()
    qw_sl = nc.dram_tensor("qw_sl", [8, 128, 4096], F32, kind="ExternalInput").ap()
    kw_sl = nc.dram_tensor("kw_sl", [8, 128, 4096], F32, kind="ExternalInput").ap()
    vw_sl = nc.dram_tensor("vw_sl", [8, 128, 4096], F32, kind="ExternalInput").ap()
    ow_sl = nc.dram_tensor("ow_sl", [32, 128, 1024], F32, kind="ExternalInput").ap()
    wb_sl = nc.dram_tensor("wb_sl", [32, 128, 4096], F32, kind="ExternalInput").ap()
    wg_sl = nc.dram_tensor("wg_sl", [128, 256], F32, kind="ExternalInput").ap()
    af_sl = nc.dram_tensor("af_sl", [128, 4096], F32, kind="ExternalInput").ap()
    BfT = nc.dram_tensor("BfT", [128, HID], F32, kind="ExternalInput").ap()
    b_pm = nc.dram_tensor("b_pm", [128, 32], F32, kind="ExternalInput").ap()
    cos_il = nc.dram_tensor("cos_il", [128, S], BF16, kind="ExternalInput").ap()
    sin_il = nc.dram_tensor("sin_il", [128, S], BF16, kind="ExternalInput").ap()
    tri_in = nc.dram_tensor("tri_in", [128, 2048], BF16, kind="ExternalInput").ap()
    e16_in = nc.dram_tensor("e16_in", [8, 128], BF16, kind="ExternalInput").ap()
    ident_in = nc.dram_tensor("ident_in", [128, 128], BF16, kind="ExternalInput").ap()
    identf_in = nc.dram_tensor("identf_in", [128, 128], F32, kind="ExternalInput").ap()
    ones_in = nc.dram_tensor("ones_in", [128, 128], BF16, kind="ExternalInput").ap()
    impsc_in = nc.dram_tensor("impsc_in", [1, 1], F32, kind="ExternalInput").ap()
    out_q = nc.dram_tensor("out_q", [NQ, HID], F32, kind="ExternalOutput").ap()

    xn_ag_in = nc.dram_tensor("xn_ag_in", [32, 128, NQ], BF16)
    xn_ag_out = nc.dram_tensor("xn_ag_out", [128, 128, NQ], BF16)
    qs = nc.dram_tensor("qs", [H_PER, 128, S], BF16)
    ks = nc.dram_tensor("ks", [H_PER, 128, S], BF16)
    vs = nc.dram_tensor("vs", [H_PER, 32, 128, 128], BF16)
    at_sp = nc.dram_tensor("at_sp", [H_PER, 128, S], BF16)
    imp_in = nc.dram_tensor("imp_in", [1, 512], F32)
    imp_out = nc.dram_tensor("imp_out", [1, 512], F32)
    s_rs_in = [nc.dram_tensor(f"s_rs_in{g}", [4, 8, 128, NQ], BF16) for g in range(4)]
    s_rs_out = [nc.dram_tensor(f"s_rs_out{g}", [8, 128, NQ], BF16) for g in range(4)]
    g_sp = nc.dram_tensor("g_sp", [32, 128, NQ], BF16)
    sgT_sp = nc.dram_tensor("sgT_sp", [8, 128, HID], BF16)

    ctxs = []

    def ent(g):
        ctxs.append(g)
        return g.__enter__()

    bigb = ent(nc.sbuf_tensor("bigb", [128, 49152], BF16))
    bigf = ent(nc.sbuf_tensor("bigf", [128, 13312], F32))
    cosb = ent(nc.sbuf_tensor("cosb", [128, S], BF16))
    sinb = ent(nc.sbuf_tensor("sinb", [128, S], BF16))
    trib = ent(nc.sbuf_tensor("trib", [128, 2048], BF16))
    identb = ent(nc.sbuf_tensor("identb", [128, 128], BF16))
    identf = ent(nc.sbuf_tensor("identf", [128, 128], F32))
    onesb = ent(nc.sbuf_tensor("onesb", [128, 128], BF16))
    e16b = ent(nc.sbuf_tensor("e16b", [8, 128], BF16))
    bpmb = ent(nc.sbuf_tensor("bpmb", [128, 32], F32))
    impscb = ent(nc.sbuf_tensor("impscb", [1, 1], F32))
    smallf = ent(nc.sbuf_tensor("smallf", [128, 64], F32))
    qbarf = ent(nc.sbuf_tensor("qbarf", [128, 8], F32))
    qbarb = ent(nc.sbuf_tensor("qbarb", [128, 8], BF16))
    biaskt = ent(nc.sbuf_tensor("biaskt", [128, 8], F32))
    impktpm = ent(nc.sbuf_tensor("impktpm", [128, 8], F32))
    impbc = ent(nc.sbuf_tensor("impbc", [128, 512], F32))
    negshb = ent(nc.sbuf_tensor("negshb", [128, 1], F32))
    imps = ent(nc.sbuf_tensor("imps", [1, 512], F32))

    psA = ent(nc.psum_tensor("psA", [128, 1024], F32))
    psB = ent(nc.psum_tensor("psB", [128, 1024], F32))
    psD = ent(nc.psum_tensor("psD", [128, 512], F32))
    psO = ent(nc.psum_tensor("psO", [128, 512], F32))
    psT = ent(nc.psum_tensor("psT", [128, 2048], BF16))
    psTf = psT[:, :].bitcast(F32)
    SEQ8 = [0, 4, 1, 5, 2, 6, 3, 7]
    PTC = [(s % 4) * 128 + (s // 4) * 1024 for s in range(8)]
    PS6 = [psA[:, 0:512], psA[:, 512:1024], psB[:, 0:512], psB[:, 512:1024],
           psD[:, :], psO[:, :]]
    PS6N = ["psA", "psA", "psB", "psB", "psDn", "psOu"]

    sems = {}
    for s in ["ld", "ld0", "ld1", "ldg", "st", "st0", "st1", "wl0", "wl1",
              "cc", "pe", "act", "dve", "go"]:
        sems[s] = ent(nc.semaphore(s))

    p = Prog(nc, sems)

    for dst, src in [(cosb[:, :], cos_il), (sinb[:, :], sin_il), (trib[:, :], tri_in),
                     (identb[:, :], ident_in), (identf[:, :], identf_in),
                     (onesb[:, :], ones_in), (e16b[:, :], e16_in),
                     (bpmb[:, :], b_pm), (impscb[:, :], impsc_in)]:
        p.dma("sync", dst, src, writes=["consts"])

    x_t = [bigf[:, 0:4096], bigf[:, 4096:8192]]
    sqs = bigf[:, 8192:12288]
    stage = bigb[:, 0:32768]
    xn_tm = [bigb[:, 32768:36864], bigb[:, 36864:40960]]
    s1 = smallf[:, 0:1]; s2 = smallf[:, 1:2]; mmean = smallf[:, 2:3]
    mvar = smallf[:, 3:4]; rstd = smallf[:, 4:5]; nbias = smallf[:, 5:6]
    mm2 = smallf[:, 6:7]; tmpw = smallf[:, 8:9]

    for tt in range(8):
        bx = f"x{tt % 2}"
        p.dma("sync" if tt % 2 == 0 else "gpsimd",
              x_t[tt % 2], x_q[tt * 128:(tt + 1) * 128, :], writes=[bx],
              sem="ld0" if tt % 2 == 0 else "ldg")
        p.op("scalar", lambda e, b=x_t[tt % 2]: e.activation(sqs, b, AF.Square, accum_out=s2),
             incs=[("act", 1)], reads=[bx], writes=["s2", "sqs"])
        p.op("vector", lambda e, b=x_t[tt % 2]: e.reduce_sum(s1, b, AX.X),
             incs=[("dve", 1)], reads=[bx], writes=["s1"])
        p.op("vector", lambda e: e.tensor_scalar(mmean, s1, 1.0 / HID, None, ALU.mult),
             incs=[("dve", 1)], reads=["s1"], writes=["m"])
        p.op("vector", lambda e: e.tensor_tensor(mm2, mmean, mmean, ALU.mult),
             incs=[("dve", 1)], reads=["m"], writes=["mm2"])
        p.op("vector", lambda e: e.tensor_scalar(mvar, s2, 1.0 / HID, None, ALU.mult),
             incs=[("dve", 1)], reads=["s2"], writes=["var"])
        p.op("vector", lambda e: e.tensor_sub(mvar, mvar, mm2),
             incs=[("dve", 1)], reads=["mm2", "var"], writes=["var"])
        p.op("vector", lambda e: e.tensor_scalar(mvar, mvar, 1e-5, None, ALU.add),
             incs=[("dve", 1)], reads=["var"], writes=["var"])
        p.op("scalar", lambda e: e.activation(rstd, mvar, AF.Ln),
             incs=[("act", 1)], reads=["var"], writes=["rstd"])
        p.op("scalar", lambda e: e.activation(rstd, rstd, AF.Exp, scale=-0.5),
             incs=[("act", 1)], reads=["rstd"], writes=["rstd"])
        p.op("vector", lambda e: e.scalar_tensor_tensor(nbias, mmean, -1.0, rstd,
                                                        ALU.mult, ALU.mult),
             incs=[("dve", 1)], reads=["m", "rstd"], writes=["nb"])
        bxn = f"xnt{tt % 2}"
        p.op("vector", lambda e, b=x_t[tt % 2], o=xn_tm[tt % 2]: e.tensor_scalar(
            o, b, rstd, nbias, ALU.mult, ALU.add),
             incs=[("dve", 1)], reads=[bx, "rstd", "nb"], writes=[bxn])
        for ci in range(32):
            s8 = SEQ8[ci % 8]
            slot = f"psT{s8}"
            p.op("tensor", lambda e, ci=ci, o=xn_tm[tt % 2], c0=PTC[s8]: e.transpose(
                psT[:, c0:c0 + 128], o[:, ci * 128:(ci + 1) * 128], identb[:, :]),
                 incs=[("pe", 1)], reads=[bxn, "consts"], writes=[slot])
            p.op("vector", lambda e, ci=ci, tt=tt, c0=PTC[s8]: e.tensor_copy(
                stage[:, ci * 1024 + tt * 128: ci * 1024 + (tt + 1) * 128],
                psT[:, c0:c0 + 128]),
                 incs=[("dve", 1)], reads=[slot],
                 writes=[f"stg_{ci}_{tt}", f"xn_{ci}"])
    for ci in range(32):
        p.dma("scalar", xn_ag_in.ap()[ci, :, :], stage[:, ci * 1024:(ci + 1) * 1024],
              reads=[f"stg_{ci}_{tt}" for tt in range(8)], writes=[f"agin_{ci}"], sem="st")
    p.op("gpsimd", lambda e: e.collective_compute(
        "AllGather", ALU.bypass, replica_groups=G4,
        ins=[xn_ag_in.ap().opt()], outs=[xn_ag_out.ap().opt()]),
         incs=[("cc", 1)], reads=[f"agin_{ci}" for ci in range(32)], writes=["xnag"])

    PSP = [psA[:, 0:512], psA[:, 512:1024]]
    wgb = bigb[:, 45056:45312]
    wgstg = bigf[:, 10240:10496]
    p.dma("sync", wgstg, wg_sl[:, :], writes=["wgs", "sqs"], sem="wl0")
    p.op("vector", lambda e: e.tensor_copy(wgb, wgstg),
         incs=[("dve", 1)], reads=["wgs"], writes=["wgb"])
    logitf = bigf[:, 0:1024]
    for rng in range(2):
        for ic in range(32):
            p.op("tensor", lambda e, ic=ic, rng=rng: e.matmul(
                psD[0:8, 0:512], wgb[:, ic * 8:(ic + 1) * 8],
                xn_sb[ic][:, rng * 512:(rng + 1) * 512],
                start=(ic == 0), stop=(ic == 31)),
                 incs=[("pe", 1)],
                 reads=["wgb", f"xn_{ic}"],
                 writes=["psDn"] if ic in (0, 31) else [])
        p.op("vector", lambda e, rng=rng: e.tensor_copy(
            logitf[0:8, rng * 512:(rng + 1) * 512], psD[0:8, 0:512]),
             incs=[("dve", 1)], reads=["psDn"], writes=[f"lg{rng}", "x0"])
    routef = bigb[:, 45312:46336]
    ltt = bigf[:, 2048:2056]
    m1 = smallf[:, 40:41]; m2 = smallf[:, 41:42]; nm1 = smallf[:, 42:43]
    eqm = smallf[:, 43:51]; l2f = smallf[:, 51:59]
    for t2 in range(8):
        p.op("tensor", lambda e, t2=t2: e.transpose(
            psD[:, 0:8], logitf[0:8, t2 * 128:(t2 + 1) * 128], identf[0:8, 0:8]),
             incs=[("pe", 1)], reads=["lg0", "lg1"], writes=["psDn"])
        p.op("vector", lambda e, t2=t2: e.tensor_copy(ltt, psD[:, 0:8]),
             incs=[("dve", 1)], reads=["psDn"],
             writes=["ltt"] + (["x0"] if t2 == 0 else []))
        p.op("vector", lambda e: e.reduce_max(m1, ltt, AX.X),
             incs=[("dve", 1)], reads=["ltt"], writes=["gm1"])
        p.op("vector", lambda e: e.tensor_scalar(eqm, ltt, m1, None, ALU.is_equal),
             incs=[("dve", 1)], reads=["ltt", "gm1"], writes=["geq"])
        p.op("vector", lambda e: e.scalar_tensor_tensor(l2f, eqm, -30000.0, ltt,
                                                        ALU.mult, ALU.add),
             incs=[("dve", 1)], reads=["geq", "ltt"], writes=["gl2"])
        p.op("vector", lambda e: e.reduce_max(m2, l2f, AX.X),
             incs=[("dve", 1)], reads=["gl2"], writes=["gm2"])
        p.op("vector", lambda e: e.tensor_scalar(eqm, ltt, m2, None, ALU.is_ge),
             incs=[("dve", 1)], reads=["ltt", "gm2"], writes=["geq"])
        p.op("vector", lambda e: e.tensor_scalar(nm1, m1, -1.0, None, ALU.mult),
             incs=[("dve", 1)], reads=["gm1"], writes=["gnm1"])
        p.op("scalar", lambda e: e.activation(l2f, ltt, AF.Exp, bias=nm1),
             incs=[("act", 1)], reads=["ltt", "gnm1", "gl2"], writes=["gl2"])
        p.op("vector", lambda e: e.tensor_mul(l2f, l2f, eqm),
             incs=[("dve", 1)], reads=["gl2", "geq"], writes=["gl2"])
        p.op("vector", lambda e: e.reduce_sum(m2, l2f, AX.X),
             incs=[("dve", 1)], reads=["gl2"], writes=["gm2"])
        p.op("vector", lambda e: e.reciprocal(m1, m2),
             incs=[("dve", 1)], reads=["gm2"], writes=["gm1"])
        p.op("vector", lambda e: e.tensor_scalar(l2f, l2f, m1, None, ALU.mult),
             incs=[("dve", 1)], reads=["gl2", "gm1"], writes=["gl2"])
        p.op("tensor", lambda e: e.transpose(psD[0:8, 128:256], l2f, identf[:, :]),
             incs=[("pe", 1)], reads=["gl2"], writes=["psDn"])
        p.op("vector", lambda e, t2=t2: e.tensor_copy(
            routef[0:8, t2 * 128:(t2 + 1) * 128], psD[0:8, 128:256]),
             incs=[("dve", 1)], reads=["psDn"], writes=[f"rt{t2}"])
    afb = bigb[:, 40960:45056]
    hsb = bigb[:, 46336:47360]
    hwb = bigb[:, 47360:48384]
    afstg = bigf[:, 0:4096]
    p.dma("sync", afstg, af_sl[:, :],
          writes=["afs", "lg0", "lg1", "ltt"], sem="wl1")
    p.op("vector", lambda e: e.tensor_copy(afb, afstg),
         incs=[("dve", 1)], reads=["afs"], writes=["afb"])
    for rng in range(2):
        for ic in range(32):
            p.op("tensor", lambda e, ic=ic, rng=rng: e.matmul(
                psO[:, :], afb[:, ic * 128:(ic + 1) * 128],
                xn_sb[ic][:, rng * 512:(rng + 1) * 512],
                start=(ic == 0), stop=(ic == 31)),
                 incs=[("pe", 1)],
                 reads=["afb", f"xn_{ic}"],
                 writes=["psOu"] if ic in (0, 31) else [])
        p.op("vector", lambda e, rng=rng: e.tensor_copy(
            hsb[:, rng * 512:(rng + 1) * 512], psO[:, :]),
             incs=[("dve", 1)], reads=["psOu"], writes=[f"h{rng}"])
        p.op("tensor", lambda e, rng=rng: e.matmul(
            psD[:, :], e16b[:, :], routef[0:8, rng * 512:(rng + 1) * 512],
            start=True, stop=True),
             incs=[("pe", 1)], reads=[f"rt{t2}" for t2 in range(8)] + ["consts"],
             writes=["psDn"])
        p.op("vector", lambda e, rng=rng: e.scalar_tensor_tensor(
            hwb[:, rng * 512:(rng + 1) * 512], psD[:, :], 1.0,
            hsb[:, rng * 512:(rng + 1) * 512], ALU.mult, ALU.mult),
             incs=[("dve", 1)], reads=["psDn", f"h{rng}"], writes=[f"hw{rng}"])
    wbb = [bigb[:, 32768:36864], bigb[:, 36864:40960]]
    bfb = bigb[:, 40960:45056]
    bfstg = bigf[:, 4096:8192]
    p.dma("sync", bfstg, BfT[:, :], writes=["bfs", "x1"], sem="wl0")
    p.op("vector", lambda e: e.tensor_copy(bfb[:, 0:4096], bfstg),
         incs=[("dve", 1)], reads=["bfs"], writes=["afb", "bfb"])
    gstg = [bigb[:, 46336:46848], bigb[:, 46848:47360]]
    wbstg = [bigf[:, 0:4096], bigf[:, 4096:8192]]
    p.dma("sync", wbstg[0], wb_sl[0, :, :], writes=["wbs0", "afs", "bfs"], sem="wl0")
    p.dma("sync", wbstg[1], wb_sl[1, :, :], writes=["wbs1", "bfs"], sem="wl1")
    p.op("vector", lambda e: e.tensor_copy(wbb[0], wbstg[0]),
         incs=[("dve", 1)], reads=["wbs0"], writes=["wb0", "xnt0"])
    for ob in range(32):
        wres = f"wb{ob % 2}"
        if ob + 2 < 32:
            p.dma("sync", wbstg[ob % 2], wb_sl[ob + 2, :, :], writes=[f"wbs{ob % 2}"],
                  sem=f"wl{ob % 2}")
        if ob + 1 < 32:
            p.op("vector", lambda e, b=(ob + 1) % 2: e.tensor_copy(wbb[b], wbstg[b]),
                 incs=[("dve", 1)], reads=[f"wbs{(ob + 1) % 2}"],
                 writes=[f"wb{(ob + 1) % 2}"] + (["xnt1"] if ob == 0 else []))
        for ic in range(32):
            for rng in range(2):
                p.op("tensor", lambda e, ic=ic, ob=ob, rng=rng, ps=PSP[rng % 2]: e.matmul(
                    ps, wbb[ob % 2][:, ic * 128:(ic + 1) * 128],
                    xn_sb[ic][:, rng * 512:(rng + 1) * 512],
                    start=(ic == 0), stop=False),
                     incs=[("pe", 1)],
                     reads=[wres, f"xn_{ic}"],
                     writes=(["psP%d" % (rng % 2)] if ic == 0 else []))
        for rng in range(2):
            pres = "psP%d" % (rng % 2)
            p.op("tensor", lambda e, ob=ob, rng=rng, ps=PSP[rng % 2]: e.matmul(
                ps, bfb[:, ob * 128:(ob + 1) * 128],
                hwb[:, rng * 512:(rng + 1) * 512],
                start=False, stop=True),
                 incs=[("pe", 1)], reads=["bfb", f"hw{rng}"], writes=[pres])
            gres = f"gs{rng}"
            p.op("scalar", lambda e, ob=ob, ps=PSP[rng % 2], sb2=gstg[rng]: e.activation(
                sb2, ps, AF.Silu, bias=bpmb[:, ob:ob + 1]),
                 incs=[("act", 1)], reads=[pres, "consts"],
                 writes=[gres] + ([f"h{rng}"] if ob == 0 else []))
            p.dma("scalar", g_sp.ap()[ob, :, rng * 512:(rng + 1) * 512], gstg[rng],
                  reads=[gres], writes=[f"g_{ob}_{rng}"], sem=f"st{rng}")


    p.barrier()

    xn_sb = [bigb[:, ci * 1024:(ci + 1) * 1024] for ci in range(32)]
    w_sb = [bigb[:, 32768:36864], bigb[:, 36864:40960]]
    qraw = bigb[:, 40960:41472]
    rotb = bigb[:, 41472:41984]
    tcos = bigb[:, 41984:42496]
    qst = [bigb[:, 42496:43008], bigb[:, 43008:43520]]
    vstage = bigb[:, 43520:44032]

    p.op("vector", lambda e: e.memset(qbarf[:, :], 0.0), incs=[("dve", 1)],
         writes=["qbar"])
    negsh = negshb[:, 0:1]
    p.op("vector", lambda e: e.memset(negsh, -SHIFT), incs=[("dve", 1)],
         writes=["nsh"])
    wsrc = {"q": qw_sl, "k": kw_sl, "v": vw_sl}
    wstg = [bigf[:, 0:4096], bigf[:, 4096:8192]]

    def w_src(s):
        obi = s % 24
        kind, h = ("q", obi) if obi < 8 else (("k", obi - 8) if obi < 16 else ("v", obi - 16))
        return wsrc[kind][h, :, :]

    def kind_h(obi):
        return ("q", obi) if obi < 8 else (("k", obi - 8) if obi < 16 else ("v", obi - 16))

    kvis = bigb[:, 44032:44544]
    p.dma("sync", wstg[0], w_src(0), writes=["ws0"], sem="wl0")
    p.dma("sync", wstg[1], w_src(1), writes=["ws1"], sem="wl1")
    p.op("vector", lambda e: e.tensor_copy(w_sb[0], wstg[0]),
         incs=[("dve", 1)], reads=["ws0"], writes=["w0"])

    for s in range(96):
        qi, obi = s // 24, s % 24
        kind, h = kind_h(obi)
        if obi == 0:
            for ci in range(32):
                p.dma("sync", xn_sb[ci], xn_ag_out.ap()[qi * 32 + ci, :, :],
                      reads=["xnag"],
                      writes=[f"xn_{ci}"] + (["xnq"] if ci == 31 else []))
        if s + 2 < 96:
            p.dma("sync", wstg[s % 2], w_src(s + 2), writes=[f"ws{s % 2}"],
                  sem=f"wl{s % 2}")
        if s + 1 < 96:
            p.op("vector", lambda e, b=(s + 1) % 2: e.tensor_copy(w_sb[b], wstg[b]),
                 incs=[("dve", 1)], reads=[f"ws{(s + 1) % 2}"], writes=[f"w{(s + 1) % 2}"])
        wres = f"w{s % 2}"
        slot0, slot1 = (s * 2) % 6, (s * 2 + 1) % 6
        for ic in range(32):
            p.op("tensor", lambda e, ic=ic, w=w_sb[s % 2], ps=PS6[slot0]:
                 e.matmul(ps, w[:, ic * 128:(ic + 1) * 128],
                          xn_sb[ic][:, 0:512],
                          start=(ic == 0), stop=(ic == 31)),
                 incs=[("pe", 1)],
                 reads=[wres, f"xn_{ic}"] + (["xnq"] if ic == 0 else []),
                 writes=[PS6N[slot0]] if ic in (0, 31) else [])
            p.op("tensor", lambda e, ic=ic, w=w_sb[s % 2], ps=PS6[slot1]:
                 e.matmul(ps, w[:, ic * 128:(ic + 1) * 128],
                          xn_sb[ic][:, 512:1024],
                          start=(ic == 0), stop=(ic == 31)),
                 incs=[("pe", 1)],
                 reads=[wres, f"xn_{ic}"],
                 writes=[PS6N[slot1]] if ic in (0, 31) else [])
        for rng in range(2):
            slot = (s * 2 + rng) % 6
            pres = PS6N[slot]
            tok0 = qi * 1024 + rng * 512
            if kind in ("q", "k"):
                p.op("scalar", lambda e, ps=PS6[slot]: e.activation(qraw, ps, AF.Copy),
                     incs=[("act", 1)], reads=[pres], writes=["qraw"])
                stres = f"qst{s % 2}"
                p.op("vector", lambda e, t0=tok0: e.tensor_mul(
                    tcos, qraw, cosb[:, t0:t0 + 512]),
                     incs=[("dve", 1)], reads=["qraw"], writes=["tcos"])
                p.op("vector", lambda e: e.stream_shuffle(rotb, qraw, SWAP_MASK),
                     incs=[("dve", 1)], reads=["qraw"], writes=["rot"])
                p.op("vector", lambda e, t0=tok0: e.tensor_mul(
                    rotb, rotb, sinb[:, t0:t0 + 512]),
                     incs=[("dve", 1)], reads=["rot"], writes=["rot"])
                p.op("vector", lambda e, sb2=qst[s % 2]: e.tensor_add(
                    sb2, tcos, rotb),
                     incs=[("dve", 1)], reads=["tcos", "rot"], writes=[stres])
                dst = (qs if kind == "q" else ks).ap()[h, :, tok0:tok0 + 512]
                p.dma("scalar", dst, qst[s % 2],
                      reads=[stres], writes=[f"sp_{kind}{h}_{qi}{rng}"],
                      sem=f"st{s % 2}")
                if kind == "q":
                    lo = max(INSTR_START, tok0); hi = min(INSTR_END, tok0 + 512)
                    if lo < hi:
                        p.op("vector", lambda e, sb2=qst[s % 2],
                             lo=lo, hi=hi, t0=tok0: e.reduce_sum(
                                 tmpw, sb2[:, lo - t0:hi - t0], AX.X),
                             incs=[("dve", 1)], reads=[stres], writes=["tmpw"])
                        p.op("vector", lambda e, h=h: e.tensor_add(
                            qbarf[:, h:h + 1], qbarf[:, h:h + 1], tmpw),
                             incs=[("dve", 1)], reads=["tmpw", "qbar"], writes=["qbar"])
            else:
                p.op("scalar", lambda e, ps=PS6[slot]: e.activation(vstage, ps, AF.Copy),
                     incs=[("act", 1)], reads=[pres], writes=["vst"])
                for i4 in range(4):
                    p.op("tensor", lambda e, i4=i4: e.transpose(
                        psT[:, i4 * 128:(i4 + 1) * 128],
                        vstage[:, i4 * 128:(i4 + 1) * 128], identb[:, :]),
                         incs=[("pe", 1)], reads=["vst"], writes=[f"psT{i4}"])
                stres = f"qst{s % 2}"
                p.op("vector", lambda e, v2=qst[s % 2]: e.tensor_copy(
                    v2, psT[:, 0:512]),
                     incs=[("dve", 1)], reads=[f"psT{i4}" for i4 in range(4)],
                     writes=[stres])
                kt0 = tok0 // 128
                p.dma("scalar", vs.ap()[h, kt0:kt0 + 4, :, :].rearrange(
                    "t p c -> p t c"),
                      qst[s % 2].rearrange("p (t c) -> p t c", c=128),
                      reads=[stres], writes=[f"sp_v{h}_{qi}{rng}"],
                      sem=f"st{s % 2}")
        if s == 72 + 7:
            p.op("vector", lambda e: e.tensor_copy(qbarb[:, :], qbarf[:, :]),
                 incs=[("dve", 1)], reads=["qbar"], writes=["qbarb"])
            for h2 in range(H_PER):
                p.dma("sync", kvis[:, :], ks.ap()[h2, :, 1:513],
                      reads=[f"sp_k{h2}_00"], writes=["kvis"])
                p.op("tensor", lambda e, h2=h2: e.matmul(
                    psD[0:1, :], qbarb[:, h2:h2 + 1], kvis[:, :],
                    start=(h2 == 0), stop=(h2 == 7)),
                     incs=[("pe", 1)], reads=["kvis", "qbarb"],
                     writes=(["psDn"] if h2 in (0, 7) else []))
            p.op("vector", lambda e: e.tensor_scalar(imps[:, :], psD[0:1, :],
                                                     impscb[0:1, 0:1], None, ALU.mult),
                 incs=[("dve", 1)], reads=["psDn", "consts"], writes=["imps"])
            p.dma("scalar", imp_in.ap()[:, :], imps[:, :], reads=["imps"],
                  writes=["impin"], sem="st")
            p.op("gpsimd", lambda e: e.collective_compute(
                "AllReduce", ALU.add, replica_groups=G8,
                ins=[imp_in.ap().opt()], outs=[imp_out.ap().opt()]),
                 incs=[("cc", 1)], reads=["impin"], writes=["impout"])
    p.op("vector", lambda e: e.memset(impktpm[:, :], 3.0e38),
         incs=[("dve", 1)], writes=["iktpm"])
    p.dma("sync", impbc[:, :], imp_out.ap()[0:1, :].partition_broadcast(128),
          reads=["impout"], writes=["impbc"])
    p.dma("sync", impktpm[1:128, 0:1], imp_out.ap()[0:1, 0:127].rearrange("a c -> c a"),
          reads=["impout"], writes=["iktpm"])
    for j in range(1, 4):
        p.dma("sync", impktpm[:, j:j + 1],
              imp_out.ap()[0:1, j * 128 - 1:(j + 1) * 128 - 1].rearrange("a c -> c a"),
              reads=["impout"], writes=["iktpm"])
    p.dma("sync", impktpm[0:1, 4:5], imp_out.ap()[0:1, 511:512],
          reads=["impout"], writes=["iktpm"])
    gtbuf = bigf[:, 12288:12800]
    cntt = smallf[:, 16:24]
    for j in range(5):
        p.op("vector", lambda e, j=j: e.tensor_scalar(
            gtbuf, impbc[:, :], impktpm[:, j:j + 1], None, ALU.is_gt),
             incs=[("dve", 1)], reads=["impbc", "iktpm"], writes=["gt"])
        p.op("vector", lambda e, j=j: e.reduce_sum(cntt[:, j:j + 1], gtbuf, AX.X),
             incs=[("dve", 1)], reads=["gt"], writes=[f"cnt{j}"])
    p.op("vector", lambda e: e.tensor_scalar(biaskt[:, 0:5], cntt[:, 0:5], 255.5,
                                             None, ALU.is_lt),
         incs=[("dve", 1)], reads=[f"cnt{j}" for j in range(5)], writes=["bk"])
    p.op("vector", lambda e: e.tensor_scalar(biaskt[:, 0:5], biaskt[:, 0:5],
                                             1.0e6 - SHIFT, -1.0e6, ALU.mult, ALU.add),
         incs=[("dve", 1)], reads=["bk"], writes=["bk"])

    p.barrier()

    qh = [bigb[:, 0:4096], bigb[:, 4096:8192]]
    kh = [bigb[:, 8192:12288], bigb[:, 12288:16384]]
    vh = [bigb[:, 16384:20480], bigb[:, 20480:24576]]
    p_sb = bigb[:, 24576:40960]
    atst = [bigb[:, 40960:41472], bigb[:, 41472:41984]]
    denf = bigf[:, 0:512]
    recipf = bigf[:, 512:1024]

    for h in range(H_PER):
        hb = h % 2
        p.dma("sync", qh[hb], qs.ap()[h, :, :],
              reads=[f"sp_q{h}_{qi}{rng}" for qi in range(4) for rng in range(2)],
              writes=[f"qh{hb}"], sem=f"ld{hb}")
        p.dma("sync", kh[hb], ks.ap()[h, :, :],
              reads=[f"sp_k{h}_{qi}{rng}" for qi in range(4) for rng in range(2)],
              writes=[f"kh{hb}"], sem=f"ld{hb}")
        p.dma("sync", vh[hb].rearrange("p (t c) -> p t c", c=128),
              vs.ap()[h, :, :, :].rearrange("t p c -> p t c"),
              reads=[f"sp_v{h}_{qi}{rng}" for qi in range(4) for rng in range(2)],
              writes=[f"vh{hb}"], sem=f"ld{hb}")
        SC3 = [psA, psB, psTf]
        SC3N = ["sc0", "sc1", "sc2"]
        DEPTH = 2
        for r in range(8):
            nkt = 4 * (r + 1)
            npair = nkt // 2

            def emit_scores(pp, r=r, hb=hb, nkt=nkt):
                jp = 2 * pp
                pb = SC3[pp % 3]
                pres = SC3N[pp % 3]
                for o2 in range(2):
                    j = jp + o2
                    p.op("tensor", lambda e, j=j, r=r, pb=pb, o2=o2, hb=hb: e.matmul(
                        pb[:, o2 * 512:(o2 + 1) * 512],
                        kh[hb][:, j * 128:(j + 1) * 128],
                        qh[hb][:, r * 512:(r + 1) * 512],
                        start=True, stop=True),
                         incs=[("pe", 1)], reads=[f"qh{hb}", f"kh{hb}"],
                         writes=[pres])
                if jp < 6:
                    for j in (jp, jp + 1):
                        o2 = j - jp
                        if j < 5:
                            p.op("scalar", lambda e, j=j, pb=pb, o2=o2: e.activation(
                                p_sb[:, j * 512:(j + 1) * 512],
                                pb[:, o2 * 512:(o2 + 1) * 512], AF.Exp,
                                bias=biaskt[:, j:j + 1], scale=SCALE),
                                 incs=[("act", 1)], reads=[pres, "bk"],
                                 writes=[f"p{j}"])
                        else:
                            p.op("scalar", lambda e, j=j, pb=pb, o2=o2: e.activation(
                                p_sb[:, j * 512:(j + 1) * 512],
                                pb[:, o2 * 512:(o2 + 1) * 512], AF.Exp,
                                bias=negsh, scale=SCALE),
                                 incs=[("act", 1)], reads=[pres, "nsh"],
                                 writes=[f"p{j}"])
                else:
                    p.op("scalar", lambda e, j0=jp, pb=pb: e.activation(
                        p_sb[:, j0 * 512:(j0 + 2) * 512], pb[:, 0:1024], AF.Exp,
                        bias=negsh, scale=SCALE),
                         incs=[("act", 1)], reads=[pres, "nsh"],
                         writes=[f"p{jp}", f"p{jp + 1}"])
                for o2 in range(2):
                    j = jp + o2
                    dj = j - 4 * r
                    if 0 <= dj < 4:
                        p.op("vector", lambda e, j=j, dj=dj: e.tensor_mul(
                            p_sb[:, j * 512:(j + 1) * 512],
                            p_sb[:, j * 512:(j + 1) * 512],
                            trib[:, dj * 512:(dj + 1) * 512]),
                             incs=[("dve", 1)], reads=[f"p{j}"], writes=[f"p{j}"])

            def emit_consume(qp, hb=hb, nkt=nkt):
                jq = 2 * qp
                for j in (jq, jq + 1):
                    p.op("tensor", lambda e, j=j, nkt=nkt: e.matmul(
                        psD[:, :], onesb[:, :], p_sb[:, j * 512:(j + 1) * 512],
                        start=(j == 0), stop=(j == nkt - 1)),
                         incs=[("pe", 1)], reads=[f"p{j}"],
                         writes=["psDn"] if j in (0, nkt - 1) else [])
                for j in (jq, jq + 1):
                    p.op("tensor", lambda e, j=j, hb=hb, nkt=nkt: e.matmul(
                        psO[:, :], vh[hb][:, j * 128:(j + 1) * 128],
                        p_sb[:, j * 512:(j + 1) * 512],
                        start=(j == 0), stop=(j == nkt - 1)),
                         incs=[("pe", 1)], reads=[f"p{j}", f"vh{hb}"],
                         writes=["psOu"] if j in (0, nkt - 1) else [])

            for pp in range(npair + DEPTH):
                if pp < npair:
                    emit_scores(pp)
                if pp >= DEPTH and pp - DEPTH < npair:
                    emit_consume(pp - DEPTH)
            p.op("scalar", lambda e: e.activation(denf, psD[:, :], AF.Ln),
                 incs=[("act", 1)], reads=["psDn"], writes=["denf"])
            p.op("scalar", lambda e: e.activation(recipf, denf, AF.Exp, scale=-1.0),
                 incs=[("act", 1)], reads=["denf"], writes=["recip"])
            ares = f"at{r % 2}"
            p.op("vector", lambda e, ab=atst[r % 2]: e.tensor_mul(ab, psO[:, :], recipf),
                 incs=[("dve", 1)], reads=["psOu", "recip"], writes=[ares])
            p.dma("scalar", at_sp.ap()[h, :, r * 512:(r + 1) * 512], atst[r % 2],
                  reads=[ares], writes=[f"at_{h}_{r}"], sem=f"st{r % 2}")

    p.barrier()

    owb = [bigb[:, 41984:43008], bigb[:, 43008:44032]]
    atc = [bigb[:, hc * 4096:(hc + 1) * 4096] for hc in range(8)]
    sstg = [bigb[:, 44032:44544], bigb[:, 44544:45056]]

    for hc in range(8):
        p.dma("sync", atc[hc], at_sp.ap()[hc, :, :],
              reads=[f"at_{hc}_{r}" for r in range(8)],
              writes=[f"atc{hc}"], sem=f"ld{hc % 2}")
    owstg = [bigf[:, 8192:9216], bigf[:, 9216:10240]]
    p.dma("sync", owstg[0], ow_sl[0, :, :], writes=["ows0"], sem="wl0")
    p.dma("sync", owstg[1], ow_sl[1, :, :], writes=["ows1"], sem="wl1")
    p.op("vector", lambda e: e.tensor_copy(owb[0], owstg[0]),
         incs=[("dve", 1)], reads=["ows0"], writes=["ow0"])
    for ob in range(32):
        wres = f"ow{ob % 2}"
        if ob + 2 < 32:
            p.dma("sync", owstg[ob % 2], ow_sl[ob + 2, :, :], writes=[f"ows{ob % 2}"],
                  sem=f"wl{ob % 2}")
        if ob + 1 < 32:
            p.op("vector", lambda e, b=(ob + 1) % 2: e.tensor_copy(owb[b], owstg[b]),
                 incs=[("dve", 1)], reads=[f"ows{(ob + 1) % 2}"], writes=[f"ow{(ob + 1) % 2}"])
        PSP8 = [psA[:, 0:512], psA[:, 512:1024], psB[:, 0:512], psB[:, 512:1024],
                psD[:, :], psO[:, :], psTf[:, 0:512], psTf[:, 512:1024]]
        for hc in range(8):
            for rr in range(8):
                p.op("tensor", lambda e, hc=hc, ob=ob, rr=rr, ps=PSP8[rr]: e.matmul(
                    ps, owb[ob % 2][:, hc * 128:(hc + 1) * 128],
                    atc[hc][:, rr * 512:(rr + 1) * 512],
                    start=(hc == 0), stop=(hc == 7)),
                     incs=[("pe", 1)],
                     reads=[wres, f"atc{hc}"],
                     writes=([f"psQ{rr}"] if hc in (0, 7) else []))
        for rr in range(8):
            sres = f"ss{rr % 2}"
            p.op("vector", lambda e, ps=PSP8[rr], sb2=sstg[rr % 2]: e.tensor_copy(sb2, ps),
                 incs=[("dve", 1)], reads=[f"psQ{rr}"], writes=[sres])
            qq, hh = rr // 2, rr % 2
            p.dma("scalar", s_rs_in[ob // 8].ap()[qq, ob % 8, :, hh * 512:(hh + 1) * 512],
                  sstg[rr % 2], reads=[sres], writes=[f"srs_{ob}_{rr}"],
                  sem=f"st{rr % 2}")
        if ob % 8 == 7:
            g = ob // 8
            p.op("gpsimd", lambda e, g=g: e.collective_compute(
                "ReduceScatter", ALU.add, replica_groups=G4,
                ins=[s_rs_in[g].ap().opt()], outs=[s_rs_out[g].ap().opt()]),
                 incs=[("cc", 1)],
                 reads=[f"srs_{ob2}_{rr}" for ob2 in range(g * 8, g * 8 + 8)
                        for rr in range(8)],
                 writes=[f"srs{g}"])

    ssb = [bigb[:, 32768:33792], bigb[:, 33792:34816]]
    gsb = [bigb[:, 34816:35840], bigb[:, 35840:36864]]
    sgT = bigb[:, 36864:37888]
    sgb = bigb[:, 37888:38912]
    xtl = [bigf[:, 0:1024], bigf[:, 1024:2048]]
    otl = [bigf[:, 10240:11264], bigf[:, 11264:12288]]
    for ob in range(32):
        p.dma("sync", ssb[ob % 2], s_rs_out[ob // 8].ap()[ob % 8, :, :],
              reads=[f"srs{ob // 8}"],
              writes=[f"sq{ob % 2}"] + (["p16", "p17"] if ob == 0 else [])
              + (["p18", "p19"] if ob == 1 else []), sem=f"ld{ob % 2}")
        p.dma("sync", gsb[ob % 2], g_sp.ap()[ob, :, :],
              reads=[f"g_{ob}_0", f"g_{ob}_1"],
              writes=[f"gq{ob % 2}"] + (["p20", "p21"] if ob == 0 else [])
              + (["p22", "p23"] if ob == 1 else []), sem=f"ld{ob % 2}")
        p.dma("scalar", xtl[ob % 2].rearrange("p (t c) -> p t c", c=128),
              x_q[:, ob * 128:(ob + 1) * 128].rearrange("(t p) c -> p t c", p=128),
              writes=[f"xt{ob % 2}"] + (["denf", "recip"] if ob == 0 else []),
              sem="ldg")
        p.op("vector", lambda e, ob=ob: e.tensor_mul(sgb, ssb[ob % 2], gsb[ob % 2]),
             incs=[("dve", 1)], reads=[f"sq{ob % 2}", f"gq{ob % 2}"],
             writes=["sgb"] + (["p26", "p27"] if ob == 0 else []))
        for t2 in range(8):
            s8 = SEQ8[t2]
            p.op("tensor", lambda e, t2=t2, c0=PTC[s8]: e.transpose(
                psT[:, c0:c0 + 128],
                sgb[:, t2 * 128:(t2 + 1) * 128], identb[:, :]),
                 incs=[("pe", 1)], reads=["sgb"],
                 writes=[f"psT{s8}"] + (["psQ6", "psQ7"]
                                        if ob == 0 and t2 == 0 else []))
            p.op("vector", lambda e, t2=t2, c0=PTC[s8]: e.tensor_copy(
                sgT[:, t2 * 128:(t2 + 1) * 128], psT[:, c0:c0 + 128]),
                 incs=[("dve", 1)], reads=[f"psT{s8}"],
                 writes=[f"sgT_{t2}"] + (["p24", "p25"]
                                         if ob == 0 and t2 == 0 else []))
        ores = f"ot{ob % 2}"
        p.op("vector", lambda e, ob=ob: e.scalar_tensor_tensor(
            otl[ob % 2], sgT, 1.0, xtl[ob % 2], ALU.mult, ALU.add),
             incs=[("dve", 1)],
             reads=[f"sgT_{t2}" for t2 in range(8)] + [f"xt{ob % 2}"],
             writes=[ores] + (["wgs", "vst"] if ob == 0 else []))
        p.dma("scalar", out_q[:, ob * 128:(ob + 1) * 128].rearrange(
            "(t p) c -> p t c", p=128),
              otl[ob % 2].rearrange("p (t c) -> p t c", c=128),
              reads=[ores], writes=[f"outd{ob}"], sem=f"st{ob % 2}")
    p.op("vector", lambda e: e.engine_nop(),
         waits=[("st0", p.n("st0")), ("st1", p.n("st1"))], incs=[("go", 1)])

    with nc.Block() as block:
        p.emit(block)
    for g in reversed(ctxs):
        g.__exit__(None, None, None)
    return nc


_NC_CACHE = None


def _get_nc():
    global _NC_CACHE
    if _NC_CACHE is None:
        _NC_CACHE = build_nc()
    return _NC_CACHE


def _host_prep(inputs):
    x = inputs["L0_attn_hidden_states"]
    qw = inputs["L0_attn_q_w"]; kw = inputs["L0_attn_k_w"]
    vw = inputs["L0_attn_v_w"]; ow = inputs["L0_attn_o_w"]
    Wb = inputs["L0_ffn_W_base"]; bb = inputs["L0_ffn_b_base"]
    Wg = inputs["L0_ffn_W_gate"]; A = inputs["L0_ffn_A"]; Bm = inputs["L0_ffn_B"]
    BF = ml_dtypes.bfloat16

    perm = np.empty(128, np.int64)
    perm[0::2] = np.arange(64)
    perm[1::2] = np.arange(64, 128)
    inv_freq = 1.0 / (10000.0 ** (np.arange(0, D, 2, dtype=np.float64) / D))
    freqs = np.arange(S, dtype=np.float64)[:, None] * inv_freq[None, :]
    cos64 = np.cos(freqs); sin64 = np.sin(freqs)
    cos_il = np.empty((128, S), np.float32)
    sin_il = np.empty((128, S), np.float32)
    cos_il[0::2] = cos64.T.astype(np.float32); cos_il[1::2] = cos64.T.astype(np.float32)
    sin_il[0::2] = (-sin64.T).astype(np.float32); sin_il[1::2] = sin64.T.astype(np.float32)
    cos_il = cos_il.astype(BF); sin_il = sin_il.astype(BF)

    pp, ff = np.mgrid[0:128, 0:512]
    tri = np.zeros((128, 2048), np.float32)
    for dj in range(4):
        tri[:, dj * 512:(dj + 1) * 512] = (ff >= pp + 128 * dj).astype(np.float32)
    tri = tri.astype(BF)
    e16 = np.zeros((8, 128), np.float32)
    for e in range(8):
        e16[e, e * 16:(e + 1) * 16] = 1.0
    e16 = e16.astype(BF)
    identb = np.eye(128, dtype=np.float32).astype(BF)
    identf = np.eye(128, dtype=np.float32)
    ones = np.ones((128, 128), np.float32).astype(BF)
    b_pm = np.ascontiguousarray(bb.reshape(32, 128).T)

    WbT = Wb.T
    wb_sl = np.ascontiguousarray(
        WbT.reshape(32, 128, 32, 128).transpose(2, 1, 0, 3).reshape(32, 128, 4096))
    wg_sl = np.ascontiguousarray(
        Wg.T.reshape(32, 128, 8).transpose(1, 0, 2).reshape(128, 256))
    af_sl = np.ascontiguousarray(
        A.reshape(128, HID).T.reshape(32, 128, 128).transpose(1, 0, 2).reshape(128, 4096))
    BfT = np.ascontiguousarray(Bm.transpose(0, 2, 1).reshape(128, HID))

    in_maps = []
    for c in range(8):
        b, i = c // 4, c % 4
        hsl = slice(i * 1024, (i + 1) * 1024)
        qwh = qw[hsl].reshape(8, 128, HID)[:, perm, :].reshape(1024, HID)
        kwh = kw[hsl].reshape(8, 128, HID)[:, perm, :].reshape(1024, HID)

        def slab_qkv(wT):
            return np.ascontiguousarray(
                wT.reshape(32, 128, 8, 128).transpose(2, 1, 0, 3).reshape(8, 128, 4096))

        owT = ow[:, hsl].T
        ow_sl = np.ascontiguousarray(
            owT.reshape(8, 128, 32, 128).transpose(2, 1, 0, 3).reshape(32, 128, 1024))
        in_maps.append({
            "x_q": np.ascontiguousarray(x[b, i * 1024:(i + 1) * 1024, :]),
            "qw_sl": slab_qkv(qwh.T),
            "kw_sl": slab_qkv(kwh.T),
            "vw_sl": slab_qkv(vw[hsl].T),
            "ow_sl": ow_sl,
            "wb_sl": wb_sl, "wg_sl": wg_sl, "af_sl": af_sl, "BfT": BfT, "b_pm": b_pm,
            "cos_il": cos_il, "sin_il": sin_il, "tri_in": tri, "e16_in": e16,
            "ident_in": identb, "identf_in": identf, "ones_in": ones,
            "impsc_in": np.array([[SCALE / (32.0 * (INSTR_END - INSTR_START))
                                   if b == 0 else 0.0]], np.float32),
        })
    return in_maps


def kernel(**inputs):
    nc = _get_nc()
    in_maps = _host_prep(inputs)
    res = run_bass_kernel_spmd(nc, in_maps, core_ids=list(range(8)))
    out = np.empty((2, S, HID), np.float32)
    for c in range(8):
        b, i = c // 4, c % 4
        out[b, i * 1024:(i + 1) * 1024, :] = res.results[c]["out_q"]
    return out
